# version 06
# baseline (speedup 1.0000x reference)
import numpy as np
import ml_dtypes
from contextlib import ExitStack
import concourse.bass as bass
import concourse.mybir as mybir
from concourse.bass_utils import run_bass_kernel_spmd

F32 = mybir.dt.float32
BF16 = mybir.dt.bfloat16
I32 = mybir.dt.int32
AF = mybir.ActivationFunctionType
ALU = mybir.AluOpType
AX = mybir.AxisListType

B, S, D = 2, 4096, 2048
T = 1024
NCH = D // 128
H, KVH, DH = 16, 4, 128
WIN = 128
THETA = 10000.0
I0 = 8192
I1 = 4096

QW = ["L0g", "L0u", "L0d", "Wi", "Wf", "Wg", "Wo1", "U1", "D1"]
QW_SHAPE = {"L0g": (I0, D), "L0u": (I0, D), "L0d": (D, I0),
            "Wi": (D, D), "Wf": (D, D), "Wg": (D, D), "Wo1": (D, D),
            "U1": (I1, D), "D1": (D, I1)}
AW = ["Wq", "Wk", "Wv", "Wo"]
AW_SHAPE = {"Wq": (D, D), "Wk": (512, D), "Wv": (512, D), "Wo": (D, D)}
QW_IDX = {n: i for i, n in enumerate(QW)}
TWO26 = 67108864.0
HALF_MINUS = 0.5 - 2.0 ** -25
MAIN_STOP = 99
MAIN_P4SUB = 99


def _stripe(w, c, n=8):
    w = np.asarray(w)
    k = w.shape[0] // n
    return np.ascontiguousarray(w[c * k:(c + 1) * k])


def _bf16(a):
    return np.asarray(a).astype(ml_dtypes.bfloat16)


def build_stats():
    nc = bass.Bass("TRN2", target_bir_lowering=False, debug=False)
    ins = {}
    for n in QW:
        r, c = QW_SHAPE[n]
        ins[n] = nc.dram_tensor("w_" + n, [r // 8, c], F32, kind="ExternalInput").ap()
    ones_in = nc.dram_tensor("ones_col", [128, 1], F32, kind="ExternalInput").ap()
    out = nc.dram_tensor("wsum", [1, 16], F32, kind="ExternalOutput").ap()

    tiles = []
    for n in QW:
        r, c = QW_SHAPE[n]
        for r0 in range(0, r // 8, 128):
            for c0 in range(0, c, 2048):
                tiles.append((n, r0, c0, min(2048, c - c0)))

    es = ExitStack()
    with es:
        buf = [es.enter_context(nc.sbuf_tensor(f"buf{i}", [128, 2048], F32)) for i in range(2)]
        acc = es.enter_context(nc.sbuf_tensor("acc", [128, 16], F32))
        red = es.enter_context(nc.sbuf_tensor("red", [128, 1], F32))
        onesc = es.enter_context(nc.sbuf_tensor("onesc", [128, 1], F32))
        row16 = es.enter_context(nc.sbuf_tensor("row16", [1, 16], F32))
        psum = es.enter_context(nc.psum_tensor("psum", [1, 16], F32))
        ld = es.enter_context(nc.semaphore("ld"))
        lda = es.enter_context(nc.semaphore("lda"))
        ldb = es.enter_context(nc.semaphore("ldb"))
        dv = es.enter_context(nc.semaphore("dv"))
        pe = es.enter_context(nc.semaphore("pe"))
        sv = es.enter_context(nc.semaphore("sv"))
        st = es.enter_context(nc.semaphore("st"))
        block = es.enter_context(nc.Block())

        @block.sync
        def _(sync):
            sync.dma_start(out=onesc[:], in_=ones_in).then_inc(ld, 16)
            for k, (n, r0, c0, cc) in enumerate(tiles):
                if k >= 2:
                    sync.wait_ge(dv, k - 1)
                sync.dma_start(out=buf[k % 2][:, 0:cc],
                               in_=ins[n][r0:r0 + 128, c0:c0 + cc]).then_inc(
                                   lda if k % 2 == 0 else ldb, 16)

        @block.vector
        def _(v):
            v.memset(acc[:], 0.0)
            v.drain()
            v.wait_ge(ld, 16)
            for k, (n, r0, c0, cc) in enumerate(tiles):
                wi = QW.index(n)
                v.wait_ge(lda if k % 2 == 0 else ldb, 16 * (k // 2 + 1))
                v.tensor_reduce(out=red[:], in_=buf[k % 2][:, 0:cc],
                                axis=AX.X, op=ALU.add, apply_absolute_value=True)
                v.drain()
                v.tensor_tensor(out=acc[:, wi:wi + 1], in0=acc[:, wi:wi + 1],
                                in1=red[:], op=ALU.add).then_inc(dv, 1)
                v.drain()

        @block.tensor
        def _(t):
            t.wait_ge(dv, len(tiles))
            t.wait_ge(ld, 16)
            t.matmul(out=psum[:], lhsT=onesc[:], rhs=acc[:], start=True,
                     stop=True).then_inc(pe, 1)

        @block.scalar
        def _(s):
            s.wait_ge(pe, 1)
            s.copy(out=row16[:], in_=psum[:]).then_inc(sv, 1)

        @block.gpsimd
        def _(g):
            g.wait_ge(sv, 1)
            g.dma_start(out=out, in_=row16[:]).then_inc(st, 16)
            g.wait_ge(st, 16)
    return nc


def stats_in_maps(inputs):
    maps = []
    for c in range(8):
        m = {"ones_col": np.ones((128, 1), np.float32)}
        m["w_L0g"] = _stripe(inputs["L0_ffn_w_gate"], c)
        m["w_L0u"] = _stripe(inputs["L0_ffn_w_up"], c)
        m["w_L0d"] = _stripe(inputs["L0_ffn_w_down"], c)
        m["w_Wi"] = _stripe(inputs["L1_attn_Wi"], c)
        m["w_Wf"] = _stripe(inputs["L1_attn_Wf"], c)
        m["w_Wg"] = _stripe(inputs["L1_attn_Wg"], c)
        m["w_Wo1"] = _stripe(inputs["L1_attn_Wo"], c)
        m["w_U1"] = _stripe(inputs["L1_ffn_w_up"], c)
        m["w_D1"] = _stripe(inputs["L1_ffn_w_down"], c)
        maps.append(m)
    return maps


def build_quantize():
    nc = bass.Bass("TRN2", target_bir_lowering=False, debug=False)
    ins, outs = {}, {}
    for n in QW:
        r, c = QW_SHAPE[n]
        ins[n] = nc.dram_tensor("w_" + n, [r // 8, c], F32, kind="ExternalInput").ap()
        outs[n] = nc.dram_tensor("q_" + n, [r // 8, c], BF16, kind="ExternalOutput").ap()
    for n in AW:
        r, c = AW_SHAPE[n]
        ins[n] = nc.dram_tensor("w_" + n, [r // 8, c], F32, kind="ExternalInput").ap()
        outs[n] = nc.dram_tensor("q_" + n, [r // 8, c], BF16, kind="ExternalOutput").ap()
    wsums = nc.dram_tensor("wsums", [8, 16], F32, kind="ExternalInput").ap()
    ones_row = nc.dram_tensor("ones_row", [1, 128], F32, kind="ExternalInput").ap()

    tiles = []
    for n in QW + AW:
        r, c = (QW_SHAPE | AW_SHAPE)[n]
        for r0 in range(0, r // 8, 128):
            rr = min(128, r // 8 - r0)
            for c0 in range(0, c, 2048):
                tiles.append((n, r0, rr, c0, min(2048, c - c0)))

    es = ExitStack()
    with es:
        buf = [es.enter_context(nc.sbuf_tensor(f"buf{i}", [128, 2048], F32)) for i in range(2)]
        qi = es.enter_context(nc.sbuf_tensor("qi", [128, 2048], I32))
        scr0 = es.enter_context(nc.sbuf_tensor("scr0", [128, 2048], F32))
        scr1 = es.enter_context(nc.sbuf_tensor("scr1", [128, 2048], F32))
        qb = [es.enter_context(nc.sbuf_tensor(f"qb{i}", [128, 2048], BF16)) for i in range(2)]
        ws = es.enter_context(nc.sbuf_tensor("ws", [8, 16], F32))
        onesf = es.enter_context(nc.sbuf_tensor("onesf", [1, 128], F32))
        srow = es.enter_context(nc.sbuf_tensor("srow", [1, 16], F32))
        scol = es.enter_context(nc.sbuf_tensor("scol", [128, 16], F32))
        pst = es.enter_context(nc.psum_tensor("pst", [16, 8], F32))
        psc = es.enter_context(nc.psum_tensor("psc", [128, 16], F32))
        idf = es.enter_context(nc.sbuf_tensor("idf", [128, 128], F32))
        ident_in = nc.dram_tensor("ident", [128, 128], F32, kind="ExternalInput").ap()
        ld = es.enter_context(nc.semaphore("ld"))
        lda = es.enter_context(nc.semaphore("lda"))
        ldb = es.enter_context(nc.semaphore("ldb"))
        pe = es.enter_context(nc.semaphore("pe"))
        dv = es.enter_context(nc.semaphore("dv"))
        dvt = es.enter_context(nc.semaphore("dvt"))
        stq = es.enter_context(nc.semaphore("stq"))
        stqa = es.enter_context(nc.semaphore("stqa"))
        stqb = es.enter_context(nc.semaphore("stqb"))
        block = es.enter_context(nc.Block())

        @block.sync
        def _(sync):
            sync.dma_start(out=ws[:], in_=wsums).then_inc(ld, 16)
            sync.dma_start(out=onesf[:], in_=ones_row).then_inc(ld, 16)
            sync.dma_start(out=idf[:], in_=ident_in).then_inc(ld, 16)
            for k, (n, r0, rr, c0, cc) in enumerate(tiles):
                if k >= 2:
                    sync.wait_ge(dvt, k - 1)
                sync.dma_start(out=buf[k % 2][0:rr, 0:cc],
                               in_=ins[n][r0:r0 + rr, c0:c0 + cc]).then_inc(
                                   lda if k % 2 == 0 else ldb, 16)

        @block.tensor
        def _(t):
            t.wait_ge(ld, 48)
            t.transpose(out=pst[:], in_=ws[:], identity=idf[0:8, 0:8]).then_inc(pe, 1)

        @block.vector
        def _(v):
            v.wait_ge(pe, 1)
            v.tensor_reduce(out=scol[0:16, 0:1], in_=pst[0:16, 0:8],
                            axis=AX.X, op=ALU.add).then_inc(dv, 1)
            v.drain()

        @block.tensor
        def _(t):
            t.wait_ge(dv, 1)
            t.transpose(out=psc[0:1, 0:16], in_=scol[0:16, 0:1],
                        identity=idf[0:16, 0:16]).then_inc(pe, 1)

        @block.vector
        def _(v):
            v.wait_ge(pe, 2)
            sizes = [float(QW_SHAPE[n][0] * QW_SHAPE[n][1]) for n in QW] + [1.0] * 7
            for wi in range(16):
                v.tensor_scalar(out=srow[0:1, wi:wi + 1], in0=psc[0:1, wi:wi + 1],
                                scalar1=1.0 / sizes[wi], scalar2=1e-5,
                                op0=ALU.mult, op1=ALU.max)
            v.drain()
            v.reciprocal(out=srow[:], in_=srow[:])
            v.drain()
            v.nop().then_inc(dv, 1)

        @block.tensor
        def _(t):
            t.wait_ge(dv, 2)
            t.matmul(out=psc[:], lhsT=onesf[:], rhs=srow[:], start=True,
                     stop=True).then_inc(pe, 1)

        @block.vector
        def _(v):
            v.wait_ge(pe, 3)
            v.tensor_copy(out=scol[:], in_=psc[:])
            v.drain()
            for k, (n, r0, rr, c0, cc) in enumerate(tiles):
                v.wait_ge(lda if k % 2 == 0 else ldb, 16 * (k // 2 + 1))
                if k >= 2:
                    v.wait_ge(stqa if k % 2 == 0 else stqb, 16 * (k // 2))
                src = buf[k % 2]
                if n in QW:
                    wi = QW.index(n)
                    v.tensor_scalar(out=src[0:rr, 0:cc], in0=src[0:rr, 0:cc],
                                    scalar1=scol[:, wi:wi + 1], scalar2=None,
                                    op0=ALU.mult)
                    v.drain()
                    v.tensor_scalar(out=scr0[0:rr, 0:cc], in0=src[0:rr, 0:cc],
                                    scalar1=HALF_MINUS, scalar2=TWO26,
                                    op0=ALU.subtract, op1=ALU.mult)
                    v.tensor_scalar(out=scr1[0:rr, 0:cc], in0=src[0:rr, 0:cc],
                                    scalar1=0.5, scalar2=-TWO26,
                                    op0=ALU.add, op1=ALU.mult)
                    v.drain()
                    v.tensor_scalar(out=scr0[0:rr, 0:cc], in0=scr0[0:rr, 0:cc],
                                    scalar1=0.0, scalar2=1.0,
                                    op0=ALU.max, op1=ALU.min)
                    v.tensor_scalar(out=scr1[0:rr, 0:cc], in0=scr1[0:rr, 0:cc],
                                    scalar1=0.0, scalar2=1.0,
                                    op0=ALU.max, op1=ALU.min)
                    v.drain()
                    v.tensor_tensor(out=qb[k % 2][0:rr, 0:cc],
                                    in0=scr0[0:rr, 0:cc], in1=scr1[0:rr, 0:cc],
                                    op=ALU.subtract).then_inc(dvt, 1)
                    v.drain()
                else:
                    v.tensor_copy(out=qb[k % 2][0:rr, 0:cc],
                                  in_=src[0:rr, 0:cc]).then_inc(dvt, 1)
                    v.drain()

        @block.gpsimd
        def _(g):
            na = nb = 0
            for k, (n, r0, rr, c0, cc) in enumerate(tiles):
                g.wait_ge(dvt, k + 1)
                if k % 2 == 0:
                    na += 16
                    g.dma_start(out=outs[n][r0:r0 + rr, c0:c0 + cc],
                                in_=qb[0][0:rr, 0:cc]).then_inc(stqa, 16)
                else:
                    nb += 16
                    g.dma_start(out=outs[n][r0:r0 + rr, c0:c0 + cc],
                                in_=qb[1][0:rr, 0:cc]).then_inc(stqb, 16)
            g.wait_ge(stqa, na)
            g.wait_ge(stqb, nb)
    return nc


def quantize_in_maps(inputs, wsums_all):
    key = {"L0g": "L0_ffn_w_gate", "L0u": "L0_ffn_w_up", "L0d": "L0_ffn_w_down",
           "Wi": "L1_attn_Wi", "Wf": "L1_attn_Wf", "Wg": "L1_attn_Wg",
           "Wo1": "L1_attn_Wo", "U1": "L1_ffn_w_up", "D1": "L1_ffn_w_down",
           "Wq": "L0_attn_Wq", "Wk": "L0_attn_Wk", "Wv": "L0_attn_Wv",
           "Wo": "L0_attn_Wo"}
    maps = []
    for c in range(8):
        m = {"wsums": np.ascontiguousarray(wsums_all),
             "ones_row": np.ones((1, 128), np.float32),
             "ident": np.eye(128, dtype=np.float32)}
        for n in QW + AW:
            m["w_" + n] = _stripe(inputs[key[n]], c)
        maps.append(m)
    return maps


ENGS = ("sync", "tensor", "vector", "scalar", "gpsimd")


class E:
    def __init__(self, nc, es):
        self.nc = nc
        self.es = es
        self.q = {e: [] for e in ENGS}
        self.c = {}
        self.h = {}
        self.n_bar = 0

    def sem(self, name):
        self.h[name] = self.es.enter_context(self.nc.semaphore(name))
        self.c[name] = 0
        return self.h[name]

    def sb(self, name, shape, dt):
        return self.es.enter_context(self.nc.sbuf_tensor(name, shape, dt))

    def psum(self, name, shape, dt=F32):
        return self.es.enter_context(self.nc.psum_tensor(name, shape, dt))

    def add(self, eng, fn):
        self.q[eng].append(fn)

    def inc(self, sem, by=1):
        self.c[sem] += by
        return self.c[sem]

    def wait(self, eng, sem, val):
        h = self.h[sem]
        self.add(eng, lambda e: e.wait_ge(h, val))

    def wait_ev(self, eng, tgt=None):
        if tgt is None:
            tgt = (self.c["evv"], self.c["evs"])
        self.wait(eng, "evv", tgt[0])
        self.wait(eng, "evs", tgt[1])

    def ev_now(self):
        return (self.c["evv"], self.c["evs"])

    def barrier(self):
        self.n_bar += 1
        tgt = 5 * self.n_bar
        h = self.h["bar"]
        for eng in ENGS:
            def f(e, h=h):
                e.drain()
                e.nop().then_inc(h, 1)
            self.add(eng, f)
        for eng in ENGS:
            self.wait(eng, "bar", tgt)

    def replay(self, block):
        for eng in ENGS:
            dec = getattr(block, eng)
            fns = self.q[eng]

            def body(e, fns=fns):
                for fn in fns:
                    fn(e)
            dec(body)


def _dma(E_, eng, out_ap, in_ap, sem="lw"):
    cur = E_.c[sem]
    h = E_.h[sem]
    tgt = E_.inc(sem, 16)
    E_.add(eng, lambda e: (e.wait_ge(h, cur),
                           e.dma_start(out=out_ap, in_=in_ap).then_inc(h, 16)))
    return tgt


def mm_fm2(E_, wt_dram, Kdim, Mout, rhs_of, n_tok, evac, wbufs, pool,
           mw=256, tw=512, ksplit=1):
    KC = Kdim // 128
    KCS = KC // ksplit
    src3 = wt_dram.rearrange("(kc p) m -> p kc m", p=128)
    pv = E_.h["pe"]
    E_.wait_ev("tensor")
    E_.wait("tensor", "vx", E_.c["vx"])
    E_.wait("tensor", "sx", E_.c["sx"])
    pe_done_at_bank = [None, None, None, None]
    bank_of = {}
    nbk = 0
    slab_done = []
    nslab = (Mout // mw) * ksplit
    for s in range(nslab):
        mi, kv = divmod(s, ksplit)
        wb = wbufs[s % 2]
        wsem = "lwa" if s % 2 == 0 else "lwb"
        if s >= 2:
            E_.wait("sync", "pe", slab_done[s - 2])
        wb3 = wb.rearrange("p (kc m) -> p kc m", m=mw)
        t_l = _dma(E_, "sync", wb3[:, 0:KCS, :],
                   src3[:, kv * KCS:(kv + 1) * KCS, mi * mw:(mi + 1) * mw],
                   wsem)
        E_.wait("tensor", wsem, t_l)
        for mm_i in range(mw // 128):
            for t0 in range(0, n_tok, tw):
                key = (mi * (mw // 128) + mm_i, t0)
                if key not in bank_of:
                    bank_of[key] = nbk % 4
                    nbk += 1
                bk = bank_of[key]
                ps = pool[bk]
                if kv == 0 and pe_done_at_bank[bk] is not None:
                    E_.wait_ev("tensor", pe_done_at_bank[bk])
                for kc in range(KCS):
                    st_ = (kv == 0 and kc == 0)
                    sp_ = (kv == ksplit - 1 and kc == KCS - 1)
                    inc_ = sp_ or kc == KCS - 1
                    lhsT = wb[:, kc * mw + mm_i * 128: kc * mw + mm_i * 128 + 128]
                    rhs = rhs_of(kv * KCS + kc, t0)
                    if inc_:
                        E_.add("tensor", lambda e, o=ps, l=lhsT, r=rhs, a=st_,
                               b=sp_: e.matmul(out=o[:], lhsT=l, rhs=r, start=a,
                                               stop=b).then_inc(pv, 1))
                        E_.inc("pe")
                    else:
                        E_.add("tensor", lambda e, o=ps, l=lhsT, r=rhs, a=st_:
                               e.matmul(out=o[:], lhsT=l, rhs=r, start=a,
                                        stop=False))
                if kv == ksplit - 1:
                    evac(mi * mw + mm_i * 128, t0, ps, E_.c["pe"])
                    pe_done_at_bank[bk] = E_.ev_now()
        slab_done.append(E_.c["pe"])


def col_reduce_to_row(E_, src_sb, W, out_row, op, identf, tmpcol, pst, psr,
                      extra_in=None):
    pv, vv, sv = E_.h["pe"], E_.h["vx"], E_.h["sx"]
    E_.add("vector", lambda e: e.nop().then_inc(vv, 1))
    t0_ = E_.inc("vx")
    E_.wait("tensor", "vx", t0_)
    for b in range(W // 128):
        sl = src_sb[:, b * 128:(b + 1) * 128]
        E_.add("tensor", lambda e, s=sl: e.transpose(out=pst[:], in_=s,
                                                     identity=identf[:]).then_inc(pv, 1))
        t1 = E_.inc("pe")
        E_.wait("vector", "pe", t1)
        E_.add("vector", lambda e: e.tensor_reduce(
            out=tmpcol[:], in_=pst[:], axis=AX.X, op=op).then_inc(vv, 1))
        t2 = E_.inc("vx")
        E_.wait("tensor", "vx", t2)
        E_.add("tensor", lambda e: e.transpose(out=psr[0:1, 0:128], in_=tmpcol[:],
                                               identity=identf[:]).then_inc(pv, 1))
        t3 = E_.inc("pe")
        E_.wait("scalar", "pe", t3)
        E_.add("scalar", lambda e, b=b: e.copy(
            out=out_row[0:1, b * 128:(b + 1) * 128],
            in_=psr[0:1, 0:128]).then_inc(sv, 1))
        t4 = E_.inc("sx")
        E_.wait("tensor", "vx", t2)
        E_.wait("tensor", "sx", t4)
    E_.wait("vector", "sx", E_.c["sx"])

def fold_chain(E_, srcs, out, op, eng="vector"):
    v = E_.h["vx" if eng == "vector" else "gx"]
    E_.add(eng, lambda e, a=srcs[0], b=srcs[1], o=out: (
        e.tensor_tensor(out=o, in0=a, in1=b, op=op), e.drain()))
    for s in srcs[2:]:
        E_.add(eng, lambda e, a=s, o=out: (
            e.tensor_tensor(out=o, in0=o, in1=a, op=op), e.drain()))


def row_recip(E_, row, W):
    E_.add("vector", lambda e: (e.reciprocal(out=row[0:1, 0:W], in_=row[0:1, 0:W]),
                                e.drain()))


def bcast_row(E_, row_ap, W, ps_out, ones_row_f):
    pv = E_.h["pe"]
    E_.wait("tensor", "vx", E_.c["vx"])
    E_.wait("tensor", "sx", E_.c["sx"])
    E_.add("tensor", lambda e: e.matmul(out=ps_out, lhsT=ones_row_f,
                                        rhs=row_ap, start=True,
                                        stop=True).then_inc(pv, 1))
    return E_.inc("pe")


def ln_block(E_, x_of, nch, W, xn_of, ident_f, ones_row_f, F2, rows, PS, eps=1e-5):
    vv, sv = E_.h["vx"], E_.h["sx"]
    D_ = nch * 128
    fold_chain(E_, [x_of(c) for c in range(nch)], F2[:, 0:W], ALU.add)
    def sq0(e):
        e.tensor_tensor(out=F2[:, W:2 * W], in0=x_of(0), in1=x_of(0), op=ALU.mult)
        e.drain()
    E_.add("vector", sq0)
    for c in range(1, nch):
        def sqc(e, c=c):
            e.tensor_tensor(out=F2[:, 2 * W:3 * W], in0=x_of(c), in1=x_of(c),
                            op=ALU.mult)
            e.drain()
            e.tensor_tensor(out=F2[:, W:2 * W], in0=F2[:, W:2 * W],
                            in1=F2[:, 2 * W:3 * W], op=ALU.add)
            e.drain()
        E_.add("vector", sqc)
    col_reduce_to_row(E_, F2[:, 0:W], W, rows["m"], ALU.add, ident_f,
                      rows["tc"], PS["pst"], PS["psr"])
    col_reduce_to_row(E_, F2[:, W:2 * W], W, rows["s"], ALU.add, ident_f,
                      rows["tc"], PS["pst"], PS["psr"])
    def rowmath(e):
        e.tensor_scalar(out=rows["m"][0:1, 0:W], in0=rows["m"][0:1, 0:W],
                        scalar1=1.0 / D_, scalar2=None, op0=ALU.mult)
        e.tensor_scalar(out=rows["s"][0:1, 0:W], in0=rows["s"][0:1, 0:W],
                        scalar1=1.0 / D_, scalar2=None, op0=ALU.mult)
        e.drain()
        e.tensor_tensor(out=rows["v"][0:1, 0:W], in0=rows["m"][0:1, 0:W],
                        in1=rows["m"][0:1, 0:W], op=ALU.mult)
        e.drain()
        e.tensor_tensor(out=rows["v"][0:1, 0:W], in0=rows["s"][0:1, 0:W],
                        in1=rows["v"][0:1, 0:W], op=ALU.subtract)
        e.drain()
        e.tensor_scalar(out=rows["v"][0:1, 0:W], in0=rows["v"][0:1, 0:W],
                        scalar1=float(eps), scalar2=None, op0=ALU.add)
        e.drain()
        e.nop().then_inc(E_.h["vx"], 1)
    E_.add("vector", rowmath)
    t = E_.inc("vx")
    E_.wait("scalar", "vx", t)
    E_.add("scalar", lambda e: (e.activation(out=rows["v"][0:1, 0:W],
                                             in_=rows["v"][0:1, 0:W],
                                             func=AF.Sqrt),
                                e.drain(), e.nop().then_inc(sv, 1)))
    ts = E_.inc("sx")
    E_.wait("vector", "sx", ts)
    def rowfin(e):
        e.reciprocal(out=rows["v"][0:1, 0:W], in_=rows["v"][0:1, 0:W])
        e.drain()
        e.tensor_scalar(out=rows["m"][0:1, 0:W], in0=rows["m"][0:1, 0:W],
                        scalar1=-1.0, scalar2=None, op0=ALU.mult)
        e.drain()
        e.nop().then_inc(E_.h["vx"], 1)
    E_.add("vector", rowfin)
    tv = E_.inc("vx")
    E_.wait("tensor", "vx", tv)
    t2 = bcast_row(E_, rows["m"][0:1, 0:W], W, PS["b1"][:, 0:W], ones_row_f)
    t2 = bcast_row(E_, rows["v"][0:1, 0:W], W, PS["b2"][:, 0:W], ones_row_f)
    E_.wait("vector", "pe", t2)
    for c in range(nch):
        E_.add("vector", lambda e, c=c: (
            e.tensor_tensor(out=F2[:, 2 * W:3 * W], in0=x_of(c),
                            in1=PS["b1"][:, 0:W], op=ALU.add),
            e.drain(),
            e.tensor_tensor(out=xn_of(c), in0=F2[:, 2 * W:3 * W],
                            in1=PS["b2"][:, 0:W], op=ALU.mult),
            e.drain()))
    E_.add("vector", lambda e: e.nop().then_inc(vv, 1))
    return E_.inc("vx")


def quant_block(E_, src_of, nch, W, qb, out_of, rows, F1, F2, I32buf,
                ident_f, ones_row_f, PS, prescale=1.0, clip_hi=False):
    vv = E_.h["vx"]
    fold_chain(E_, [src_of(c) for c in range(nch)], F2[:, 0:W], ALU.max)
    fold_chain(E_, [src_of(c) for c in range(nch)], F2[:, W:2 * W], ALU.min)
    col_reduce_to_row(E_, F2[:, 0:W], W, rows["q"], ALU.max, ident_f,
                      rows["tc"], PS["pst"], PS["psr"])
    col_reduce_to_row(E_, F2[:, W:2 * W], W, rows["v"], ALU.min, ident_f,
                      rows["tc"], PS["pst"], PS["psr"])
    def rowmath(e):
        e.tensor_scalar(out=rows["v"][0:1, 0:W], in0=rows["v"][0:1, 0:W],
                        scalar1=-1.0, scalar2=None, op0=ALU.mult)
        e.drain()
        e.tensor_tensor(out=rows["q"][0:1, 0:W], in0=rows["q"][0:1, 0:W],
                        in1=rows["v"][0:1, 0:W], op=ALU.max)
        e.drain()
        e.tensor_scalar(out=rows["q"][0:1, 0:W], in0=rows["q"][0:1, 0:W],
                        scalar1=prescale, scalar2=1e-5, op0=ALU.mult,
                        op1=ALU.max)
        e.drain()
        e.reciprocal(out=rows["m"][0:1, 0:W], in_=rows["q"][0:1, 0:W])
        e.drain()
        e.tensor_scalar(out=rows["s"][0:1, 0:W], in0=rows["m"][0:1, 0:W],
                        scalar1=qb * prescale, scalar2=None, op0=ALU.mult)
        e.drain()
    E_.add("vector", rowmath)
    tv = E_.inc("vx")
    E_.add("vector", lambda e: (e.drain(), e.nop().then_inc(vv, 1)))
    E_.wait("tensor", "vx", tv)
    tb = bcast_row(E_, rows["s"][0:1, 0:W], W, PS["b1"][:, 0:W], ones_row_f)
    E_.wait("vector", "pe", tb)
    for c in range(nch):
        def f(e, c=c):
            e.tensor_tensor(out=F1[:, 0:W], in0=src_of(c), in1=PS["b1"][:, 0:W],
                            op=ALU.mult)
            e.drain()
            e.tensor_copy(out=I32buf[:, 0:W], in_=F1[:, 0:W])
            e.drain()
            e.tensor_copy(out=F2[:, 0:W], in_=I32buf[:, 0:W])
            e.drain()
            e.tensor_tensor(out=F2[:, W:2 * W], in0=F1[:, 0:W],
                            in1=F2[:, 0:W], op=ALU.subtract)
            e.drain()
            e.tensor_scalar(out=F1[:, 0:W], in0=F2[:, W:2 * W],
                            scalar1=HALF_MINUS, scalar2=TWO26,
                            op0=ALU.subtract, op1=ALU.mult)
            e.tensor_scalar(out=F2[:, W:2 * W], in0=F2[:, W:2 * W],
                            scalar1=0.5, scalar2=-TWO26,
                            op0=ALU.add, op1=ALU.mult)
            e.drain()
            e.tensor_scalar(out=F1[:, 0:W], in0=F1[:, 0:W],
                            scalar1=0.0, scalar2=1.0,
                            op0=ALU.max, op1=ALU.min)
            e.tensor_scalar(out=F2[:, W:2 * W], in0=F2[:, W:2 * W],
                            scalar1=0.0, scalar2=1.0,
                            op0=ALU.max, op1=ALU.min)
            e.drain()
            e.tensor_tensor(out=F2[:, 0:W], in0=F2[:, 0:W],
                            in1=F1[:, 0:W], op=ALU.add)
            e.drain()
            if clip_hi:
                e.tensor_tensor(out=F2[:, 0:W], in0=F2[:, 0:W],
                                in1=F2[:, W:2 * W], op=ALU.subtract)
                e.drain()
                e.tensor_scalar(out=out_of(c), in0=F2[:, 0:W],
                                scalar1=float(qb - 1), scalar2=None, op0=ALU.min)
            else:
                e.tensor_tensor(out=out_of(c), in0=F2[:, 0:W],
                                in1=F2[:, W:2 * W], op=ALU.subtract)
            e.drain()
        E_.add("vector", f)
    tv = E_.inc("vx")
    E_.add("vector", lambda e: (e.drain(), e.nop().then_inc(vv, 1)))
    return tv


def dequant_row(E_, src_row, dst_row, dst_off, W, mrow_ap, qb, extra=1.0):
    vv = E_.h["vx"]

    def f(e):
        e.tensor_scalar(out=dst_row[0:1, dst_off:dst_off + W],
                        in0=src_row[0:1, 0:W],
                        scalar1=mrow_ap, scalar2=extra / qb,
                        op0=ALU.mult, op1=ALU.mult)
        e.drain()
        e.nop().then_inc(vv, 1)
    E_.add("vector", f)
    return E_.inc("vx")


def fwht_pe_stage(E_, ch_of, nch, W, h128, pool):
    pv = E_.h["pe"]
    E_.wait("tensor", "vx", E_.c["vx"])
    E_.wait("tensor", "sx", E_.c["sx"])
    E_.wait_ev("tensor")
    done = [None, None, None, None]
    for c in range(nch):
        bk = c % 4
        ps = pool[bk]
        if done[bk] is not None:
            E_.wait_ev("tensor", done[bk])
        E_.add("tensor", lambda e, ps=ps, c=c: e.matmul(
            out=ps[:, 0:W], lhsT=h128, rhs=ch_of(c), start=True,
            stop=True).then_inc(pv, 1))
        t = E_.inc("pe")
        E_.wait("scalar", "pe", t)
        E_.add("scalar", lambda e, ps=ps, c=c: (
            e.copy(out=ch_of(c), in_=ps[:, 0:W]).then_inc(E_.h["evs"], 1),
            e.drain()))
        E_.inc("evs")
        done[bk] = E_.ev_now()


def fwht_chunk_stages(E_, ch_of, nch, W, slot):
    import math

    Wh = ((W * 5) // 8) & ~31
    ptmp = E_.fwht_ptmp
    E_.wait("vector", "sx", E_.c["sx"])
    E_.wait_ev("vector")
    E_.wait("gpsimd", "sx", E_.c["sx"])
    E_.wait("gpsimd", "vx", E_.c["vx"])
    E_.wait("gpsimd", "lg", E_.c["lg"])
    E_.wait_ev("gpsimd")

    def buf(key):
        kind, idx = key
        return ch_of(idx) if kind == "c" else slot[:, idx * W:(idx + 1) * W]

    def emit_pair(pa, pb, o1, o2):
        def fv(e, pa=pa, pb=pb, o1=o1, o2=o2):
            e.tensor_tensor(out=o1[:, 0:Wh], in0=pa[:, 0:Wh],
                            in1=pb[:, 0:Wh], op=ALU.add)
            e.tensor_tensor(out=o2[:, 0:Wh], in0=pa[:, 0:Wh],
                            in1=pb[:, 0:Wh], op=ALU.subtract)
            e.drain()
        E_.add("vector", fv)

        def fg(e, pa=pa, pb=pb, o1=o1, o2=o2):
            e.tensor_scalar_mul(ptmp[:, 0:W - Wh], pb[:, Wh:W], -1.0)
            e.drain()
            e.tensor_add(out=o1[:, Wh:W], in0=pa[:, Wh:W], in1=pb[:, Wh:W])
            e.tensor_add(out=o2[:, Wh:W], in0=pa[:, Wh:W],
                         in1=ptmp[:, 0:W - Wh])
            e.drain()
        E_.add("gpsimd", fg)

    def emit_move(s, d):
        E_.add("vector", lambda e, s=s, d=d: (
            e.tensor_copy(out=d[:, 0:Wh], in_=s[:, 0:Wh]), e.drain()))
        E_.add("gpsimd", lambda e, s=s, d=d: (
            e.tensor_copy(out=d[:, Wh:W], in_=s[:, Wh:W]), e.drain()))

    loc = {c: ("c", c) for c in range(nch)}
    free = [("s", 0), ("s", 1)]
    for t in range(int(math.log2(nch))):
        step = 1 << t
        for base in range(0, nch, 2 * step):
            for off in range(step):
                a_, b_ = base + off, base + off + step
                d1 = free.pop()
                d2 = free.pop()
                emit_pair(buf(loc[a_]), buf(loc[b_]), buf(d1), buf(d2))
                free.append(loc[a_])
                free.append(loc[b_])
                loc[a_] = d1
                loc[b_] = d2
    occupied = {loc[c]: c for c in range(nch)}
    pending = set(c for c in range(nch) if loc[c] != ("c", c))
    while pending:
        prog = False
        for c in sorted(pending):
            tgt = ("c", c)
            if tgt not in occupied:
                emit_move(buf(loc[c]), buf(tgt))
                del occupied[loc[c]]
                free.append(loc[c])
                if tgt in free:
                    free.remove(tgt)
                loc[c] = tgt
                occupied[tgt] = c
                pending.discard(c)
                prog = True
                break
        if not prog:
            c = sorted(pending)[0]
            sp = free.pop()
            emit_move(buf(loc[c]), buf(sp))
            del occupied[loc[c]]
            free.append(loc[c])
            loc[c] = sp
            occupied[sp] = c
    E_.add("gpsimd", lambda e: (e.drain(), e.nop().then_inc(E_.h["gx"], 1)))
    tg = E_.inc("gx")
    E_.wait("vector", "gx", tg)
    E_.wait("tensor", "gx", tg)
    E_.wait("scalar", "gx", tg)
    E_.add("vector", lambda e: e.nop().then_inc(E_.h["vx"], 1))
    return E_.inc("vx")


def rope_emit(E_, ps, ps2, sb_raw, cos, sin, out, W, pe_tgt, rotb, s1, s2):
    E_.wait("scalar", "pe", pe_tgt)
    E_.wait_ev("scalar")
    E_.add("scalar", lambda e: (e.copy(out=sb_raw, in_=ps), e.drain(),
                                e.nop().then_inc(E_.h["sx"], 1)))
    ts = E_.inc("sx")
    E_.wait("tensor", "sx", ts)
    E_.add("tensor", lambda e: e.matmul(out=ps2, lhsT=rotb, rhs=sb_raw,
                                        start=True, stop=True).then_inc(E_.h["pe"], 1))
    tp = E_.inc("pe")
    E_.wait("vector", "pe", tp)

    def f(e):
        e.tensor_tensor(out=s1, in0=sb_raw, in1=cos, op=ALU.mult)
        e.tensor_tensor(out=s2, in0=ps2, in1=sin, op=ALU.mult)
        e.drain()
        e.tensor_tensor(out=out, in0=s1, in1=s2,
                        op=ALU.add).then_inc(E_.h["evv"], 1)
        e.drain()
    E_.add("vector", f)
    return E_.inc("evv")


def build_main():
    nc = bass.Bass("TRN2", target_bir_lowering=False, debug=False)
    dr = lambda n, s, dt, k: nc.dram_tensor(n, s, dt, kind=k).ap()
    x_fm = dr("x_fm", [D, S], F32, "ExternalInput")
    x_own = dr("x_own", [D, T], F32, "ExternalInput")
    wt = {}
    for n, (r, c) in {"q": (D, D), "k": (D, 512), "v": (D, 512), "o": (D, D),
                      "g0": (D, I0), "u0": (D, I0), "d0": (I0, D),
                      "wi": (D, D), "wf": (D, D), "wg": (D, D),
                      "u1": (D, I1), "d1": (I1, D)}.items():
        wt[n] = dr("wt_" + n, [r, c], BF16, "ExternalInput")
    wsums = dr("wsums_d", [8, 16], F32, "ExternalInput")
    kcos_d = dr("kcos_d", [128, S], BF16, "ExternalInput")
    ksin_d = dr("ksin_d", [128, S], BF16, "ExternalInput")
    qcos_d = dr("qcos_d", [128, T], BF16, "ExternalInput")
    qsin_d = dr("qsin_d", [128, T], BF16, "ExternalInput")
    masks_d = dr("masks_d", [8, 128, 512], BF16, "ExternalInput")
    nrm_d = dr("nrms_d", [128, 48], F32, "ExternalInput")
    identf_d = dr("identf_d", [128, 128], F32, "ExternalInput")
    identb_d = dr("identb_d", [128, 128], BF16, "ExternalInput")
    h128_d = dr("h128_d", [128, 128], BF16, "ExternalInput")
    h128f_d = dr("h128f_d", [128, 128], F32, "ExternalInput")
    rotb_d = dr("rotb_d", [128, 128], BF16, "ExternalInput")
    onesr_d = dr("onesr_d", [1, 128], F32, "ExternalInput")
    x1_o = dr("x1_o", [128, 16 * T], F32, "ExternalOutput")
    sl_o = dr("sl_o", [128, 16 * T], BF16, "ExternalOutput")
    P_o = dr("P_o", [128, 16 * T], BF16, "ExternalOutput")
    sg_o = dr("sg_o", [128, 16 * T], BF16, "ExternalOutput")
    g1_o = dr("g1_o", [128, 16 * T], BF16, "ExternalOutput")
    summ_o = dr("summ_o", [128, 256], F32, "ExternalOutput")
    xn1_d = dr("xn1_d", [128, 16 * T], F32, "Internal")
    x1i_d = dr("x1i_d", [128, 16 * T], F32, "Internal")
    i_d = dr("i_d", [128, 16 * T], BF16, "Internal")

    es = ExitStack()
    E_ = E(nc, es)
    with es:
        A1 = E_.sb("A1", [128, 16384], BF16)
        A2 = E_.sb("A2", [128, 16384], BF16)
        A3 = E_.sb("A3", [128, 16384], BF16)
        A4 = E_.sb("A4", [128, 16384], BF16)
        F1 = E_.sb("F1", [128, 4096], F32)
        F2 = E_.sb("F2", [128, 2048], F32)
        W0 = E_.sb("W0", [128, 4096], BF16)
        W1 = E_.sb("W1", [128, 4096], BF16)
        qcos = E_.sb("qcos", [128, T], BF16)
        qsin = E_.sb("qsin", [128, T], BF16)
        nrm = E_.sb("nrm", [128, 48], F32)
        identf = E_.sb("identf", [128, 128], F32)
        identb = E_.sb("identb", [128, 128], BF16)
        h128 = E_.sb("h128", [128, 128], BF16)
        h128f = E_.sb("h128f", [128, 128], F32)
        rotb = E_.sb("rotb", [128, 128], BF16)
        onesr = E_.sb("onesr", [1, 128], F32)
        krop = E_.sb("krop", [128, 512], BF16)
        wqh = E_.sb("wqh", [128, 2048], BF16)
        qh = E_.sb("qh", [128, T], BF16)
        zz = E_.sb("zz", [128, 128], BF16)
        E_.fwht_ptmp = E_.sb("ptmp", [128, 192], F32)
        sc_extra = E_.sb("sc_extra", [128, 512], F32)
        scm = E_.sb("scm", [128, 512], F32)
        rcpb = E_.sb("rcpb", [128, 128], F32)
        rrow = E_.sb("rrow", [1, 128], F32)
        ibuf = E_.sb("ibuf", [128, 256], I32)
        summ = E_.sb("summ", [128, 256], F32)
        wsr = E_.sb("wsr", [8, 16], F32)
        mrow = E_.sb("mrow", [1, 16], F32)
        stg = E_.sb("stg", [128, 1024], BF16)
        rows = {}
        for k, w in [("m", 256), ("s", 256), ("v", 256), ("q", 256),
                     ("r1", 1024), ("r2", 1024), ("r3", 1024)]:
            rows[k] = E_.sb("row_" + k, [1, w], F32)
        rows["tc"] = E_.sb("row_tc", [128, 1], F32)
        stat = {k: E_.sb("st_" + k, [128, 1], F32)
                for k in ["mx1", "mx2", "neg", "sum", "rcp"]}
        PSf = E_.psum("PSf", [128, 3584], F32)
        PSB = E_.psum("PSBt", [128, 1024], BF16)
        PS = {
            "pool": [PSf[:, i * 512:(i + 1) * 512] for i in range(4)],
            "pst": PSf[:, 2048:2176], "psr": PSf[0:1, 2176:2304],
            "b1": PSf[:, 2560:3072], "b2": PSf[:, 3072:3584],
        }
        for s_ in ["lw", "lwa", "lwb", "lg", "pe", "vx", "sx", "gx", "evv",
                   "evs", "bar"]:
            E_.sem(s_)

        def _stop(n):
            if MAIN_STOP > n:
                return False
            tlg_ = E_.c["lg"]
            tlw_ = E_.c["lw"]
            for g_ in ENGS:
                E_.wait(g_, "lg", tlg_)
                E_.wait(g_, "lw", tlw_)
            E_.barrier()
            blk_ = es.enter_context(nc.Block())
            E_.replay(blk_)
            return True

        psbslot = [PSB[:, 0:128],
                   PSf[:, 0:512].bitcast(BF16)[:, 0:128],
                   PSf[:, 1024:1536].bitcast(BF16)[:, 0:128],
                   PSf[:, 1536:2048].bitcast(BF16)[:, 0:128],
                   PSf[:, 2048:2560].bitcast(BF16)[:, 0:128],
                   PSf[:, 2560:3072].bitcast(BF16)[:, 0:128],
                   PSf[:, 3072:3584].bitcast(BF16)[:, 0:128]]

        kcos = A4[:, 8192:12288]
        ksin = A4[:, 12288:16384]
        wk_sl = A3[:, 4096:12288]
        wv_sl = A4[:, 0:8192]
        mskb = F2[:].bitcast(BF16)
        pr = F1[:].bitcast(BF16)[:, 0:4096]
        pT = F1[:].bitcast(BF16)[:, 4096:8192]

        for dst, src in [(kcos, kcos_d), (ksin, ksin_d), (qcos[:], qcos_d),
                         (qsin[:], qsin_d), (nrm[:], nrm_d), (identf[:], identf_d),
                         (identb[:], identb_d), (h128[:], h128_d),
                         (h128f[:], h128f_d),
                         (rotb[:], rotb_d),
                         (onesr[:], onesr_d), (wsr[:], wsums)]:
            _dma(E_, "sync", dst, src)
        for wname, dstsl in [("k", wk_sl), ("v", wv_sl)]:
            src3 = wt[wname].rearrange("(kc p) m -> p kc m", p=128)
            dst3 = dstsl.rearrange("p (kc m) -> p kc m", m=512)
            _dma(E_, "sync", dst3, src3)
        tl = E_.c["lw"]
        for g in ENGS:
            E_.wait(g, "lw", tl)
        E_.add("vector", lambda e: (e.memset(zz[:], 0.0), e.drain()))
        E_.add("tensor", lambda e: e.transpose(
            out=PSf[0:16, 2048:2056], in_=wsr[:], identity=identf[0:8, 0:8]
        ).then_inc(E_.h["pe"], 1))
        t = E_.inc("pe")
        E_.wait("vector", "pe", t)
        E_.add("vector", lambda e: (e.tensor_reduce(
            out=rows["tc"][0:16, 0:1], in_=PSf[0:16, 2048:2056],
            axis=AX.X, op=ALU.add),
            e.drain(), e.nop().then_inc(E_.h["vx"], 1)))
        t = E_.inc("vx")
        E_.wait("tensor", "vx", t)
        E_.add("tensor", lambda e: e.transpose(
            out=PSf[0:1, 2176:2192], in_=rows["tc"][0:16, 0:1],
            identity=identf[0:16, 0:16]).then_inc(E_.h["pe"], 1))
        t = E_.inc("pe")
        E_.wait("vector", "pe", t)
        sizes = {n: float(QW_SHAPE[n][0] * QW_SHAPE[n][1]) for n in QW}

        def mprep(e):
            for wi, n in enumerate(QW):
                e.tensor_scalar(out=mrow[0:1, wi:wi + 1],
                                in0=PSf[0:1, 2176 + wi:2177 + wi],
                                scalar1=1.0 / sizes[n], scalar2=1e-5,
                                op0=ALU.mult, op1=ALU.max)
            e.drain()
            e.nop().then_inc(E_.h["vx"], 1)
        E_.add("vector", mprep)
        E_.inc("vx")
        E_.barrier()

        if _stop(1):
            return nc
        xsl3 = x_fm.rearrange("(c p) t -> p c t", p=128)
        for s_ in range(16):
            t0 = s_ * 256
            xs3 = F1.rearrange("p (c w) -> p c w", w=256)
            tlw = _dma(E_, "sync", xs3, xsl3[:, :, t0:t0 + 256])
            E_.wait("vector", "lw", tlw)
            E_.wait("scalar", "lw", tlw)
            x_of = lambda c: F1[:, c * 256:(c + 1) * 256]
            xn_of = lambda c: A3[:, c * 256:(c + 1) * 256]
            tln = ln_block(E_, x_of, 16, 256, xn_of, identf[:], onesr[:], F2, rows, PS)
            E_.wait("tensor", "vx", tln)
            for m in range(4):
                ps = PS["pool"][m % 2]
                ps2 = PS["pool"][2 + m % 2]
                if m >= 2:
                    E_.wait_ev("tensor")
                for kc in range(16):
                    lhsT = wk_sl[:, kc * 512 + m * 128: kc * 512 + m * 128 + 128]
                    if kc == 15:
                        E_.add("tensor", lambda e, o=ps, l=lhsT, c=kc: e.matmul(
                            out=o[:, 0:256], lhsT=l,
                            rhs=A3[:, c * 256:(c + 1) * 256],
                            start=(c == 0), stop=True).then_inc(E_.h["pe"], 1))
                    else:
                        E_.add("tensor", lambda e, o=ps, l=lhsT, c=kc: e.matmul(
                            out=o[:, 0:256], lhsT=l,
                            rhs=A3[:, c * 256:(c + 1) * 256],
                            start=(c == 0), stop=False))
                tpe = E_.inc("pe")
                rope_emit(E_, ps[:, 0:256], ps2[:, 0:256], krop[:, 0:256],
                          kcos[:, t0:t0 + 256], ksin[:, t0:t0 + 256],
                          A1[:, m * 4096 + t0: m * 4096 + t0 + 256], 256,
                          tpe, rotb[:], F2[:, 0:256], F2[:, 256:512])
            for tt in range(2):
                ps = PS["pool"][tt]
                E_.wait_ev("tensor")
                for kc in range(16):
                    lhsT = A3[:, kc * 256 + tt * 128: kc * 256 + tt * 128 + 128]
                    rhs = wv_sl[:, kc * 512:(kc + 1) * 512]
                    if kc == 15:
                        E_.add("tensor", lambda e, o=ps, l=lhsT, r=rhs: e.matmul(
                            out=o[:], lhsT=l, rhs=r, start=False,
                            stop=True).then_inc(E_.h["pe"], 1))
                    else:
                        E_.add("tensor", lambda e, o=ps, l=lhsT, r=rhs, c=kc:
                               e.matmul(out=o[:], lhsT=l, rhs=r, start=(c == 0),
                                        stop=False))
                tpe = E_.inc("pe")
                E_.wait("scalar", "pe", tpe)
                gt = (t0 + tt * 128) // 128
                E_.add("scalar", lambda e, o=A2[:, gt * 512:(gt + 1) * 512],
                       i_=ps: (e.copy(out=o, in_=i_[:]), e.drain(),
                               e.nop().then_inc(E_.h["sx"], 1)))
                E_.inc("sx")
            E_.wait_ev("sync")
            E_.wait("sync", "sx", E_.c["sx"])
            E_.wait("sync", "vx", E_.c["vx"])
        E_.barrier()

        if _stop(2):
            return nc
        xo3 = x_own.rearrange("(c p) t -> p c t", p=128)
        for s_ in range(4):
            t0 = s_ * 256
            xs3 = F1.rearrange("p (c w) -> p c w", w=256)
            tlw = _dma(E_, "sync", xs3, xo3[:, :, t0:t0 + 256])
            E_.wait("vector", "lw", tlw)
            E_.wait("scalar", "lw", tlw)
            x_of = lambda c: F1[:, c * 256:(c + 1) * 256]
            xn_of = lambda c, t0=t0: A3[:, c * 1024 + t0: c * 1024 + t0 + 256]
            tln = ln_block(E_, x_of, 16, 256, xn_of, identf[:], onesr[:], F2, rows, PS)
            E_.wait("sync", "vx", tln)
        E_.barrier()

        if _stop(3):
            return nc
        tmsk = _dma(E_, "sync", mskb.rearrange("p (i w) -> p i w", w=512),
                    masks_d.rearrange("i p w -> p i w"))
        E_.wait("vector", "lw", tmsk)
        wq3 = wt["q"].rearrange("(kc p) m -> p kc m", p=128)
        for h in range(16):
            kvh = h // 4
            wqh3 = wqh.rearrange("p (kc m) -> p kc m", m=128)
            E_.wait("sync", "pe", E_.c["pe"])
            tlw = _dma(E_, "sync", wqh3, wq3[:, :, h * 128:(h + 1) * 128])
            E_.wait("tensor", "lw", tlw)
            for th in range(2):
                ps = PS["pool"][th]
                ps2 = PS["pool"][2 + th]
                E_.wait_ev("tensor")
                for kc in range(16):
                    lhsT = wqh[:, kc * 128:(kc + 1) * 128]
                    rhs = A3[:, kc * 1024 + th * 512: kc * 1024 + th * 512 + 512]
                    if kc == 15:
                        E_.add("tensor", lambda e, o=ps, l=lhsT, r=rhs: e.matmul(
                            out=o[:], lhsT=l, rhs=r, start=False,
                            stop=True).then_inc(E_.h["pe"], 1))
                    else:
                        E_.add("tensor", lambda e, o=ps, l=lhsT, r=rhs, c=kc:
                               e.matmul(out=o[:], lhsT=l, rhs=r, start=(c == 0),
                                        stop=False))
                tpe = E_.inc("pe")
                rope_emit(E_, ps, ps2, krop[:, 0:512],
                          qcos[:, th * 512:(th + 1) * 512],
                          qsin[:, th * 512:(th + 1) * 512],
                          qh[:, th * 512:(th + 1) * 512], 512, tpe, rotb[:],
                          sc_extra[:, 0:512], scm[:, 0:512])
            E_.wait_ev("tensor")

            def p4cut(n):
                if MAIN_P4SUB >= n:
                    return False
                E_.wait("tensor", "vx", E_.c["vx"])
                E_.wait("tensor", "sx", E_.c["sx"])
                return True

            for i in range(8):
                if p4cut(2):
                    continue
                NK = min(4 * i + 5, 32)
                pre = max(0, NK - 28)
                mstart = 4 * i + 1
                mend = min(4 * i + 4, 31)
                mw_ = 128 * (mend - mstart + 1)
                qt = qh[:, 128 * i:128 * i + 128]
                if pre:
                    for k in range(pre):
                        if k == pre - 1:
                            E_.add("tensor", lambda e, k=k, qt=qt, kvh=kvh:
                                   e.matmul(out=PSf[:, 128 * k:128 * k + 128],
                                            lhsT=qt,
                                            rhs=A1[:, kvh * 4096 + 128 * k:
                                                   kvh * 4096 + 128 * (k + 1)],
                                            start=True,
                                            stop=True).then_inc(E_.h["pe"], 1))
                        else:
                            E_.add("tensor", lambda e, k=k, qt=qt, kvh=kvh:
                                   e.matmul(out=PSf[:, 128 * k:128 * k + 128],
                                            lhsT=qt,
                                            rhs=A1[:, kvh * 4096 + 128 * k:
                                                   kvh * 4096 + 128 * (k + 1)],
                                            start=True, stop=True))
                    tpe = E_.inc("pe")
                    E_.wait("scalar", "pe", tpe)
                    E_.add("scalar", lambda e, w=128 * pre: (
                        e.copy(out=sc_extra[:, 0:w], in_=PSf[:, 0:w]), e.drain(),
                        e.nop().then_inc(E_.h["sx"], 1)))
                    tsx = E_.inc("sx")
                    E_.wait("tensor", "sx", tsx)
                k = pre
                groups = []
                while k < NK:
                    n = min(4, NK - k)
                    groups.append((k, n))
                    k += n
                for gi, (k, n) in enumerate(groups):
                    last = gi == len(groups) - 1

                    def qk(e, k=k, n=n, qt=qt, kvh=kvh, pre=pre, last=last):
                        mm = e.matmul(
                            out=PSf[:, 128 * (k - pre):128 * (k - pre) + 128 * n],
                            lhsT=qt,
                            rhs=A1[:, kvh * 4096 + 128 * k:
                                   kvh * 4096 + 128 * (k + n)],
                            start=True, stop=True)
                        if last:
                            mm.then_inc(E_.h["pe"], 1)
                    E_.add("tensor", qk)
                tqk = E_.inc("pe")
                if p4cut(3):
                    continue
                E_.wait("vector", "pe", tqk)
                E_.add("vector", lambda e, o=128 * (mstart - pre), w=mw_, i=i: (
                    e.tensor_tensor(out=scm[:, 0:w], in0=PSf[:, o:o + w],
                                    in1=mskb[:, i * 512:i * 512 + w], op=ALU.add),
                    e.drain()))

                def stats(e, w_clean=128 * (mstart - pre), w=mw_, pre=pre):
                    e.tensor_reduce(out=stat["mx1"][:], in_=PSf[:, 0:w_clean],
                                    axis=AX.X, op=ALU.max)
                    e.tensor_reduce(out=stat["mx2"][:], in_=scm[:, 0:w],
                                    axis=AX.X, op=ALU.max)
                    e.drain()
                    e.tensor_tensor(out=stat["mx1"][:], in0=stat["mx1"][:],
                                    in1=stat["mx2"][:], op=ALU.max)
                    e.drain()
                    if pre:
                        e.tensor_reduce(out=stat["mx2"][:],
                                        in_=sc_extra[:, 0:128 * pre],
                                        axis=AX.X, op=ALU.max)
                        e.drain()
                        e.tensor_tensor(out=stat["mx1"][:], in0=stat["mx1"][:],
                                        in1=stat["mx2"][:], op=ALU.max)
                        e.drain()
                    e.tensor_scalar(out=stat["neg"][:], in0=stat["mx1"][:],
                                    scalar1=-1.0, scalar2=None, op0=ALU.mult)
                    e.drain()
                    e.nop().then_inc(E_.h["vx"], 1)
                E_.add("vector", stats)
                tst = E_.inc("vx")
                if p4cut(4):
                    continue
                E_.wait("scalar", "vx", tst)

                def expf(e, pre=pre, mstart=mstart, NK=NK, w=mw_):
                    if pre:
                        e.activation(out=pr[:, 0:128 * pre],
                                     in_=sc_extra[:, 0:128 * pre], func=AF.Exp,
                                     bias=stat["neg"][:, 0:1])
                    e.activation(out=pr[:, 128 * pre:128 * mstart],
                                 in_=PSf[:, 0:128 * (mstart - pre)], func=AF.Exp,
                                 bias=stat["neg"][:, 0:1])
                    e.activation(out=pr[:, 128 * mstart:128 * mstart + w],
                                 in_=scm[:, 0:w], func=AF.Exp,
                                 bias=stat["neg"][:, 0:1])
                    e.drain()
                    e.nop().then_inc(E_.h["sx"], 1)
                E_.add("scalar", expf)
                texp = E_.inc("sx")
                if p4cut(5):
                    continue
                E_.wait("vector", "sx", texp)

                def sumf(e, NK=NK):
                    e.tensor_reduce(out=stat["sum"][:], in_=pr[:, 0:128 * NK],
                                    axis=AX.X, op=ALU.add)
                    e.drain()
                    e.reciprocal(out=stat["rcp"][:], in_=stat["sum"][:])
                    e.drain()
                    e.nop().then_inc(E_.h["vx"], 1)
                E_.add("vector", sumf)
                trcp = E_.inc("vx")
                if p4cut(6):
                    continue
                E_.wait("tensor", "sx", texp)
                slot_t = [None] * 7
                for k in range(NK):
                    sl_ = k % 7
                    sap = psbslot[sl_]
                    if slot_t[sl_] is not None:
                        E_.wait_ev("tensor", slot_t[sl_])
                    E_.add("tensor", lambda e, k=k, sap=sap: e.transpose(
                        out=sap,
                        in_=pr[:, 128 * k:128 * k + 128],
                        identity=identb[:]).then_inc(E_.h["pe"], 1))
                    tp = E_.inc("pe")
                    eng = "scalar" if k % 2 == 0 else "vector"
                    E_.wait(eng, "pe", tp)
                    if eng == "scalar":
                        E_.inc("evs")
                        E_.add("scalar", lambda e, k=k, sap=sap: (
                            e.copy(out=pT[:, 128 * k:128 * k + 128],
                                   in_=sap).then_inc(E_.h["evs"], 1),
                            e.drain()))
                    else:
                        E_.inc("evv")
                        E_.add("vector", lambda e, k=k, sap=sap: (
                            e.tensor_copy(out=pT[:, 128 * k:128 * k + 128],
                                          in_=sap).then_inc(E_.h["evv"], 1),
                            e.drain()))
                    slot_t[sl_] = E_.ev_now()
                E_.wait("tensor", "vx", trcp)
                E_.add("tensor", lambda e: e.transpose(
                    out=PSf[0:1, 640:768], in_=stat["rcp"][:],
                    identity=identf[:]).then_inc(E_.h["pe"], 1))
                t1 = E_.inc("pe")
                E_.wait("scalar", "pe", t1)
                E_.add("scalar", lambda e: (e.copy(out=rrow[:], in_=PSf[0:1, 640:768]),
                                            e.drain(),
                                            e.nop().then_inc(E_.h["sx"], 1)))
                t2 = E_.inc("sx")
                E_.wait("tensor", "sx", t2)
                t3 = bcast_row(E_, rrow[0:1, 0:128], 128, PSf[:, 512:640], onesr[:])
                E_.wait("scalar", "pe", t3)
                E_.add("scalar", lambda e: (e.copy(out=rcpb[:], in_=PSf[:, 512:640]),
                                            e.drain(),
                                            e.nop().then_inc(E_.h["sx"], 1)))
                trcb = E_.inc("sx")
                if p4cut(7):
                    continue
                E_.wait_ev("tensor")
                for k in range(NK):
                    if k == NK - 1:
                        E_.add("tensor", lambda e, k=k, kvh=kvh: e.matmul(
                            out=PSf[:, 0:128],
                            lhsT=A2[:, 512 * k + 128 * kvh:
                                   512 * k + 128 * kvh + 128],
                            rhs=pT[:, 128 * k:128 * k + 128],
                            start=(k == 0),
                            stop=True).then_inc(E_.h["pe"], 1))
                    else:
                        E_.add("tensor", lambda e, k=k, kvh=kvh: e.matmul(
                            out=PSf[:, 0:128],
                            lhsT=A2[:, 512 * k + 128 * kvh:
                                   512 * k + 128 * kvh + 128],
                            rhs=pT[:, 128 * k:128 * k + 128],
                            start=(k == 0), stop=False))
                tav = E_.inc("pe")
                if p4cut(8):
                    continue
                E_.wait("vector", "pe", tav)
                E_.wait("vector", "sx", trcb)
                E_.add("vector", lambda e, h=h, i=i: (
                    e.tensor_tensor(
                        out=A4[:, h * 1024 + 128 * i: h * 1024 + 128 * i + 128],
                        in0=PSf[:, 0:128], in1=rcpb[:], op=ALU.mult),
                    e.drain(), e.nop().then_inc(E_.h["vx"], 1)))
                tfin = E_.inc("vx")
                E_.wait("tensor", "vx", tfin)
                E_.wait("tensor", "sx", E_.c["sx"])
        E_.barrier()

        if _stop(4):
            return nc
        def wo_evac(m0, t0, ps, tgt):
            E_.wait("scalar", "pe", tgt)
            E_.add("scalar", lambda e, m0=m0, t0=t0, ps=ps: (
                e.copy(out=A1[:, (m0 // 128) * 1024 + t0:
                              (m0 // 128) * 1024 + t0 + 512],
                       in_=ps[:]).then_inc(E_.h["evs"], 1), e.drain()))
            E_.inc("evs")
        E_.wait("sync", "pe", E_.c["pe"])
        mm_fm2(E_, wt["o"], D, D,
               lambda kc, t0: A4[:, kc * 1024 + t0: kc * 1024 + t0 + 512],
               T, wo_evac, [W0, W1], PS["pool"])
        E_.barrier()

        if _stop(5):
            return nc
        rsb_g, rsb_u, rsb_d = F1[:, 0:512], F1[:, 512:1024], F1[:, 1024:1536]
        ftmp = F1[:, 1536:2048]
        g0f = F1[:, 2048:2560]
        xof = F1[:, 2560:3072]
        x1f = [F1[:, 3072:3584], F1[:, 3584:4096]]

        for p_ in range(4):
            t0 = p_ * 256
            tq = quant_block(E_, lambda c, t0=t0: A3[:, c * 1024 + t0: c * 1024 + t0 + 256],
                             16, 256, 128.0,
                             lambda c, t0=t0: A3[:, c * 1024 + t0: c * 1024 + t0 + 256],
                             rows, F1, F2, ibuf, identf[:], onesr[:], PS, clip_hi=True)
            dequant_row(E_, rows["q"], rows["r1"], t0, 256,
                        mrow[0:1, QW_IDX["L0g"]:QW_IDX["L0g"] + 1], 128.0)
            dequant_row(E_, rows["q"], rows["r2"], t0, 256,
                        mrow[0:1, QW_IDX["L0u"]:QW_IDX["L0u"] + 1], 128.0)
        E_.barrier()

        ic_ap = lambda c: (A2[:, c * 512:(c + 1) * 512] if c < 32
                           else A4[:, (c - 32) * 512:(c - 31) * 512])
        for sub in range(2):
            st0 = sub * 512
            for rsrc, rdst in [(rows["r1"], rsb_g), (rows["r2"], rsb_u)]:
                tb = bcast_row(E_, rsrc[0:1, st0:st0 + 512], 512, PS["b1"], onesr[:])
                E_.wait("scalar", "pe", tb)
                E_.add("scalar", lambda e, d=rdst: (
                    e.copy(out=d, in_=PS["b1"]), e.drain(),
                    e.nop().then_inc(E_.h["sx"], 1)))
                E_.inc("sx")
            E_.wait("vector", "sx", E_.c["sx"])

            g_sx = [E_.c["sx"], E_.c["sx"]]

            def gate_evac(m0, t0, ps, tgt):
                c = m0 // 128
                b = c % 2
                ft = ftmp if b == 0 else F2[:, 0:512]
                E_.wait("vector", "pe", tgt)
                E_.wait("vector", "sx", g_sx[b])
                E_.add("vector", lambda e, ps=ps, ft=ft: (
                    e.tensor_tensor(out=ft, in0=ps[:], in1=rsb_g,
                                    op=ALU.mult).then_inc(E_.h["evv"], 1),
                    e.drain(), e.nop().then_inc(E_.h["vx"], 1)))
                E_.inc("evv")
                tv = E_.inc("vx")
                E_.wait("scalar", "vx", tv)
                E_.add("scalar", lambda e, c=c, ft=ft: (
                    e.activation(out=ic_ap(c), in_=ft,
                                 func=AF.Silu).then_inc(E_.h["sx"], 1),
                    e.drain()))
                g_sx[b] = E_.inc("sx")
            E_.wait("sync", "pe", E_.c["pe"])
            mm_fm2(E_, wt["g0"], D, I0,
                   lambda kc, t0, st0=st0: A3[:, kc * 1024 + st0: kc * 1024 + st0 + 512],
                   512, gate_evac, [W0, W1], PS["pool"])

            def up_evac(m0, t0, ps, tgt):
                c = m0 // 128
                E_.wait("vector", "pe", tgt)
                E_.wait("vector", "sx", E_.c["sx"])
                def f(e, c=c, ps=ps):
                    e.tensor_tensor(out=ftmp, in0=ps[:], in1=ic_ap(c), op=ALU.mult)
                    e.drain()
                    e.tensor_tensor(out=ic_ap(c), in0=ftmp, in1=rsb_u,
                                    op=ALU.mult).then_inc(E_.h["evv"], 1)
                    e.drain()
                E_.add("vector", f)
                E_.inc("evv")
            E_.wait("sync", "pe", E_.c["pe"])
            mm_fm2(E_, wt["u0"], D, I0,
                   lambda kc, t0, st0=st0: A3[:, kc * 1024 + st0: kc * 1024 + st0 + 512],
                   512, up_evac, [W0, W1], PS["pool"])
            E_.barrier()

            fwht_pe_stage(E_, ic_ap, 64, 512, h128[:], PS["pool"])
            fwht_chunk_stages(E_, ic_ap, 64, 512, stg)
            for half in range(2):
                t0 = half * 256
                tq = quant_block(E_, lambda c, t0=t0: ic_ap(c)[:, t0:t0 + 256],
                                 64, 256, 128.0,
                                 lambda c, t0=t0: ic_ap(c)[:, t0:t0 + 256],
                                 rows, F1, F2, ibuf, identf[:], onesr[:], PS,
                                 prescale=float(1.0 / np.sqrt(I0)), clip_hi=True)
                dequant_row(E_, rows["q"], rows["r1"], t0, 256,
                            mrow[0:1, QW_IDX["L0d"]:QW_IDX["L0d"] + 1], 128.0)
            tb = bcast_row(E_, rows["r1"][0:1, 0:512], 512, PS["b1"], onesr[:])
            E_.wait("scalar", "pe", tb)
            E_.add("scalar", lambda e: (e.copy(out=rsb_d, in_=PS["b1"]), e.drain(),
                                        e.nop().then_inc(E_.h["sx"], 1)))
            E_.inc("sx")
            E_.wait("vector", "sx", E_.c["sx"])

            xo3b = x_own.rearrange("(c p) t -> p c t", p=128)

            def down_evac(m0, t0, ps, tgt, st0=sub * 512, sub=sub):
                c = m0 // 128
                buf = x1f[c % 2]
                E_.wait("sync", "vx", E_.c["vx"])
                tld = _dma(E_, "sync", xof, xo3b[:, c, st0:st0 + 512])
                E_.wait("vector", "pe", tgt)
                E_.wait("vector", "sx", E_.c["sx"])
                E_.wait("vector", "lw", tld)
                E_.wait("vector", "lg", E_.c["lg"])
                E_.add("vector", lambda e, ps=ps: (
                    e.tensor_tensor(out=ftmp, in0=ps[:], in1=rsb_d,
                                    op=ALU.mult).then_inc(E_.h["evv"], 1),
                    e.drain(), e.nop().then_inc(E_.h["vx"], 1)))
                E_.inc("evv")
                tv = E_.inc("vx")
                E_.wait("scalar", "vx", tv)
                E_.add("scalar", lambda e: (
                    e.activation(out=g0f, in_=ftmp, func=AF.Silu), e.drain(),
                    e.nop().then_inc(E_.h["sx"], 1)))
                tsx = E_.inc("sx")
                E_.wait("vector", "sx", tsx)

                def fin(e, c=c, buf=buf, st0=st0):
                    e.tensor_tensor(out=ftmp,
                                    in0=A1[:, c * 1024 + st0: c * 1024 + st0 + 512],
                                    in1=g0f, op=ALU.mult)
                    e.drain()
                    e.tensor_tensor(out=buf, in0=ftmp, in1=xof, op=ALU.add)
                    e.drain()
                    e.tensor_copy(out=A1[:, c * 1024 + st0: c * 1024 + st0 + 512],
                                  in_=buf)
                    e.drain()
                    e.nop().then_inc(E_.h["vx"], 1)
                E_.add("vector", fin)
                tv2 = E_.inc("vx")
                E_.wait("gpsimd", "vx", tv2)
                _dma(E_, "gpsimd",
                     x1_o[:, c * 1024 + st0: c * 1024 + st0 + 512], buf, "lg")
                _dma(E_, "gpsimd",
                     x1i_d[:, c * 1024 + st0: c * 1024 + st0 + 512], buf, "lg")
            E_.wait("sync", "pe", E_.c["pe"])
            mm_fm2(E_, wt["d0"], I0, D,
                   lambda kc, t0: ic_ap(kc)[:, t0:t0 + 512],
                   512, down_evac, [W0, W1], PS["pool"], mw=128, ksplit=2)
            E_.wait("gpsimd", "lg", E_.c["lg"])
            E_.barrier()
        E_.barrier()
        if _stop(6):
            return nc
        A2f = A2[:].bitcast(F32)
        A3f = A3[:].bitcast(F32)
        A4f = A4[:].bitcast(F32)
        xn1f = lambda c: (A2f[:, c * 1024:(c + 1) * 1024] if c < 8
                          else A3f[:, (c - 8) * 1024:(c - 7) * 1024])
        x1i3 = x1i_d.rearrange("p (c t) -> p c t", t=T)
        acc_s = A4f[:, 0:1024]
        acc_q = A4f[:, 1024:2048]
        strm = A4f[:, 2048:3072]
        sqt = A4f[:, 3072:4096]
        E_.add("vector", lambda e: (e.memset(acc_s, 0.0), e.memset(acc_q, 0.0),
                                    e.drain()))
        for c in range(16):
            tld = _dma(E_, "sync", strm, x1i3[:, c, :])
            E_.wait("vector", "lw", tld)

            def accf(e):
                e.tensor_tensor(out=acc_s, in0=acc_s, in1=strm, op=ALU.add)
                e.drain()
                e.tensor_tensor(out=sqt, in0=strm, in1=strm, op=ALU.mult)
                e.drain()
                e.tensor_tensor(out=acc_q, in0=acc_q, in1=sqt, op=ALU.add)
                e.drain()
                e.nop().then_inc(E_.h["vx"], 1)
            E_.add("vector", accf)
            E_.inc("vx")
            E_.wait("sync", "vx", E_.c["vx"])
        for p_ in range(4):
            t0 = p_ * 256
            col_reduce_to_row(E_, acc_s[:, t0:t0 + 256], 256, rows["m"], ALU.add,
                              identf[:], rows["tc"], PS["pst"], PS["psr"])
            col_reduce_to_row(E_, acc_q[:, t0:t0 + 256], 256, rows["s"], ALU.add,
                              identf[:], rows["tc"], PS["pst"], PS["psr"])

            def lnrow(e, t0=t0):
                e.tensor_scalar(out=rows["m"][0:1, 0:256], in0=rows["m"][0:1, 0:256],
                                scalar1=1.0 / D, scalar2=None, op0=ALU.mult)
                e.tensor_scalar(out=rows["s"][0:1, 0:256], in0=rows["s"][0:1, 0:256],
                                scalar1=1.0 / D, scalar2=None, op0=ALU.mult)
                e.drain()
                e.tensor_tensor(out=rows["v"][0:1, 0:256], in0=rows["m"][0:1, 0:256],
                                in1=rows["m"][0:1, 0:256], op=ALU.mult)
                e.drain()
                e.tensor_tensor(out=rows["v"][0:1, 0:256], in0=rows["s"][0:1, 0:256],
                                in1=rows["v"][0:1, 0:256], op=ALU.subtract)
                e.drain()
                e.tensor_scalar(out=rows["v"][0:1, 0:256], in0=rows["v"][0:1, 0:256],
                                scalar1=1e-5, scalar2=None, op0=ALU.add)
                e.drain()
                e.nop().then_inc(E_.h["vx"], 1)
            E_.add("vector", lnrow)
            tv = E_.inc("vx")
            E_.wait("scalar", "vx", tv)
            E_.add("scalar", lambda e: (
                e.activation(out=rows["v"][0:1, 0:256], in_=rows["v"][0:1, 0:256],
                             func=AF.Sqrt),
                e.drain(), e.nop().then_inc(E_.h["sx"], 1)))
            ts = E_.inc("sx")
            E_.wait("vector", "sx", ts)

            def lnfin(e, t0=t0):
                e.reciprocal(out=rows["v"][0:1, 0:256], in_=rows["v"][0:1, 0:256])
                e.drain()
                e.tensor_scalar(out=rows["m"][0:1, 0:256], in0=rows["m"][0:1, 0:256],
                                scalar1=-1.0, scalar2=None, op0=ALU.mult)
                e.drain()
                e.tensor_copy(out=rows["r1"][0:1, t0:t0 + 256],
                              in_=rows["m"][0:1, 0:256])
                e.tensor_copy(out=rows["r2"][0:1, t0:t0 + 256],
                              in_=rows["v"][0:1, 0:256])
                e.drain()
                e.nop().then_inc(E_.h["vx"], 1)
            E_.add("vector", lnfin)
            E_.inc("vx")
        E_.add("vector", lambda e: (e.memset(acc_q, 0.0), e.drain()))
        for half in range(2):
            t0 = half * 512
            tb1 = bcast_row(E_, rows["r1"][0:1, t0:t0 + 512], 512, PS["b1"], onesr[:])
            tb2 = bcast_row(E_, rows["r2"][0:1, t0:t0 + 512], 512, PS["b2"], onesr[:])
            E_.wait("vector", "pe", tb2)
            for c in range(16):
                tld = _dma(E_, "sync", strm[:, 0:512], x1i3[:, c, t0:t0 + 512])
                E_.wait("vector", "lw", tld)

                def apl(e, c=c, t0=t0):
                    e.tensor_tensor(out=sqt[:, 0:512], in0=strm[:, 0:512],
                                    in1=PS["b1"], op=ALU.add)
                    e.drain()
                    e.tensor_tensor(out=xn1f(c)[:, t0:t0 + 512], in0=sqt[:, 0:512],
                                    in1=PS["b2"], op=ALU.mult)
                    e.drain()
                    e.tensor_tensor(out=sqt[:, 0:512],
                                    in0=xn1f(c)[:, t0:t0 + 512],
                                    in1=xn1f(c)[:, t0:t0 + 512], op=ALU.mult)
                    e.drain()
                    e.tensor_tensor(out=acc_q[:, t0:t0 + 512],
                                    in0=acc_q[:, t0:t0 + 512],
                                    in1=sqt[:, 0:512], op=ALU.add)
                    e.drain()
                    e.nop().then_inc(E_.h["vx"], 1)
                E_.add("vector", apl)
                E_.inc("vx")
                E_.wait("sync", "vx", E_.c["vx"])
        for p_ in range(4):
            t0 = p_ * 256
            col_reduce_to_row(E_, acc_q[:, t0:t0 + 256], 256, rows["s"], ALU.add,
                              identf[:], rows["tc"], PS["pst"], PS["psr"])

            def rm2(e, t0=t0):
                e.tensor_scalar(out=rows["v"][0:1, 0:256], in0=rows["s"][0:1, 0:256],
                                scalar1=1.0 / D, scalar2=1e-8, op0=ALU.mult,
                                op1=ALU.add)
                e.drain()
                e.nop().then_inc(E_.h["vx"], 1)
            E_.add("vector", rm2)
            tv = E_.inc("vx")
            E_.wait("scalar", "vx", tv)
            E_.add("scalar", lambda e: (
                e.activation(out=rows["v"][0:1, 0:256], in_=rows["v"][0:1, 0:256],
                             func=AF.Sqrt),
                e.drain(), e.nop().then_inc(E_.h["sx"], 1)))
            ts = E_.inc("sx")
            E_.wait("vector", "sx", ts)

            def rm2b(e, t0=t0):
                e.reciprocal(out=rows["v"][0:1, 0:256], in_=rows["v"][0:1, 0:256])
                e.drain()
                e.tensor_copy(out=rows["r3"][0:1, t0:t0 + 256],
                              in_=rows["v"][0:1, 0:256])
                e.drain()
                e.nop().then_inc(E_.h["vx"], 1)
            E_.add("vector", rm2b)
            E_.inc("vx")
        E_.barrier()

        rsb0, rsb1 = F1[:, 512:1024], F1[:, 1024:1536]
        ftmp7 = F1[:, 1536:2048]
        ftmp7b = F1[:, 2048:2560]
        i_d3 = i_d.rearrange("p (c t) -> p c t", t=T)

        def gate_half(widx, wname, half, evac):
            t0h = half * 512
            for q_ in range(2):
                t0q = t0h + q_ * 256
                tb2 = bcast_row(E_, rows["r3"][0:1, t0q:t0q + 256], 256,
                                PS["b2"][:, 0:256], onesr[:])
                E_.wait("vector", "pe", tb2)
                for c in range(16):
                    def zp(e, c=c, t0q=t0q, widx=widx):
                        e.tensor_tensor(out=ftmp7[:, 0:256],
                                        in0=xn1f(c)[:, t0q:t0q + 256],
                                        in1=PS["b2"][:, 0:256], op=ALU.mult)
                        e.drain()
                        e.tensor_scalar(out=A4f[:, c * 256:(c + 1) * 256],
                                        in0=ftmp7[:, 0:256],
                                        scalar1=nrm[:, 16 * widx + c:
                                                    16 * widx + c + 1],
                                        scalar2=None, op0=ALU.mult)
                        e.drain()
                        e.nop().then_inc(E_.h["vx"], 1)
                    E_.add("vector", zp)
                    E_.inc("vx")
                quant_block(E_, lambda c: A4f[:, c * 256:(c + 1) * 256],
                            16, 256, 127.0,
                            lambda c, q_=q_: A4[:, 8192 + c * 512 + q_ * 256:
                                               8192 + c * 512 + q_ * 256 + 256],
                            rows, F1, F2, ibuf, identf[:], onesr[:], PS,
                            clip_hi=False)
                dequant_row(E_, rows["q"], rows["r2"], q_ * 256, 256,
                            mrow[0:1, QW_IDX[wname]:QW_IDX[wname] + 1], 127.0)
            tb = bcast_row(E_, rows["r2"][0:1, 0:512], 512, PS["b1"], onesr[:])
            E_.wait("scalar", "pe", tb)
            E_.add("scalar", lambda e: (
                e.copy(out=rsb0, in_=PS["b1"]), e.drain(),
                e.nop().then_inc(E_.h["sx"], 1)))
            E_.inc("sx")
            E_.wait("vector", "sx", E_.c["sx"])
            E_.wait("sync", "pe", E_.c["pe"])
            mm_fm2(E_, wt[{"Wi": "wi", "Wf": "wf", "Wg": "wg"}[wname]], D, D,
                   lambda kc, t0: A4[:, 8192 + kc * 512: 8192 + kc * 512 + 512],
                   512, evac, [W0, W1], PS["pool"])

        for half in range(2):
            f_sx = [None, None]

            def f_evac(m0, t0, ps, tgt, half=half, f_sx=f_sx):
                c = m0 // 128
                b = c % 2
                ft = ftmp7 if b == 0 else F2[:, 0:512]
                E_.wait("vector", "pe", tgt)
                E_.wait("vector", "sx",
                        E_.c["sx"] if f_sx[b] is None else f_sx[b])
                E_.add("vector", lambda e, ps=ps, ft=ft: (
                    e.tensor_tensor(out=ft, in0=ps[:], in1=rsb0,
                                    op=ALU.mult).then_inc(E_.h["evv"], 1),
                    e.drain(), e.nop().then_inc(E_.h["vx"], 1)))
                E_.inc("evv")
                tv = E_.inc("vx")
                E_.wait("scalar", "vx", tv)
                E_.add("scalar", lambda e, c=c, half=half, ft=ft: (
                    e.activation(out=A1[:, c * 1024 + half * 512:
                                       c * 1024 + half * 512 + 512],
                                 in_=ft, func=AF.Sigmoid).then_inc(E_.h["sx"], 1),
                    e.drain()))
                f_sx[b] = E_.inc("sx")
            gate_half(1, "Wf", half, f_evac)
        E_.barrier()
        for half in range(2):
            def i_evac(m0, t0, ps, tgt, half=half):
                c = m0 // 128
                E_.wait("vector", "pe", tgt)
                E_.wait("vector", "sx", E_.c["sx"])
                E_.add("vector", lambda e, ps=ps: (
                    e.tensor_tensor(out=ftmp7, in0=ps[:], in1=rsb0,
                                    op=ALU.mult).then_inc(E_.h["evv"], 1),
                    e.drain(), e.nop().then_inc(E_.h["vx"], 1)))
                E_.inc("evv")
                tv = E_.inc("vx")
                E_.wait("scalar", "vx", tv)
                E_.add("scalar", lambda e: (
                    e.activation(out=ftmp7b, in_=ftmp7,
                                 func=AF.Silu).then_inc(E_.h["sx"], 1),
                    e.drain()))
                ts = E_.inc("sx")
                E_.wait("vector", "sx", ts)
                E_.wait("vector", "lg", E_.c["lg"])

                def fin(e, c=c, half=half):
                    e.tensor_scalar(out=ftmp7,
                                    in0=A1[:, c * 1024 + half * 512:
                                           c * 1024 + half * 512 + 512],
                                    scalar1=-1.0, scalar2=1.0, op0=ALU.mult,
                                    op1=ALU.add)
                    e.drain()
                    e.tensor_tensor(out=stg[:, 0:512], in0=ftmp7b, in1=ftmp7,
                                    op=ALU.mult)
                    e.drain()
                    e.nop().then_inc(E_.h["vx"], 1)
                E_.add("vector", fin)
                tv2 = E_.inc("vx")
                E_.wait("gpsimd", "vx", tv2)
                _dma(E_, "gpsimd", i_d3[:, c, half * 512:half * 512 + 512],
                     stg[:, 0:512], "lg")
            gate_half(0, "Wi", half, i_evac)
        E_.barrier()
        for half in range(2):
            g_sx7 = [None, None]
            g_lg7 = [None, None]

            def g_evac(m0, t0, ps, tgt, half=half, g_sx7=g_sx7, g_lg7=g_lg7):
                c = m0 // 128
                b = c % 2
                ft = ftmp7 if b == 0 else F2[:, 0:512]
                sg = stg[:, 0:512] if b == 0 else stg[:, 512:1024]
                E_.wait("vector", "pe", tgt)
                E_.wait("vector", "sx",
                        E_.c["sx"] if g_sx7[b] is None else g_sx7[b])
                E_.add("vector", lambda e, ps=ps, ft=ft: (
                    e.tensor_tensor(out=ft, in0=ps[:], in1=rsb0,
                                    op=ALU.mult).then_inc(E_.h["evv"], 1),
                    e.drain(), e.nop().then_inc(E_.h["vx"], 1)))
                E_.inc("evv")
                tv = E_.inc("vx")
                E_.wait("scalar", "vx", tv)
                if g_lg7[b] is not None:
                    E_.wait("scalar", "lg", g_lg7[b])
                else:
                    E_.wait("scalar", "lg", E_.c["lg"])
                E_.add("scalar", lambda e, ft=ft, sg=sg: (
                    e.activation(out=sg, in_=ft,
                                 func=AF.Silu).then_inc(E_.h["sx"], 1),
                    e.drain()))
                ts = E_.inc("sx")
                g_sx7[b] = ts
                E_.wait("gpsimd", "sx", ts)
                g_lg7[b] = _dma(E_, "gpsimd",
                                sg_o[:, c * 1024 + half * 512:
                                     c * 1024 + half * 512 + 512], sg, "lg")
            gate_half(2, "Wg", half, g_evac)
        E_.barrier()

        _dma(E_, "gpsimd", xn1_d[:, 0:8192], A2f[:, 0:8192], "lg")
        _dma(E_, "gpsimd", xn1_d[:, 8192:16384], A3f[:, 0:8192], "lg")
        tsp = E_.c["lg"]
        for g_ in ENGS:
            E_.wait(g_, "lg", tsp)
        tld = _dma(E_, "sync", A2[:], i_d)
        E_.wait("vector", "lw", tld)

        E_.add("vector", lambda e: (e.memset(zz[:], 0.0), e.drain()))
        for c in range(16):
            for p_ in range(8):
                o = c * 1024 + p_ * 128

                def sc1(e, o=o):
                    e.tensor_tensor_scan(out=A3[:, o:o + 128],
                                         data0=A1[:, o:o + 128],
                                         data1=A2[:, o:o + 128],
                                         initial=0.0, op0=ALU.mult, op1=ALU.add)
                    e.drain()
                    e.tensor_tensor_scan(out=A4[:, o:o + 128],
                                         data0=A1[:, o:o + 128],
                                         data1=zz[:], initial=1.0,
                                         op0=ALU.mult, op1=ALU.add)
                    e.drain()
                E_.add("vector", sc1)
        E_.add("vector", lambda e: e.nop().then_inc(E_.h["vx"], 1))
        tv = E_.inc("vx")
        for p_ in range(8):
            for c in range(16):

                def smc(e, p_=p_, c=c):
                    e.tensor_copy(out=summ[:, p_ * 32 + c:p_ * 32 + c + 1],
                                  in_=A3[:, c * 1024 + p_ * 128 + 127:
                                         c * 1024 + p_ * 128 + 128])
                    e.tensor_copy(out=summ[:, p_ * 32 + 16 + c:p_ * 32 + 17 + c],
                                  in_=A4[:, c * 1024 + p_ * 128 + 127:
                                         c * 1024 + p_ * 128 + 128])
                E_.add("vector", smc)
        E_.add("vector", lambda e: (e.drain(), e.nop().then_inc(E_.h["vx"], 1)))
        tv = E_.inc("vx")
        E_.wait("gpsimd", "vx", tv)
        _dma(E_, "gpsimd", summ_o, summ[:], "lg")
        _dma(E_, "gpsimd", sl_o, A3[:], "lg")
        _dma(E_, "gpsimd", P_o, A4[:], "lg")
        tlg = E_.c["lg"]
        for g in ENGS:
            E_.wait(g, "lg", tlg)
        E_.barrier()

        if _stop(7):
            return nc
        A1f = A1[:].bitcast(F32)
        tl8 = _dma(E_, "sync", A1f[:, 0:8192], xn1_d[:, 0:8192])
        tl8 = _dma(E_, "sync", A2f[:, 0:8192], xn1_d[:, 8192:16384])
        for g in ["vector", "tensor", "scalar"]:
            E_.wait(g, "lw", tl8)
        wsf = lambda c: (A1f[:, c * 1024:(c + 1) * 1024] if c < 8
                         else A2f[:, (c - 8) * 1024:(c - 7) * 1024])
        slotf = F1[:, 2560:3584]
        for half in range(2):
            hof = lambda c, half=half: wsf(c)[:, half * 512:half * 512 + 512]
            fwht_pe_stage(E_, hof, 16, 512, h128f[:], PS["pool"])
            fwht_chunk_stages(E_, hof, 16, 512, slotf)
        for p_ in range(4):
            t0 = p_ * 256
            quant_block(E_, lambda c, t0=t0: wsf(c)[:, t0:t0 + 256],
                        16, 256, 127.0,
                        lambda c, t0=t0: A3[:, c * 1024 + t0: c * 1024 + t0 + 256],
                        rows, F1, F2, ibuf, identf[:], onesr[:], PS,
                        prescale=float(1.0 / np.sqrt(D)), clip_hi=False)
            dequant_row(E_, rows["q"], rows["r2"], t0, 256,
                        mrow[0:1, QW_IDX["U1"]:QW_IDX["U1"] + 1], 127.0)
        for half, dst in [(0, rsb0), (1, rsb1)]:
            tb = bcast_row(E_, rows["r2"][0:1, half * 512:half * 512 + 512],
                           512, PS["b1"], onesr[:])
            E_.wait("scalar", "pe", tb)
            E_.add("scalar", lambda e, d=dst: (
                e.copy(out=d, in_=PS["b1"]), e.drain(),
                e.nop().then_inc(E_.h["sx"], 1)))
            E_.inc("sx")
        E_.wait("vector", "sx", E_.c["sx"])
        E_.barrier()
        i1f = lambda c: (A1f[:, c * 1024:(c + 1) * 1024] if c < 8
                         else A2f[:, (c - 8) * 1024:(c - 7) * 1024] if c < 16
                         else None)
        for th in range(2):
            t0h = th * 512
            ih = lambda c: (A1f[:, c * 512:(c + 1) * 512] if c < 16
                            else A2f[:, (c - 16) * 512:(c - 15) * 512])

            u_sx = [None, None]

            def u1_evac(m0, t0, ps, tgt, th=th, u_sx=u_sx):
                c = m0 // 128
                b = c % 2
                ft = ftmp7 if b == 0 else F2[:, 0:512]
                ftb = ftmp7b if b == 0 else F2[:, 512:1024]
                rsb = rsb0 if th == 0 else rsb1
                E_.wait("vector", "pe", tgt)
                E_.wait("vector", "sx",
                        E_.c["sx"] if u_sx[b] is None else u_sx[b])
                E_.add("vector", lambda e, ps=ps, rsb=rsb, ft=ft: (
                    e.tensor_tensor(out=ft, in0=ps[:], in1=rsb,
                                    op=ALU.mult).then_inc(E_.h["evv"], 1),
                    e.drain(), e.nop().then_inc(E_.h["vx"], 1)))
                E_.inc("evv")
                tv = E_.inc("vx")
                E_.wait("scalar", "vx", tv)
                E_.add("scalar", lambda e, ft=ft, ftb=ftb: (
                    e.activation(out=ftb, in_=ft,
                                 func=AF.Relu).then_inc(E_.h["sx"], 1),
                    e.drain()))
                ts = E_.inc("sx")
                u_sx[b] = ts
                E_.wait("vector", "sx", ts)
                E_.add("vector", lambda e, c=c, ftb=ftb: (
                    e.tensor_tensor(out=ih(c), in0=ftb,
                                    in1=ftb, op=ALU.mult),
                    e.drain(), e.nop().then_inc(E_.h["vx"], 1)))
                E_.inc("vx")
            E_.wait("sync", "pe", E_.c["pe"])
            mm_fm2(E_, wt["u1"], D, I1,
                   lambda kc, t0, t0h=t0h: A3[:, kc * 1024 + t0h: kc * 1024 + t0h + 512],
                   512, u1_evac, [W0, W1], PS["pool"])
            E_.barrier()
            fwht_pe_stage(E_, ih, 32, 512, h128f[:], PS["pool"])
            fwht_chunk_stages(E_, ih, 32, 512, slotf)
            for p_ in range(2):
                t0 = p_ * 256
                quant_block(E_, lambda c, t0=t0: ih(c)[:, t0:t0 + 256],
                            32, 256, 127.0,
                            lambda c, t0=t0: A4[:, c * 512 + t0: c * 512 + t0 + 256],
                            rows, F1, F2, ibuf, identf[:], onesr[:], PS,
                            prescale=float(1.0 / np.sqrt(I1)), clip_hi=False)
                dequant_row(E_, rows["q"], rows["r2"], t0, 256,
                            mrow[0:1, QW_IDX["D1"]:QW_IDX["D1"] + 1], 127.0)
            tb = bcast_row(E_, rows["r2"][0:1, 0:512], 512, PS["b1"], onesr[:])
            E_.wait("scalar", "pe", tb)
            E_.add("scalar", lambda e, th=th: (
                e.copy(out=rsb1 if th else rsb0, in_=PS["b1"]), e.drain(),
                e.nop().then_inc(E_.h["sx"], 1)))
            E_.inc("sx")
            E_.wait("vector", "sx", E_.c["sx"])

            def d1_evac(m0, t0, ps, tgt, th=th):
                c = m0 // 128
                rsb = rsb1 if th else rsb0
                E_.wait("vector", "pe", tgt)
                E_.wait("vector", "sx", E_.c["sx"])
                E_.add("vector", lambda e, ps=ps, rsb=rsb: (
                    e.tensor_tensor(out=ftmp7, in0=ps[:], in1=rsb,
                                    op=ALU.mult).then_inc(E_.h["evv"], 1),
                    e.drain(), e.nop().then_inc(E_.h["vx"], 1)))
                E_.inc("evv")
                tv = E_.inc("vx")
                E_.wait("scalar", "vx", tv)
                E_.wait("scalar", "lg", E_.c["lg"])
                E_.add("scalar", lambda e, c=c: (
                    e.activation(out=stg[:, 0:512], in_=ftmp7,
                                 func=AF.Silu).then_inc(E_.h["sx"], 1),
                    e.drain()))
                ts = E_.inc("sx")
                E_.wait("gpsimd", "sx", ts)
                _dma(E_, "gpsimd", g1_o[:, c * 1024 + th * 512:
                                        c * 1024 + th * 512 + 512],
                     stg[:, 0:512], "lg")
            E_.wait("sync", "pe", E_.c["pe"])
            mm_fm2(E_, wt["d1"], I1, D,
                   lambda kc, t0: A4[:, kc * 512: kc * 512 + 512],
                   512, d1_evac, [W0, W1], PS["pool"], mw=128)
            E_.barrier()
        tlg = E_.c["lg"]
        tlw = E_.c["lw"]
        for g in ENGS:
            E_.wait(g, "lg", tlg)
            E_.wait(g, "lw", tlw)
        E_.barrier()

        block = es.enter_context(nc.Block())
        E_.replay(block)
    return nc



def build_finish():
    nc = bass.Bass("TRN2", target_bir_lowering=False, debug=False)
    dr = lambda n, s, dt, k: nc.dram_tensor(n, s, dt, kind=k).ap()
    summs_d = dr("summs_d", [128, 1024], F32, "ExternalInput")
    sl_d = dr("sl_d", [128, 16 * T], BF16, "ExternalInput")
    P_d = dr("P_d", [128, 16 * T], BF16, "ExternalInput")
    sg_d = dr("sg_d", [128, 16 * T], BF16, "ExternalInput")
    g1_d = dr("g1_d", [128, 16 * T], BF16, "ExternalInput")
    x1_d = dr("x1_d", [128, 16 * T], F32, "ExternalInput")
    wo1_d = dr("wo1_d", [D, D], BF16, "ExternalInput")
    wsums = dr("wsums_d", [8, 16], F32, "ExternalInput")
    nrm2_d = dr("nrm2_d", [128, 32], F32, "ExternalInput")
    identf_d = dr("identf_d", [128, 128], F32, "ExternalInput")
    onesr_d = dr("onesr_d", [1, 128], F32, "ExternalInput")
    out_o = dr("out_o", [128, 16 * T], F32, "ExternalOutput")

    es = ExitStack()
    E_ = E(nc, es)
    with es:
        B1 = E_.sb("B1", [128, 16384], BF16)
        B2 = E_.sb("B2", [128, 16384], BF16)
        B3 = E_.sb("B3", [128, 16384], BF16)
        B4 = E_.sb("B4", [128, 16384], BF16)
        F1 = E_.sb("F1", [128, 4096], F32)
        F2 = E_.sb("F2", [128, 2048], F32)
        W0 = E_.sb("W0", [128, 4096], BF16)
        W1 = E_.sb("W1", [128, 4096], BF16)
        summs = E_.sb("summs", [128, 1024], F32)
        carr = E_.sb("carr", [128, 512], F32)
        mycar = E_.sb("mycar", [128, 128], F32)
        g1c = E_.sb("g1c", [128, 1024], BF16)
        nrm2 = E_.sb("nrm2", [128, 32], F32)
        identf = E_.sb("identf", [128, 128], F32)
        onesr = E_.sb("onesr", [1, 128], F32)
        wsr = E_.sb("wsr", [8, 16], F32)
        mrow = E_.sb("mrow", [1, 16], F32)
        ibuf = E_.sb("ibuf", [128, 256], I32)
        rows = {}
        for k, w in [("m", 256), ("s", 256), ("v", 256), ("q", 256),
                     ("r1", 1024), ("r2", 1024)]:
            rows[k] = E_.sb("row_" + k, [1, w], F32)
        rows["tc"] = E_.sb("row_tc", [128, 1], F32)
        PSf = E_.psum("PSf", [128, 3584], F32)
        PS = {
            "pool": [PSf[:, i * 512:(i + 1) * 512] for i in range(4)],
            "pst": PSf[:, 2048:2176], "psr": PSf[0:1, 2176:2304],
            "b1": PSf[:, 2560:3072], "b2": PSf[:, 3072:3584],
        }
        for s_ in ["lw", "lwa", "lwb", "lg", "lg2", "pe", "vx", "sx", "gx",
                   "evv", "evs", "bar"]:
            E_.sem(s_)
        rsb0, rsb1 = F1[:, 512:1024], F1[:, 1024:1536]
        ftmp = F1[:, 1536:2048]
        ftmp2 = F1[:, 3584:4096]
        x1c = F1[:, 2048:2560]
        outf = [F1[:, 2560:3072], F1[:, 3072:3584]]

        for dst, src in [(summs[:], summs_d), (B1[:], sl_d), (B2[:], P_d),
                         (B3[:], sg_d), (nrm2[:], nrm2_d), (identf[:], identf_d),
                         (onesr[:], onesr_d), (wsr[:], wsums)]:
            _dma(E_, "sync", dst, src)
        tl = E_.c["lw"]
        for g in ENGS:
            E_.wait(g, "lw", tl)
        E_.add("tensor", lambda e: e.transpose(
            out=PSf[0:16, 2048:2056], in_=wsr[:], identity=identf[0:8, 0:8]
        ).then_inc(E_.h["pe"], 1))
        t = E_.inc("pe")
        E_.wait("vector", "pe", t)
        E_.add("vector", lambda e: (e.tensor_reduce(
            out=rows["tc"][0:16, 0:1], in_=PSf[0:16, 2048:2056],
            axis=AX.X, op=ALU.add),
            e.drain(), e.nop().then_inc(E_.h["vx"], 1)))
        t = E_.inc("vx")
        E_.wait("tensor", "vx", t)
        E_.add("tensor", lambda e: e.transpose(
            out=PSf[0:1, 2176:2192], in_=rows["tc"][0:16, 0:1],
            identity=identf[0:16, 0:16]).then_inc(E_.h["pe"], 1))
        t = E_.inc("pe")
        E_.wait("vector", "pe", t)
        sizes = {n: float(QW_SHAPE[n][0] * QW_SHAPE[n][1]) for n in QW}

        def mprep(e):
            for wi, n in enumerate(QW):
                e.tensor_scalar(out=mrow[0:1, wi:wi + 1],
                                in0=PSf[0:1, 2176 + wi:2177 + wi],
                                scalar1=1.0 / sizes[n], scalar2=1e-5,
                                op0=ALU.mult, op1=ALU.max)
            e.drain()
            e.nop().then_inc(E_.h["vx"], 1)
        E_.add("vector", mprep)
        E_.inc("vx")
        E_.barrier()

        def carries(e):
            e.memset(carr[:, 0:16], 0.0)
            e.drain()
            for g in range(31):
                e.tensor_tensor(out=carr[:, (g + 1) * 16:(g + 2) * 16],
                                in0=summs[:, g * 32 + 16:g * 32 + 32],
                                in1=carr[:, g * 16:(g + 1) * 16], op=ALU.mult)
                e.drain()
                e.tensor_tensor(out=carr[:, (g + 1) * 16:(g + 2) * 16],
                                in0=carr[:, (g + 1) * 16:(g + 2) * 16],
                                in1=summs[:, g * 32:g * 32 + 16], op=ALU.add)
                e.drain()
            pid = e.partition_id()
            for j in e.Switch(pid, 8):
                jj = j % 4
                for i_ in range(8):
                    g = 4 * i_ + jj
                    e.tensor_copy(out=mycar[:, i_ * 16:(i_ + 1) * 16],
                                  in_=carr[:, g * 16:(g + 1) * 16])
                e.drain()
            e.drain()
            e.nop().then_inc(E_.h["vx"], 1)
        E_.add("vector", carries)
        tv = E_.inc("vx")
        for c in range(16):
            for i_ in range(8):
                o = c * 1024 + i_ * 128

                def sall(e, o=o, c=c, i_=i_):
                    e.scalar_tensor_tensor(out=B4[:, o:o + 128],
                                           in0=B2[:, o:o + 128],
                                           scalar=mycar[:, i_ * 16 + c:i_ * 16 + c + 1],
                                           in1=B1[:, o:o + 128],
                                           op0=ALU.mult, op1=ALU.add)
                E_.add("vector", sall)
        E_.add("vector", lambda e: (e.drain(), e.nop().then_inc(E_.h["vx"], 1)))
        E_.inc("vx")
        E_.barrier()

        for p_ in range(4):
            t0 = p_ * 256
            s_of = lambda c, t0=t0: B4[:, c * 1024 + t0: c * 1024 + t0 + 256]
            def sq0(e, s_of=s_of):
                e.tensor_tensor(out=F2[:, 0:256], in0=s_of(0), in1=s_of(0),
                                op=ALU.mult)
                e.drain()
            E_.add("vector", sq0)
            for c in range(1, 16):
                def sqc(e, c=c, s_of=s_of):
                    e.tensor_tensor(out=F2[:, 256:512], in0=s_of(c), in1=s_of(c),
                                    op=ALU.mult)
                    e.drain()
                    e.tensor_tensor(out=F2[:, 0:256], in0=F2[:, 0:256],
                                    in1=F2[:, 256:512], op=ALU.add)
                    e.drain()
                E_.add("vector", sqc)
            col_reduce_to_row(E_, F2[:, 0:256], 256, rows["s"], ALU.add,
                              identf[:], rows["tc"], PS["pst"], PS["psr"])

            def rmrow(e):
                e.tensor_scalar(out=rows["v"][0:1, 0:256], in0=rows["s"][0:1, 0:256],
                                scalar1=1.0 / D, scalar2=1e-5, op0=ALU.mult,
                                op1=ALU.add)
                e.drain()
                e.nop().then_inc(E_.h["vx"], 1)
            E_.add("vector", rmrow)
            tv = E_.inc("vx")
            E_.wait("scalar", "vx", tv)
            E_.add("scalar", lambda e: (
                e.activation(out=rows["v"][0:1, 0:256], in_=rows["v"][0:1, 0:256],
                             func=AF.Sqrt),
                e.drain(), e.nop().then_inc(E_.h["sx"], 1)))
            ts = E_.inc("sx")
            E_.wait("vector", "sx", ts)
            E_.add("vector", lambda e: (
                e.reciprocal(out=rows["v"][0:1, 0:256], in_=rows["v"][0:1, 0:256]),
                e.drain(), e.nop().then_inc(E_.h["vx"], 1)))
            tv = E_.inc("vx")
            E_.wait("tensor", "vx", tv)
            tb = bcast_row(E_, rows["v"][0:1, 0:256], 256, PS["b2"][:, 0:256],
                           onesr[:])
            E_.wait("vector", "pe", tb)
            for c in range(16):
                def yc(e, c=c, t0=t0, s_of=s_of):
                    e.tensor_tensor(out=F2[:, 512:768],
                                    in0=s_of(c), in1=PS["b2"][:, 0:256],
                                    op=ALU.mult)
                    e.drain()
                    e.tensor_scalar(out=F2[:, 512:768], in0=F2[:, 512:768],
                                    scalar1=nrm2[:, 16 + c:17 + c], scalar2=None,
                                    op0=ALU.mult)
                    e.drain()
                    e.tensor_tensor(out=B1[:, c * 1024 + t0: c * 1024 + t0 + 256],
                                    in0=F2[:, 512:768],
                                    in1=B3[:, c * 1024 + t0: c * 1024 + t0 + 256],
                                    op=ALU.mult)
                    e.drain()
                E_.add("vector", yc)
            E_.add("vector", lambda e: e.nop().then_inc(E_.h["vx"], 1))
            E_.inc("vx")
        E_.barrier()

        for p_ in range(4):
            t0 = p_ * 256
            y_of = lambda c, t0=t0: B1[:, c * 1024 + t0: c * 1024 + t0 + 256]
            def sq0b(e, y_of=y_of):
                e.tensor_tensor(out=F2[:, 0:256], in0=y_of(0), in1=y_of(0),
                                op=ALU.mult)
                e.drain()
            E_.add("vector", sq0b)
            for c in range(1, 16):
                def sqcb(e, c=c, y_of=y_of):
                    e.tensor_tensor(out=F2[:, 256:512], in0=y_of(c), in1=y_of(c),
                                    op=ALU.mult)
                    e.drain()
                    e.tensor_tensor(out=F2[:, 0:256], in0=F2[:, 0:256],
                                    in1=F2[:, 256:512], op=ALU.add)
                    e.drain()
                E_.add("vector", sqcb)
            col_reduce_to_row(E_, F2[:, 0:256], 256, rows["s"], ALU.add,
                              identf[:], rows["tc"], PS["pst"], PS["psr"])

            def rmrowb(e):
                e.tensor_scalar(out=rows["v"][0:1, 0:256], in0=rows["s"][0:1, 0:256],
                                scalar1=1.0 / D, scalar2=1e-8, op0=ALU.mult,
                                op1=ALU.add)
                e.drain()
                e.nop().then_inc(E_.h["vx"], 1)
            E_.add("vector", rmrowb)
            tv = E_.inc("vx")
            E_.wait("scalar", "vx", tv)
            E_.add("scalar", lambda e: (
                e.activation(out=rows["v"][0:1, 0:256], in_=rows["v"][0:1, 0:256],
                             func=AF.Sqrt),
                e.drain(), e.nop().then_inc(E_.h["sx"], 1)))
            ts = E_.inc("sx")
            E_.wait("vector", "sx", ts)
            E_.add("vector", lambda e: (
                e.reciprocal(out=rows["v"][0:1, 0:256], in_=rows["v"][0:1, 0:256]),
                e.drain(), e.nop().then_inc(E_.h["vx"], 1)))
            tv = E_.inc("vx")
            E_.wait("tensor", "vx", tv)
            tb = bcast_row(E_, rows["v"][0:1, 0:256], 256, PS["b2"][:, 0:256],
                           onesr[:])
            E_.wait("vector", "pe", tb)
            for c in range(16):
                def zc(e, c=c, t0=t0, y_of=y_of):
                    e.tensor_tensor(out=F2[:, 512:768], in0=y_of(c),
                                    in1=PS["b2"][:, 0:256], op=ALU.mult)
                    e.drain()
                    e.tensor_scalar(out=B2[:, c * 1024 + t0: c * 1024 + t0 + 256],
                                    in0=F2[:, 512:768],
                                    scalar1=nrm2[:, c:c + 1], scalar2=None,
                                    op0=ALU.mult)
                    e.drain()
                E_.add("vector", zc)
            E_.add("vector", lambda e: e.nop().then_inc(E_.h["vx"], 1))
            E_.inc("vx")
            quant_block(E_, lambda c, t0=t0: B2[:, c * 1024 + t0: c * 1024 + t0 + 256],
                        16, 256, 127.0,
                        lambda c, t0=t0: B2[:, c * 1024 + t0: c * 1024 + t0 + 256],
                        rows, F1, F2, ibuf, identf[:], onesr[:], PS, clip_hi=False)
            dequant_row(E_, rows["q"], rows["r2"], t0, 256,
                        mrow[0:1, QW_IDX["Wo1"]:QW_IDX["Wo1"] + 1], 127.0)
        for half, dst in [(0, rsb0), (1, rsb1)]:
            tb = bcast_row(E_, rows["r2"][0:1, half * 512:half * 512 + 512],
                           512, PS["b1"], onesr[:])
            E_.wait("scalar", "pe", tb)
            E_.add("scalar", lambda e, d=dst: (
                e.copy(out=d, in_=PS["b1"]), e.drain(),
                e.nop().then_inc(E_.h["sx"], 1)))
            E_.inc("sx")
        E_.wait("vector", "sx", E_.c["sx"])
        E_.barrier()

        x13 = x1_d.rearrange("p (c t) -> p c t", t=T)
        g13 = g1_d.rearrange("p (c t) -> p c t", t=T)

        fin_vx = [None, None]
        fin_lg = [None, None]

        def fin_evac(m0, t0, ps, tgt):
            c = m0 // 128
            b = (c * 2 + t0 // 512) % 2
            rsb = rsb0 if t0 == 0 else rsb1
            buf = outf[b]
            ft = ftmp if b == 0 else ftmp2
            xb = x1c if b == 0 else F2[:, 0:512]
            gb = g1c[:, 0:512] if b == 0 else g1c[:, 512:1024]
            E_.wait("sync", "vx",
                    E_.c["vx"] if fin_vx[b] is None else fin_vx[b])
            tld1 = _dma(E_, "sync", xb, x13[:, c, t0:t0 + 512])
            tld2 = _dma(E_, "sync", gb, g13[:, c, t0:t0 + 512], "gx")
            E_.wait("vector", "pe", tgt)
            E_.wait("vector", "lw", tld1)
            E_.wait("vector", "gx", tld2)
            if fin_lg[b] is not None:
                E_.wait("vector", "lg" if b == 0 else "lg2", fin_lg[b])

            def f(e, c=c, t0=t0, buf=buf, rsb=rsb, ft=ft, xb=xb, gb=gb):
                e.tensor_tensor(out=ft, in0=ps[:], in1=rsb,
                                op=ALU.mult).then_inc(E_.h["evv"], 1)
                e.drain()
                e.tensor_tensor(out=ft, in0=ft, in1=gb, op=ALU.mult)
                e.drain()
                e.tensor_tensor(out=buf, in0=ft, in1=xb, op=ALU.add)
                e.drain()
                e.nop().then_inc(E_.h["vx"], 1)
            E_.add("vector", f)
            E_.inc("evv")
            tv = E_.inc("vx")
            fin_vx[b] = tv
            E_.wait("gpsimd", "vx", tv)
            fin_lg[b] = _dma(E_, "gpsimd",
                             out_o[:, c * 1024 + t0: c * 1024 + t0 + 512],
                             buf, "lg" if b == 0 else "lg2")
        E_.wait("sync", "pe", E_.c["pe"])
        mm_fm2(E_, wo1_d, D, D,
               lambda kc, t0: B2[:, kc * 1024 + t0: kc * 1024 + t0 + 512],
               T, fin_evac, [W0, W1], PS["pool"])
        tlg = E_.c["lg"]
        tlg2 = E_.c["lg2"]
        tgx = E_.c["gx"]
        for g in ENGS:
            E_.wait(g, "lg", tlg)
            E_.wait(g, "lg2", tlg2)
            E_.wait(g, "gx", tgx)
        E_.barrier()
        block = es.enter_context(nc.Block())
        E_.replay(block)
    return nc

def _core_pos(j):
    return np.concatenate([np.arange(128 * (4 * i + j), 128 * (4 * i + j) + 128)
                           for i in range(8)])


def _rope_tables():
    inv = (1.0 / THETA ** (np.arange(0, DH, 2, dtype=np.float32) / DH)).astype(np.float32)
    fr = np.arange(S, dtype=np.float32)[:, None] * inv[None, :]
    cos = np.repeat(np.cos(fr), 2, axis=-1).astype(np.float32)
    sin = np.repeat(np.sin(fr), 2, axis=-1).astype(np.float32)
    sinm = sin.T.copy()
    sinm[0:64] *= -1.0
    return cos.T.copy(), sinm


def _masks_for(j):
    m = np.zeros((8, 128, 512), np.float32)
    for i in range(8):
        tq = 4 * i + j
        mstart = 4 * i + 1
        for kk in range(4):
            k = mstart + kk
            blk = m[i][:, kk * 128:(kk + 1) * 128]
            if k > 31 or k > tq + 1:
                blk[:] = -1e9
            elif k == tq + 1:
                qq = np.arange(128)[:, None]
                kkx = np.arange(128)[None, :]
                blk[:] = np.where(kkx <= qq, 0.0, -1e9)
    return _bf16(m)


def main_in_maps(inputs, qw, wsums_all):
    cos, sin = _rope_tables()
    scale = np.float32(DH ** -0.5)
    identf = np.eye(128, dtype=np.float32)
    identb = _bf16(identf)
    h = np.array([[1.0]], np.float32)
    while h.shape[0] < 128:
        h = np.block([[h, h], [h, -h]])
    h128 = _bf16(h)
    rot = np.zeros((128, 128), np.float32)
    for d in range(64):
        rot[d, d + 64] = 1.0
    for d in range(64, 128):
        rot[d, d - 64] = 1.0
    rotb = _bf16(rot)
    onesr = np.ones((1, 128), np.float32)
    nrms = np.zeros((128, 48), np.float32)
    for widx, key in enumerate(["L1_attn_norm_i", "L1_attn_norm_f", "L1_attn_norm_g"]):
        nrms[:, widx * 16:(widx + 1) * 16] = np.asarray(inputs[key]).reshape(16, 128).T
    wtd = {}
    for n, key in [("q", "Wq"), ("k", "Wk"), ("v", "Wv"), ("o", "Wo"),
                   ("g0", "L0g"), ("u0", "L0u"), ("d0", "L0d"),
                   ("wi", "Wi"), ("wf", "Wf"), ("wg", "Wg"),
                   ("u1", "U1"), ("d1", "D1")]:
        wtd["wt_" + n] = np.ascontiguousarray(np.asarray(qw[key]).T)
    x = np.asarray(inputs["L0_attn_x"])
    maps = []
    for c in range(8):
        b, j = divmod(c, 4)
        pos = _core_pos(j)
        xb = np.ascontiguousarray(x[b].T)
        m = dict(wtd)
        m["x_fm"] = xb
        m["x_own"] = np.ascontiguousarray(xb[:, pos])
        m["wsums_d"] = np.ascontiguousarray(wsums_all)
        m["kcos_d"] = _bf16(cos)
        m["ksin_d"] = _bf16(sin)
        m["qcos_d"] = _bf16(cos[:, pos] * scale)
        m["qsin_d"] = _bf16(sin[:, pos] * scale)
        m["masks_d"] = _masks_for(j)
        m["nrms_d"] = nrms
        m["identf_d"] = identf
        m["identb_d"] = identb
        m["h128_d"] = h128
        m["h128f_d"] = h.astype(np.float32)
        m["rotb_d"] = rotb
        m["onesr_d"] = onesr
        maps.append(m)
    return maps


def finish_in_maps(inputs, qw, wsums_all, res_main):
    identf = np.eye(128, dtype=np.float32)
    onesr = np.ones((1, 128), np.float32)
    nrm2 = np.zeros((128, 32), np.float32)
    nrm2[:, 0:16] = np.asarray(inputs["L1_attn_norm_o"]).reshape(16, 128).T
    nrm2[:, 16:32] = np.asarray(inputs["L1_attn_rms_w"]).reshape(16, 128).T
    wo1 = np.ascontiguousarray(np.asarray(qw["Wo1"]).T)
    maps = []
    for c in range(8):
        b, j = divmod(c, 4)
        summs = np.zeros((128, 1024), np.float32)
        for g in range(32):
            i_, jj = divmod(g, 4)
            src = np.asarray(res_main[b * 4 + jj]["summ_o"])
            summs[:, g * 32:g * 32 + 32] = src[:, i_ * 32:(i_ + 1) * 32]
        m = {
            "summs_d": summs,
            "sl_d": np.asarray(res_main[c]["sl_o"]),
            "P_d": np.asarray(res_main[c]["P_o"]),
            "sg_d": np.asarray(res_main[c]["sg_o"]),
            "g1_d": np.asarray(res_main[c]["g1_o"]),
            "x1_d": np.asarray(res_main[c]["x1_o"]),
            "wo1_d": wo1,
            "wsums_d": np.ascontiguousarray(wsums_all),
            "nrm2_d": nrm2,
            "identf_d": identf,
            "onesr_d": onesr,
        }
        maps.append(m)
    return maps


def kernel(**inputs):
    nc0 = build_stats()
    res0 = run_bass_kernel_spmd(nc0, stats_in_maps(inputs), core_ids=list(range(8)))
    wsums_all = np.stack([np.asarray(res0.results[c]["wsum"])[0] for c in range(8)])

    nc1 = build_quantize()
    res1 = run_bass_kernel_spmd(nc1, quantize_in_maps(inputs, wsums_all),
                                core_ids=list(range(8)))
    qw = {}
    for n in QW + AW:
        qw[n] = np.concatenate([np.asarray(res1.results[c]["q_" + n])
                                for c in range(8)], axis=0)

    nc2 = build_main()
    res2 = run_bass_kernel_spmd(nc2, main_in_maps(inputs, qw, wsums_all),
                                core_ids=list(range(8)))

    nc3 = build_finish()
    res3 = run_bass_kernel_spmd(nc3, finish_in_maps(inputs, qw, wsums_all,
                                                    [r for r in res2.results]),
                                core_ids=list(range(8)))

    out = np.zeros((B, S, D), np.float32)
    for c in range(8):
        b, j = divmod(c, 4)
        pos = _core_pos(j)
        o = np.asarray(res3.results[c]["out_o"])
        o = o.reshape(128, 16, T).transpose(2, 1, 0).reshape(T, D)
        out[b, pos, :] = o
    return out


# version 07
# speedup vs baseline: 1.1496x; 1.1496x over previous
import numpy as np
import ml_dtypes
from contextlib import ExitStack
import concourse.bass as bass
import concourse.mybir as mybir
from concourse.bass_utils import run_bass_kernel_spmd

F32 = mybir.dt.float32
BF16 = mybir.dt.bfloat16
I32 = mybir.dt.int32
AF = mybir.ActivationFunctionType
ALU = mybir.AluOpType
AX = mybir.AxisListType

B, S, D = 2, 4096, 2048
T = 1024
NCH = D // 128
H, KVH, DH = 16, 4, 128
WIN = 128
THETA = 10000.0
I0 = 8192
I1 = 4096

QW = ["L0g", "L0u", "L0d", "Wi", "Wf", "Wg", "Wo1", "U1", "D1"]
QW_SHAPE = {"L0g": (I0, D), "L0u": (I0, D), "L0d": (D, I0),
            "Wi": (D, D), "Wf": (D, D), "Wg": (D, D), "Wo1": (D, D),
            "U1": (I1, D), "D1": (D, I1)}
AW = ["Wq", "Wk", "Wv", "Wo"]
AW_SHAPE = {"Wq": (D, D), "Wk": (512, D), "Wv": (512, D), "Wo": (D, D)}
QW_IDX = {n: i for i, n in enumerate(QW)}
TWO26 = 67108864.0
HALF_MINUS = 0.5 - 2.0 ** -25
FWHT_POOL = False
MAIN_STOP = 99
MAIN_P4SUB = 99


def _stripe(w, c, n=8):
    w = np.asarray(w)
    k = w.shape[0] // n
    return np.ascontiguousarray(w[c * k:(c + 1) * k])


def _bf16(a):
    return np.asarray(a).astype(ml_dtypes.bfloat16)


def build_stats():
    nc = bass.Bass("TRN2", target_bir_lowering=False, debug=False)
    ins = {}
    for n in QW:
        r, c = QW_SHAPE[n]
        ins[n] = nc.dram_tensor("w_" + n, [r // 8, c], F32, kind="ExternalInput").ap()
    ones_in = nc.dram_tensor("ones_col", [128, 1], F32, kind="ExternalInput").ap()
    out = nc.dram_tensor("wsum", [1, 16], F32, kind="ExternalOutput").ap()

    tiles = []
    for n in QW:
        r, c = QW_SHAPE[n]
        for r0 in range(0, r // 8, 128):
            for c0 in range(0, c, 2048):
                tiles.append((n, r0, c0, min(2048, c - c0)))

    es = ExitStack()
    with es:
        buf = [es.enter_context(nc.sbuf_tensor(f"buf{i}", [128, 2048], F32)) for i in range(2)]
        acc = es.enter_context(nc.sbuf_tensor("acc", [128, 16], F32))
        red = es.enter_context(nc.sbuf_tensor("red", [128, 1], F32))
        onesc = es.enter_context(nc.sbuf_tensor("onesc", [128, 1], F32))
        row16 = es.enter_context(nc.sbuf_tensor("row16", [1, 16], F32))
        psum = es.enter_context(nc.psum_tensor("psum", [1, 16], F32))
        ld = es.enter_context(nc.semaphore("ld"))
        lda = es.enter_context(nc.semaphore("lda"))
        ldb = es.enter_context(nc.semaphore("ldb"))
        dv = es.enter_context(nc.semaphore("dv"))
        pe = es.enter_context(nc.semaphore("pe"))
        sv = es.enter_context(nc.semaphore("sv"))
        st = es.enter_context(nc.semaphore("st"))
        block = es.enter_context(nc.Block())

        @block.sync
        def _(sync):
            sync.dma_start(out=onesc[:], in_=ones_in).then_inc(ld, 16)
            for k, (n, r0, c0, cc) in enumerate(tiles):
                if k >= 2:
                    sync.wait_ge(dv, k - 1)
                sync.dma_start(out=buf[k % 2][:, 0:cc],
                               in_=ins[n][r0:r0 + 128, c0:c0 + cc]).then_inc(
                                   lda if k % 2 == 0 else ldb, 16)

        @block.vector
        def _(v):
            v.memset(acc[:], 0.0)
            v.drain()
            v.wait_ge(ld, 16)
            for k, (n, r0, c0, cc) in enumerate(tiles):
                wi = QW.index(n)
                v.wait_ge(lda if k % 2 == 0 else ldb, 16 * (k // 2 + 1))
                v.tensor_reduce(out=red[:], in_=buf[k % 2][:, 0:cc],
                                axis=AX.X, op=ALU.add, apply_absolute_value=True)
                v.drain()
                v.tensor_tensor(out=acc[:, wi:wi + 1], in0=acc[:, wi:wi + 1],
                                in1=red[:], op=ALU.add).then_inc(dv, 1)
                v.drain()

        @block.tensor
        def _(t):
            t.wait_ge(dv, len(tiles))
            t.wait_ge(ld, 16)
            t.matmul(out=psum[:], lhsT=onesc[:], rhs=acc[:], start=True,
                     stop=True).then_inc(pe, 1)

        @block.scalar
        def _(s):
            s.wait_ge(pe, 1)
            s.copy(out=row16[:], in_=psum[:]).then_inc(sv, 1)

        @block.gpsimd
        def _(g):
            g.wait_ge(sv, 1)
            g.dma_start(out=out, in_=row16[:]).then_inc(st, 16)
            g.wait_ge(st, 16)
    return nc


def stats_in_maps(inputs):
    maps = []
    for c in range(8):
        m = {"ones_col": np.ones((128, 1), np.float32)}
        m["w_L0g"] = _stripe(inputs["L0_ffn_w_gate"], c)
        m["w_L0u"] = _stripe(inputs["L0_ffn_w_up"], c)
        m["w_L0d"] = _stripe(inputs["L0_ffn_w_down"], c)
        m["w_Wi"] = _stripe(inputs["L1_attn_Wi"], c)
        m["w_Wf"] = _stripe(inputs["L1_attn_Wf"], c)
        m["w_Wg"] = _stripe(inputs["L1_attn_Wg"], c)
        m["w_Wo1"] = _stripe(inputs["L1_attn_Wo"], c)
        m["w_U1"] = _stripe(inputs["L1_ffn_w_up"], c)
        m["w_D1"] = _stripe(inputs["L1_ffn_w_down"], c)
        maps.append(m)
    return maps


def build_quantize():
    nc = bass.Bass("TRN2", target_bir_lowering=False, debug=False)
    ins, outs = {}, {}
    for n in QW:
        r, c = QW_SHAPE[n]
        ins[n] = nc.dram_tensor("w_" + n, [r // 8, c], F32, kind="ExternalInput").ap()
        outs[n] = nc.dram_tensor("q_" + n, [r // 8, c], BF16, kind="ExternalOutput").ap()
    for n in AW:
        r, c = AW_SHAPE[n]
        ins[n] = nc.dram_tensor("w_" + n, [r // 8, c], F32, kind="ExternalInput").ap()
        outs[n] = nc.dram_tensor("q_" + n, [r // 8, c], BF16, kind="ExternalOutput").ap()
    wsums = nc.dram_tensor("wsums", [8, 16], F32, kind="ExternalInput").ap()
    ones_row = nc.dram_tensor("ones_row", [1, 128], F32, kind="ExternalInput").ap()

    tiles = []
    for n in QW + AW:
        r, c = (QW_SHAPE | AW_SHAPE)[n]
        for r0 in range(0, r // 8, 128):
            rr = min(128, r // 8 - r0)
            for c0 in range(0, c, 2048):
                tiles.append((n, r0, rr, c0, min(2048, c - c0)))

    es = ExitStack()
    with es:
        buf = [es.enter_context(nc.sbuf_tensor(f"buf{i}", [128, 2048], F32)) for i in range(2)]
        qi = es.enter_context(nc.sbuf_tensor("qi", [128, 2048], I32))
        scr0 = es.enter_context(nc.sbuf_tensor("scr0", [128, 2048], F32))
        scr1 = es.enter_context(nc.sbuf_tensor("scr1", [128, 2048], F32))
        qb = [es.enter_context(nc.sbuf_tensor(f"qb{i}", [128, 2048], BF16)) for i in range(2)]
        ws = es.enter_context(nc.sbuf_tensor("ws", [8, 16], F32))
        onesf = es.enter_context(nc.sbuf_tensor("onesf", [1, 128], F32))
        srow = es.enter_context(nc.sbuf_tensor("srow", [1, 16], F32))
        scol = es.enter_context(nc.sbuf_tensor("scol", [128, 16], F32))
        pst = es.enter_context(nc.psum_tensor("pst", [16, 8], F32))
        psc = es.enter_context(nc.psum_tensor("psc", [128, 16], F32))
        idf = es.enter_context(nc.sbuf_tensor("idf", [128, 128], F32))
        ident_in = nc.dram_tensor("ident", [128, 128], F32, kind="ExternalInput").ap()
        ld = es.enter_context(nc.semaphore("ld"))
        lda = es.enter_context(nc.semaphore("lda"))
        ldb = es.enter_context(nc.semaphore("ldb"))
        pe = es.enter_context(nc.semaphore("pe"))
        dv = es.enter_context(nc.semaphore("dv"))
        dvt = es.enter_context(nc.semaphore("dvt"))
        stq = es.enter_context(nc.semaphore("stq"))
        stqa = es.enter_context(nc.semaphore("stqa"))
        stqb = es.enter_context(nc.semaphore("stqb"))
        block = es.enter_context(nc.Block())

        @block.sync
        def _(sync):
            sync.dma_start(out=ws[:], in_=wsums).then_inc(ld, 16)
            sync.dma_start(out=onesf[:], in_=ones_row).then_inc(ld, 16)
            sync.dma_start(out=idf[:], in_=ident_in).then_inc(ld, 16)
            for k, (n, r0, rr, c0, cc) in enumerate(tiles):
                if k >= 2:
                    sync.wait_ge(dvt, k - 1)
                sync.dma_start(out=buf[k % 2][0:rr, 0:cc],
                               in_=ins[n][r0:r0 + rr, c0:c0 + cc]).then_inc(
                                   lda if k % 2 == 0 else ldb, 16)

        @block.tensor
        def _(t):
            t.wait_ge(ld, 48)
            t.transpose(out=pst[:], in_=ws[:], identity=idf[0:8, 0:8]).then_inc(pe, 1)

        @block.vector
        def _(v):
            v.wait_ge(pe, 1)
            v.tensor_reduce(out=scol[0:16, 0:1], in_=pst[0:16, 0:8],
                            axis=AX.X, op=ALU.add).then_inc(dv, 1)
            v.drain()

        @block.tensor
        def _(t):
            t.wait_ge(dv, 1)
            t.transpose(out=psc[0:1, 0:16], in_=scol[0:16, 0:1],
                        identity=idf[0:16, 0:16]).then_inc(pe, 1)

        @block.vector
        def _(v):
            v.wait_ge(pe, 2)
            sizes = [float(QW_SHAPE[n][0] * QW_SHAPE[n][1]) for n in QW] + [1.0] * 7
            for wi in range(16):
                v.tensor_scalar(out=srow[0:1, wi:wi + 1], in0=psc[0:1, wi:wi + 1],
                                scalar1=1.0 / sizes[wi], scalar2=1e-5,
                                op0=ALU.mult, op1=ALU.max)
            v.drain()
            v.reciprocal(out=srow[:], in_=srow[:])
            v.drain()
            v.nop().then_inc(dv, 1)

        @block.tensor
        def _(t):
            t.wait_ge(dv, 2)
            t.matmul(out=psc[:], lhsT=onesf[:], rhs=srow[:], start=True,
                     stop=True).then_inc(pe, 1)

        @block.vector
        def _(v):
            v.wait_ge(pe, 3)
            v.tensor_copy(out=scol[:], in_=psc[:])
            v.drain()
            for k, (n, r0, rr, c0, cc) in enumerate(tiles):
                v.wait_ge(lda if k % 2 == 0 else ldb, 16 * (k // 2 + 1))
                if k >= 2:
                    v.wait_ge(stqa if k % 2 == 0 else stqb, 16 * (k // 2))
                src = buf[k % 2]
                if n in QW:
                    wi = QW.index(n)
                    v.tensor_scalar(out=src[0:rr, 0:cc], in0=src[0:rr, 0:cc],
                                    scalar1=scol[:, wi:wi + 1], scalar2=None,
                                    op0=ALU.mult)
                    v.drain()
                    v.tensor_scalar(out=scr0[0:rr, 0:cc], in0=src[0:rr, 0:cc],
                                    scalar1=HALF_MINUS, scalar2=TWO26,
                                    op0=ALU.subtract, op1=ALU.mult)
                    v.tensor_scalar(out=scr1[0:rr, 0:cc], in0=src[0:rr, 0:cc],
                                    scalar1=0.5, scalar2=-TWO26,
                                    op0=ALU.add, op1=ALU.mult)
                    v.drain()
                    v.tensor_scalar(out=scr0[0:rr, 0:cc], in0=scr0[0:rr, 0:cc],
                                    scalar1=0.0, scalar2=1.0,
                                    op0=ALU.max, op1=ALU.min)
                    v.tensor_scalar(out=scr1[0:rr, 0:cc], in0=scr1[0:rr, 0:cc],
                                    scalar1=0.0, scalar2=1.0,
                                    op0=ALU.max, op1=ALU.min)
                    v.drain()
                    v.tensor_tensor(out=qb[k % 2][0:rr, 0:cc],
                                    in0=scr0[0:rr, 0:cc], in1=scr1[0:rr, 0:cc],
                                    op=ALU.subtract).then_inc(dvt, 1)
                    v.drain()
                else:
                    v.tensor_copy(out=qb[k % 2][0:rr, 0:cc],
                                  in_=src[0:rr, 0:cc]).then_inc(dvt, 1)
                    v.drain()

        @block.gpsimd
        def _(g):
            na = nb = 0
            for k, (n, r0, rr, c0, cc) in enumerate(tiles):
                g.wait_ge(dvt, k + 1)
                if k % 2 == 0:
                    na += 16
                    g.dma_start(out=outs[n][r0:r0 + rr, c0:c0 + cc],
                                in_=qb[0][0:rr, 0:cc]).then_inc(stqa, 16)
                else:
                    nb += 16
                    g.dma_start(out=outs[n][r0:r0 + rr, c0:c0 + cc],
                                in_=qb[1][0:rr, 0:cc]).then_inc(stqb, 16)
            g.wait_ge(stqa, na)
            g.wait_ge(stqb, nb)
    return nc


def quantize_in_maps(inputs, wsums_all):
    key = {"L0g": "L0_ffn_w_gate", "L0u": "L0_ffn_w_up", "L0d": "L0_ffn_w_down",
           "Wi": "L1_attn_Wi", "Wf": "L1_attn_Wf", "Wg": "L1_attn_Wg",
           "Wo1": "L1_attn_Wo", "U1": "L1_ffn_w_up", "D1": "L1_ffn_w_down",
           "Wq": "L0_attn_Wq", "Wk": "L0_attn_Wk", "Wv": "L0_attn_Wv",
           "Wo": "L0_attn_Wo"}
    maps = []
    for c in range(8):
        m = {"wsums": np.ascontiguousarray(wsums_all),
             "ones_row": np.ones((1, 128), np.float32),
             "ident": np.eye(128, dtype=np.float32)}
        for n in QW + AW:
            m["w_" + n] = _stripe(inputs[key[n]], c)
        maps.append(m)
    return maps


ENGS = ("sync", "tensor", "vector", "scalar", "gpsimd")


class E:
    def __init__(self, nc, es):
        self.nc = nc
        self.es = es
        self.q = {e: [] for e in ENGS}
        self.c = {}
        self.h = {}
        self.n_bar = 0

    def sem(self, name):
        self.h[name] = self.es.enter_context(self.nc.semaphore(name))
        self.c[name] = 0
        return self.h[name]

    def sb(self, name, shape, dt):
        return self.es.enter_context(self.nc.sbuf_tensor(name, shape, dt))

    def psum(self, name, shape, dt=F32):
        return self.es.enter_context(self.nc.psum_tensor(name, shape, dt))

    def add(self, eng, fn):
        self.q[eng].append(fn)

    def inc(self, sem, by=1):
        self.c[sem] += by
        return self.c[sem]

    def wait(self, eng, sem, val):
        h = self.h[sem]
        self.add(eng, lambda e: e.wait_ge(h, val))

    def wait_ev(self, eng, tgt=None):
        if tgt is None:
            tgt = (self.c["evv"], self.c["evs"])
        self.wait(eng, "evv", tgt[0])
        self.wait(eng, "evs", tgt[1])

    def ev_now(self):
        return (self.c["evv"], self.c["evs"])

    def barrier(self):
        self.n_bar += 1
        tgt = 5 * self.n_bar
        h = self.h["bar"]
        for eng in ENGS:
            def f(e, h=h):
                e.drain()
                e.nop().then_inc(h, 1)
            self.add(eng, f)
        for eng in ENGS:
            self.wait(eng, "bar", tgt)

    def replay(self, block):
        for eng in ENGS:
            dec = getattr(block, eng)
            fns = self.q[eng]

            def body(e, fns=fns):
                for fn in fns:
                    fn(e)
            dec(body)


def _dma(E_, eng, out_ap, in_ap, sem="lw"):
    cur = E_.c[sem]
    h = E_.h[sem]
    tgt = E_.inc(sem, 16)
    E_.add(eng, lambda e: (e.wait_ge(h, cur),
                           e.dma_start(out=out_ap, in_=in_ap).then_inc(h, 16)))
    return tgt


def mm_fm2(E_, wt_dram, Kdim, Mout, rhs_of, n_tok, evac, wbufs, pool,
           mw=256, tw=512, ksplit=1):
    KC = Kdim // 128
    KCS = KC // ksplit
    src3 = wt_dram.rearrange("(kc p) m -> p kc m", p=128)
    pv = E_.h["pe"]
    E_.wait_ev("tensor")
    E_.wait("tensor", "vx", E_.c["vx"])
    E_.wait("tensor", "sx", E_.c["sx"])
    pe_done_at_bank = [None, None, None, None]
    bank_of = {}
    nbk = 0
    slab_done = []
    nslab = (Mout // mw) * ksplit
    for s in range(nslab):
        mi, kv = divmod(s, ksplit)
        wb = wbufs[s % 2]
        wsem = "lwa" if s % 2 == 0 else "lwb"
        if s >= 2:
            E_.wait("sync", "pe", slab_done[s - 2])
        wb3 = wb.rearrange("p (kc m) -> p kc m", m=mw)
        t_l = _dma(E_, "sync", wb3[:, 0:KCS, :],
                   src3[:, kv * KCS:(kv + 1) * KCS, mi * mw:(mi + 1) * mw],
                   wsem)
        E_.wait("tensor", wsem, t_l)
        for mm_i in range(mw // 128):
            for t0 in range(0, n_tok, tw):
                key = (mi * (mw // 128) + mm_i, t0)
                if key not in bank_of:
                    bank_of[key] = nbk % 4
                    nbk += 1
                bk = bank_of[key]
                ps = pool[bk]
                if kv == 0 and pe_done_at_bank[bk] is not None:
                    E_.wait_ev("tensor", pe_done_at_bank[bk])
                for kc in range(KCS):
                    st_ = (kv == 0 and kc == 0)
                    sp_ = (kv == ksplit - 1 and kc == KCS - 1)
                    inc_ = sp_ or kc == KCS - 1
                    lhsT = wb[:, kc * mw + mm_i * 128: kc * mw + mm_i * 128 + 128]
                    rhs = rhs_of(kv * KCS + kc, t0)
                    if inc_:
                        E_.add("tensor", lambda e, o=ps, l=lhsT, r=rhs, a=st_,
                               b=sp_: e.matmul(out=o[:], lhsT=l, rhs=r, start=a,
                                               stop=b).then_inc(pv, 1))
                        E_.inc("pe")
                    else:
                        E_.add("tensor", lambda e, o=ps, l=lhsT, r=rhs, a=st_:
                               e.matmul(out=o[:], lhsT=l, rhs=r, start=a,
                                        stop=False))
                if kv == ksplit - 1:
                    evac(mi * mw + mm_i * 128, t0, ps, E_.c["pe"])
                    pe_done_at_bank[bk] = E_.ev_now()
        slab_done.append(E_.c["pe"])


def col_reduce_to_row(E_, src_sb, W, out_row, op, identf, tmpcol, pst, psr,
                      extra_in=None):
    pv, vv, sv = E_.h["pe"], E_.h["vx"], E_.h["sx"]
    E_.add("vector", lambda e: e.nop().then_inc(vv, 1))
    t0_ = E_.inc("vx")
    E_.wait("tensor", "vx", t0_)
    for b in range(W // 128):
        sl = src_sb[:, b * 128:(b + 1) * 128]
        E_.add("tensor", lambda e, s=sl: e.transpose(out=pst[:], in_=s,
                                                     identity=identf[:]).then_inc(pv, 1))
        t1 = E_.inc("pe")
        E_.wait("vector", "pe", t1)
        E_.add("vector", lambda e: e.tensor_reduce(
            out=tmpcol[:], in_=pst[:], axis=AX.X, op=op).then_inc(vv, 1))
        t2 = E_.inc("vx")
        E_.wait("tensor", "vx", t2)
        E_.add("tensor", lambda e: e.transpose(out=psr[0:1, 0:128], in_=tmpcol[:],
                                               identity=identf[:]).then_inc(pv, 1))
        t3 = E_.inc("pe")
        E_.wait("scalar", "pe", t3)
        E_.add("scalar", lambda e, b=b: e.copy(
            out=out_row[0:1, b * 128:(b + 1) * 128],
            in_=psr[0:1, 0:128]).then_inc(sv, 1))
        t4 = E_.inc("sx")
        E_.wait("tensor", "vx", t2)
        E_.wait("tensor", "sx", t4)
    E_.wait("vector", "sx", E_.c["sx"])

def fold_chain(E_, srcs, out, op, eng="vector"):
    v = E_.h["vx" if eng == "vector" else "gx"]
    E_.add(eng, lambda e, a=srcs[0], b=srcs[1], o=out: (
        e.tensor_tensor(out=o, in0=a, in1=b, op=op), e.drain()))
    for s in srcs[2:]:
        E_.add(eng, lambda e, a=s, o=out: (
            e.tensor_tensor(out=o, in0=o, in1=a, op=op), e.drain()))


def row_recip(E_, row, W):
    E_.add("vector", lambda e: (e.reciprocal(out=row[0:1, 0:W], in_=row[0:1, 0:W]),
                                e.drain()))


def bcast_row(E_, row_ap, W, ps_out, ones_row_f):
    pv = E_.h["pe"]
    E_.wait("tensor", "vx", E_.c["vx"])
    E_.wait("tensor", "sx", E_.c["sx"])
    E_.add("tensor", lambda e: e.matmul(out=ps_out, lhsT=ones_row_f,
                                        rhs=row_ap, start=True,
                                        stop=True).then_inc(pv, 1))
    return E_.inc("pe")


def ln_block(E_, x_of, nch, W, xn_of, ident_f, ones_row_f, F2, rows, PS, eps=1e-5):
    vv, sv = E_.h["vx"], E_.h["sx"]
    D_ = nch * 128
    fold_chain(E_, [x_of(c) for c in range(nch)], F2[:, 0:W], ALU.add)
    def sq0(e):
        e.tensor_tensor(out=F2[:, W:2 * W], in0=x_of(0), in1=x_of(0), op=ALU.mult)
        e.drain()
    E_.add("vector", sq0)
    for c in range(1, nch):
        def sqc(e, c=c):
            e.tensor_tensor(out=F2[:, 2 * W:3 * W], in0=x_of(c), in1=x_of(c),
                            op=ALU.mult)
            e.drain()
            e.tensor_tensor(out=F2[:, W:2 * W], in0=F2[:, W:2 * W],
                            in1=F2[:, 2 * W:3 * W], op=ALU.add)
            e.drain()
        E_.add("vector", sqc)
    col_reduce_to_row(E_, F2[:, 0:W], W, rows["m"], ALU.add, ident_f,
                      rows["tc"], PS["pst"], PS["psr"])
    col_reduce_to_row(E_, F2[:, W:2 * W], W, rows["s"], ALU.add, ident_f,
                      rows["tc"], PS["pst"], PS["psr"])
    def rowmath(e):
        e.tensor_scalar(out=rows["m"][0:1, 0:W], in0=rows["m"][0:1, 0:W],
                        scalar1=1.0 / D_, scalar2=None, op0=ALU.mult)
        e.tensor_scalar(out=rows["s"][0:1, 0:W], in0=rows["s"][0:1, 0:W],
                        scalar1=1.0 / D_, scalar2=None, op0=ALU.mult)
        e.drain()
        e.tensor_tensor(out=rows["v"][0:1, 0:W], in0=rows["m"][0:1, 0:W],
                        in1=rows["m"][0:1, 0:W], op=ALU.mult)
        e.drain()
        e.tensor_tensor(out=rows["v"][0:1, 0:W], in0=rows["s"][0:1, 0:W],
                        in1=rows["v"][0:1, 0:W], op=ALU.subtract)
        e.drain()
        e.tensor_scalar(out=rows["v"][0:1, 0:W], in0=rows["v"][0:1, 0:W],
                        scalar1=float(eps), scalar2=None, op0=ALU.add)
        e.drain()
        e.nop().then_inc(E_.h["vx"], 1)
    E_.add("vector", rowmath)
    t = E_.inc("vx")
    E_.wait("scalar", "vx", t)
    E_.add("scalar", lambda e: (e.activation(out=rows["v"][0:1, 0:W],
                                             in_=rows["v"][0:1, 0:W],
                                             func=AF.Sqrt),
                                e.drain(), e.nop().then_inc(sv, 1)))
    ts = E_.inc("sx")
    E_.wait("vector", "sx", ts)
    def rowfin(e):
        e.reciprocal(out=rows["v"][0:1, 0:W], in_=rows["v"][0:1, 0:W])
        e.drain()
        e.tensor_scalar(out=rows["m"][0:1, 0:W], in0=rows["m"][0:1, 0:W],
                        scalar1=-1.0, scalar2=None, op0=ALU.mult)
        e.drain()
        e.nop().then_inc(E_.h["vx"], 1)
    E_.add("vector", rowfin)
    tv = E_.inc("vx")
    E_.wait("tensor", "vx", tv)
    t2 = bcast_row(E_, rows["m"][0:1, 0:W], W, PS["b1"][:, 0:W], ones_row_f)
    t2 = bcast_row(E_, rows["v"][0:1, 0:W], W, PS["b2"][:, 0:W], ones_row_f)
    E_.wait("vector", "pe", t2)
    for c in range(nch):
        E_.add("vector", lambda e, c=c: (
            e.tensor_tensor(out=F2[:, 2 * W:3 * W], in0=x_of(c),
                            in1=PS["b1"][:, 0:W], op=ALU.add),
            e.drain(),
            e.tensor_tensor(out=xn_of(c), in0=F2[:, 2 * W:3 * W],
                            in1=PS["b2"][:, 0:W], op=ALU.mult),
            e.drain()))
    E_.add("vector", lambda e: e.nop().then_inc(vv, 1))
    return E_.inc("vx")


def quant_block(E_, src_of, nch, W, qb, out_of, rows, F1, F2, I32buf,
                ident_f, ones_row_f, PS, prescale=1.0, clip_hi=False):
    vv = E_.h["vx"]
    fold_chain(E_, [src_of(c) for c in range(nch)], F2[:, 0:W], ALU.max)
    fold_chain(E_, [src_of(c) for c in range(nch)], F2[:, W:2 * W], ALU.min)
    col_reduce_to_row(E_, F2[:, 0:W], W, rows["q"], ALU.max, ident_f,
                      rows["tc"], PS["pst"], PS["psr"])
    col_reduce_to_row(E_, F2[:, W:2 * W], W, rows["v"], ALU.min, ident_f,
                      rows["tc"], PS["pst"], PS["psr"])
    def rowmath(e):
        e.tensor_scalar(out=rows["v"][0:1, 0:W], in0=rows["v"][0:1, 0:W],
                        scalar1=-1.0, scalar2=None, op0=ALU.mult)
        e.drain()
        e.tensor_tensor(out=rows["q"][0:1, 0:W], in0=rows["q"][0:1, 0:W],
                        in1=rows["v"][0:1, 0:W], op=ALU.max)
        e.drain()
        e.tensor_scalar(out=rows["q"][0:1, 0:W], in0=rows["q"][0:1, 0:W],
                        scalar1=prescale, scalar2=1e-5, op0=ALU.mult,
                        op1=ALU.max)
        e.drain()
        e.reciprocal(out=rows["m"][0:1, 0:W], in_=rows["q"][0:1, 0:W])
        e.drain()
        e.tensor_scalar(out=rows["s"][0:1, 0:W], in0=rows["m"][0:1, 0:W],
                        scalar1=qb * prescale, scalar2=None, op0=ALU.mult)
        e.drain()
    E_.add("vector", rowmath)
    tv = E_.inc("vx")
    E_.add("vector", lambda e: (e.drain(), e.nop().then_inc(vv, 1)))
    E_.wait("tensor", "vx", tv)
    tb = bcast_row(E_, rows["s"][0:1, 0:W], W, PS["b1"][:, 0:W], ones_row_f)
    E_.wait("vector", "pe", tb)
    for c in range(nch):
        def f(e, c=c):
            e.tensor_tensor(out=F1[:, 0:W], in0=src_of(c), in1=PS["b1"][:, 0:W],
                            op=ALU.mult)
            e.drain()
            e.tensor_copy(out=I32buf[:, 0:W], in_=F1[:, 0:W])
            e.drain()
            e.tensor_copy(out=F2[:, 0:W], in_=I32buf[:, 0:W])
            e.drain()
            e.tensor_tensor(out=F2[:, W:2 * W], in0=F1[:, 0:W],
                            in1=F2[:, 0:W], op=ALU.subtract)
            e.drain()
            e.tensor_scalar(out=F1[:, 0:W], in0=F2[:, W:2 * W],
                            scalar1=HALF_MINUS, scalar2=TWO26,
                            op0=ALU.subtract, op1=ALU.mult)
            e.tensor_scalar(out=F2[:, W:2 * W], in0=F2[:, W:2 * W],
                            scalar1=0.5, scalar2=-TWO26,
                            op0=ALU.add, op1=ALU.mult)
            e.drain()
            e.tensor_scalar(out=F1[:, 0:W], in0=F1[:, 0:W],
                            scalar1=0.0, scalar2=1.0,
                            op0=ALU.max, op1=ALU.min)
            e.tensor_scalar(out=F2[:, W:2 * W], in0=F2[:, W:2 * W],
                            scalar1=0.0, scalar2=1.0,
                            op0=ALU.max, op1=ALU.min)
            e.drain()
            e.tensor_tensor(out=F2[:, 0:W], in0=F2[:, 0:W],
                            in1=F1[:, 0:W], op=ALU.add)
            e.drain()
            if clip_hi:
                e.tensor_tensor(out=F2[:, 0:W], in0=F2[:, 0:W],
                                in1=F2[:, W:2 * W], op=ALU.subtract)
                e.drain()
                e.tensor_scalar(out=out_of(c), in0=F2[:, 0:W],
                                scalar1=float(qb - 1), scalar2=None, op0=ALU.min)
            else:
                e.tensor_tensor(out=out_of(c), in0=F2[:, 0:W],
                                in1=F2[:, W:2 * W], op=ALU.subtract)
            e.drain()
        E_.add("vector", f)
    tv = E_.inc("vx")
    E_.add("vector", lambda e: (e.drain(), e.nop().then_inc(vv, 1)))
    return tv


def dequant_row(E_, src_row, dst_row, dst_off, W, mrow_ap, qb, extra=1.0):
    vv = E_.h["vx"]

    def f(e):
        e.tensor_scalar(out=dst_row[0:1, dst_off:dst_off + W],
                        in0=src_row[0:1, 0:W],
                        scalar1=mrow_ap, scalar2=extra / qb,
                        op0=ALU.mult, op1=ALU.mult)
        e.drain()
        e.nop().then_inc(vv, 1)
    E_.add("vector", f)
    return E_.inc("vx")


def fwht_pe_stage(E_, ch_of, nch, W, h128, pool):
    pv = E_.h["pe"]
    E_.wait("tensor", "vx", E_.c["vx"])
    E_.wait("tensor", "sx", E_.c["sx"])
    E_.wait_ev("tensor")
    done = [None, None, None, None]
    for c in range(nch):
        bk = c % 4
        ps = pool[bk]
        if done[bk] is not None:
            E_.wait_ev("tensor", done[bk])
        E_.add("tensor", lambda e, ps=ps, c=c: e.matmul(
            out=ps[:, 0:W], lhsT=h128, rhs=ch_of(c), start=True,
            stop=True).then_inc(pv, 1))
        t = E_.inc("pe")
        E_.wait("scalar", "pe", t)
        E_.add("scalar", lambda e, ps=ps, c=c: (
            e.copy(out=ch_of(c), in_=ps[:, 0:W]).then_inc(E_.h["evs"], 1),
            e.drain()))
        E_.inc("evs")
        done[bk] = E_.ev_now()


def fwht_chunk_stages(E_, ch_of, nch, W, slot):
    import math

    Wh = W if not FWHT_POOL else ((W * 5) // 8) & ~31
    ptmp = E_.fwht_ptmp
    E_.wait("vector", "sx", E_.c["sx"])
    E_.wait_ev("vector")
    E_.wait("gpsimd", "sx", E_.c["sx"])
    E_.wait("gpsimd", "vx", E_.c["vx"])
    E_.wait("gpsimd", "lg", E_.c["lg"])
    E_.wait_ev("gpsimd")

    def buf(key):
        kind, idx = key
        return ch_of(idx) if kind == "c" else slot[:, idx * W:(idx + 1) * W]

    def emit_pair(pa, pb, o1, o2):
        def fv(e, pa=pa, pb=pb, o1=o1, o2=o2):
            e.tensor_tensor(out=o1[:, 0:Wh], in0=pa[:, 0:Wh],
                            in1=pb[:, 0:Wh], op=ALU.add)
            e.tensor_tensor(out=o2[:, 0:Wh], in0=pa[:, 0:Wh],
                            in1=pb[:, 0:Wh], op=ALU.subtract)
            e.drain()
        E_.add("vector", fv)

        if Wh == W:
            return

        def fg(e, pa=pa, pb=pb, o1=o1, o2=o2):
            e.tensor_scalar_mul(ptmp[:, 0:W - Wh], pb[:, Wh:W], -1.0)
            e.drain()
            e.tensor_add(out=o1[:, Wh:W], in0=pa[:, Wh:W], in1=pb[:, Wh:W])
            e.tensor_add(out=o2[:, Wh:W], in0=pa[:, Wh:W],
                         in1=ptmp[:, 0:W - Wh])
            e.drain()
        E_.add("gpsimd", fg)

    def emit_move(s, d):
        E_.add("vector", lambda e, s=s, d=d: (
            e.tensor_copy(out=d[:, 0:Wh], in_=s[:, 0:Wh]), e.drain()))
        if Wh == W:
            return
        E_.add("gpsimd", lambda e, s=s, d=d: (
            e.tensor_copy(out=d[:, Wh:W], in_=s[:, Wh:W]), e.drain()))

    loc = {c: ("c", c) for c in range(nch)}
    free = [("s", 0), ("s", 1)]
    for t in range(int(math.log2(nch))):
        step = 1 << t
        for base in range(0, nch, 2 * step):
            for off in range(step):
                a_, b_ = base + off, base + off + step
                d1 = free.pop()
                d2 = free.pop()
                emit_pair(buf(loc[a_]), buf(loc[b_]), buf(d1), buf(d2))
                free.append(loc[a_])
                free.append(loc[b_])
                loc[a_] = d1
                loc[b_] = d2
    occupied = {loc[c]: c for c in range(nch)}
    pending = set(c for c in range(nch) if loc[c] != ("c", c))
    while pending:
        prog = False
        for c in sorted(pending):
            tgt = ("c", c)
            if tgt not in occupied:
                emit_move(buf(loc[c]), buf(tgt))
                del occupied[loc[c]]
                free.append(loc[c])
                if tgt in free:
                    free.remove(tgt)
                loc[c] = tgt
                occupied[tgt] = c
                pending.discard(c)
                prog = True
                break
        if not prog:
            c = sorted(pending)[0]
            sp = free.pop()
            emit_move(buf(loc[c]), buf(sp))
            del occupied[loc[c]]
            free.append(loc[c])
            loc[c] = sp
            occupied[sp] = c
    E_.add("gpsimd", lambda e: (e.drain(), e.nop().then_inc(E_.h["gx"], 1)))
    tg = E_.inc("gx")
    E_.wait("vector", "gx", tg)
    E_.wait("tensor", "gx", tg)
    E_.wait("scalar", "gx", tg)
    E_.add("vector", lambda e: e.nop().then_inc(E_.h["vx"], 1))
    return E_.inc("vx")


def rope_emit(E_, ps, ps2, sb_raw, cos, sin, out, W, pe_tgt, rotb, s1, s2):
    E_.wait("scalar", "pe", pe_tgt)
    E_.wait_ev("scalar")
    E_.add("scalar", lambda e: (e.copy(out=sb_raw, in_=ps), e.drain(),
                                e.nop().then_inc(E_.h["sx"], 1)))
    ts = E_.inc("sx")
    E_.wait("tensor", "sx", ts)
    E_.add("tensor", lambda e: e.matmul(out=ps2, lhsT=rotb, rhs=sb_raw,
                                        start=True, stop=True).then_inc(E_.h["pe"], 1))
    tp = E_.inc("pe")
    E_.wait("vector", "pe", tp)

    def f(e):
        e.tensor_tensor(out=s1, in0=sb_raw, in1=cos, op=ALU.mult)
        e.tensor_tensor(out=s2, in0=ps2, in1=sin, op=ALU.mult)
        e.drain()
        e.tensor_tensor(out=out, in0=s1, in1=s2,
                        op=ALU.add).then_inc(E_.h["evv"], 1)
        e.drain()
    E_.add("vector", f)
    return E_.inc("evv")


def build_main():
    nc = bass.Bass("TRN2", target_bir_lowering=False, debug=False)
    dr = lambda n, s, dt, k: nc.dram_tensor(n, s, dt, kind=k).ap()
    x_fm = dr("x_fm", [D, S], F32, "ExternalInput")
    x_own = dr("x_own", [D, T], F32, "ExternalInput")
    wt = {}
    for n, (r, c) in {"q": (D, D), "k": (D, 512), "v": (D, 512), "o": (D, D),
                      "g0": (D, I0), "u0": (D, I0), "d0": (I0, D),
                      "wi": (D, D), "wf": (D, D), "wg": (D, D),
                      "u1": (D, I1), "d1": (I1, D)}.items():
        wt[n] = dr("wt_" + n, [r, c], BF16, "ExternalInput")
    wsums = dr("wsums_d", [8, 16], F32, "ExternalInput")
    kcos_d = dr("kcos_d", [128, S], BF16, "ExternalInput")
    ksin_d = dr("ksin_d", [128, S], BF16, "ExternalInput")
    qcos_d = dr("qcos_d", [128, T], BF16, "ExternalInput")
    qsin_d = dr("qsin_d", [128, T], BF16, "ExternalInput")
    masks_d = dr("masks_d", [8, 128, 512], BF16, "ExternalInput")
    nrm_d = dr("nrms_d", [128, 48], F32, "ExternalInput")
    identf_d = dr("identf_d", [128, 128], F32, "ExternalInput")
    identb_d = dr("identb_d", [128, 128], BF16, "ExternalInput")
    h128_d = dr("h128_d", [128, 128], BF16, "ExternalInput")
    h128f_d = dr("h128f_d", [128, 128], F32, "ExternalInput")
    rotb_d = dr("rotb_d", [128, 128], BF16, "ExternalInput")
    onesr_d = dr("onesr_d", [1, 128], F32, "ExternalInput")
    x1_o = dr("x1_o", [128, 16 * T], F32, "ExternalOutput")
    sl_o = dr("sl_o", [128, 16 * T], BF16, "ExternalOutput")
    P_o = dr("P_o", [128, 16 * T], BF16, "ExternalOutput")
    sg_o = dr("sg_o", [128, 16 * T], BF16, "ExternalOutput")
    g1_o = dr("g1_o", [128, 16 * T], BF16, "ExternalOutput")
    summ_o = dr("summ_o", [128, 256], F32, "ExternalOutput")
    xn1_d = dr("xn1_d", [128, 16 * T], F32, "Internal")
    x1i_d = dr("x1i_d", [128, 16 * T], F32, "Internal")
    i_d = dr("i_d", [128, 16 * T], BF16, "Internal")

    es = ExitStack()
    E_ = E(nc, es)
    with es:
        A1 = E_.sb("A1", [128, 16384], BF16)
        A2 = E_.sb("A2", [128, 16384], BF16)
        A3 = E_.sb("A3", [128, 16384], BF16)
        A4 = E_.sb("A4", [128, 16384], BF16)
        F1 = E_.sb("F1", [128, 4096], F32)
        F2 = E_.sb("F2", [128, 2048], F32)
        W0 = E_.sb("W0", [128, 4096], BF16)
        W1 = E_.sb("W1", [128, 4096], BF16)
        qcos = E_.sb("qcos", [128, T], BF16)
        qsin = E_.sb("qsin", [128, T], BF16)
        nrm = E_.sb("nrm", [128, 48], F32)
        identf = E_.sb("identf", [128, 128], F32)
        identb = E_.sb("identb", [128, 128], BF16)
        h128 = E_.sb("h128", [128, 128], BF16)
        h128f = E_.sb("h128f", [128, 128], F32)
        rotb = E_.sb("rotb", [128, 128], BF16)
        onesr = E_.sb("onesr", [1, 128], F32)
        krop = E_.sb("krop", [128, 512], BF16)
        wqh = E_.sb("wqh", [128, 2048], BF16)
        qh = E_.sb("qh", [128, T], BF16)
        zz = E_.sb("zz", [128, 128], BF16)
        E_.fwht_ptmp = E_.sb("ptmp", [128, 192], F32)
        sc_extra = E_.sb("sc_extra", [128, 512], F32)
        scm = E_.sb("scm", [128, 512], F32)
        rcpb = E_.sb("rcpb", [128, 128], F32)
        rrow = E_.sb("rrow", [1, 128], F32)
        ibuf = E_.sb("ibuf", [128, 256], I32)
        summ = E_.sb("summ", [128, 256], F32)
        wsr = E_.sb("wsr", [8, 16], F32)
        mrow = E_.sb("mrow", [1, 16], F32)
        stg = E_.sb("stg", [128, 1024], BF16)
        rows = {}
        for k, w in [("m", 256), ("s", 256), ("v", 256), ("q", 256),
                     ("r1", 1024), ("r2", 1024), ("r3", 1024)]:
            rows[k] = E_.sb("row_" + k, [1, w], F32)
        rows["tc"] = E_.sb("row_tc", [128, 1], F32)
        stat = {k: E_.sb("st_" + k, [128, 1], F32)
                for k in ["mx1", "mx2", "neg", "sum", "rcp"]}
        PSf = E_.psum("PSf", [128, 3584], F32)
        PSB = E_.psum("PSBt", [128, 1024], BF16)
        PS = {
            "pool": [PSf[:, i * 512:(i + 1) * 512] for i in range(4)],
            "pst": PSf[:, 2048:2176], "psr": PSf[0:1, 2176:2304],
            "b1": PSf[:, 2560:3072], "b2": PSf[:, 3072:3584],
        }
        for s_ in ["lw", "lwa", "lwb", "lg", "pe", "vx", "sx", "gx", "evv",
                   "evs", "bar"]:
            E_.sem(s_)

        def _stop(n):
            if MAIN_STOP > n:
                return False
            tlg_ = E_.c["lg"]
            tlw_ = E_.c["lw"]
            for g_ in ENGS:
                E_.wait(g_, "lg", tlg_)
                E_.wait(g_, "lw", tlw_)
            E_.barrier()
            blk_ = es.enter_context(nc.Block())
            E_.replay(blk_)
            return True

        psbslot = [PSB[:, 0:128],
                   PSf[:, 0:512].bitcast(BF16)[:, 0:128],
                   PSf[:, 1024:1536].bitcast(BF16)[:, 0:128],
                   PSf[:, 1536:2048].bitcast(BF16)[:, 0:128],
                   PSf[:, 2048:2560].bitcast(BF16)[:, 0:128],
                   PSf[:, 2560:3072].bitcast(BF16)[:, 0:128],
                   PSf[:, 3072:3584].bitcast(BF16)[:, 0:128]]

        kcos = A4[:, 8192:12288]
        ksin = A4[:, 12288:16384]
        wk_sl = A3[:, 4096:12288]
        wv_sl = A4[:, 0:8192]
        mskb = F2[:].bitcast(BF16)
        pr = F1[:].bitcast(BF16)[:, 0:4096]
        pT = F1[:].bitcast(BF16)[:, 4096:8192]

        for dst, src in [(kcos, kcos_d), (ksin, ksin_d), (qcos[:], qcos_d),
                         (qsin[:], qsin_d), (nrm[:], nrm_d), (identf[:], identf_d),
                         (identb[:], identb_d), (h128[:], h128_d),
                         (h128f[:], h128f_d),
                         (rotb[:], rotb_d),
                         (onesr[:], onesr_d), (wsr[:], wsums)]:
            _dma(E_, "sync", dst, src)
        for wname, dstsl in [("k", wk_sl), ("v", wv_sl)]:
            src3 = wt[wname].rearrange("(kc p) m -> p kc m", p=128)
            dst3 = dstsl.rearrange("p (kc m) -> p kc m", m=512)
            _dma(E_, "sync", dst3, src3)
        tl = E_.c["lw"]
        for g in ENGS:
            E_.wait(g, "lw", tl)
        E_.add("vector", lambda e: (e.memset(zz[:], 0.0), e.drain()))
        E_.add("tensor", lambda e: e.transpose(
            out=PSf[0:16, 2048:2056], in_=wsr[:], identity=identf[0:8, 0:8]
        ).then_inc(E_.h["pe"], 1))
        t = E_.inc("pe")
        E_.wait("vector", "pe", t)
        E_.add("vector", lambda e: (e.tensor_reduce(
            out=rows["tc"][0:16, 0:1], in_=PSf[0:16, 2048:2056],
            axis=AX.X, op=ALU.add),
            e.drain(), e.nop().then_inc(E_.h["vx"], 1)))
        t = E_.inc("vx")
        E_.wait("tensor", "vx", t)
        E_.add("tensor", lambda e: e.transpose(
            out=PSf[0:1, 2176:2192], in_=rows["tc"][0:16, 0:1],
            identity=identf[0:16, 0:16]).then_inc(E_.h["pe"], 1))
        t = E_.inc("pe")
        E_.wait("vector", "pe", t)
        sizes = {n: float(QW_SHAPE[n][0] * QW_SHAPE[n][1]) for n in QW}

        def mprep(e):
            for wi, n in enumerate(QW):
                e.tensor_scalar(out=mrow[0:1, wi:wi + 1],
                                in0=PSf[0:1, 2176 + wi:2177 + wi],
                                scalar1=1.0 / sizes[n], scalar2=1e-5,
                                op0=ALU.mult, op1=ALU.max)
            e.drain()
            e.nop().then_inc(E_.h["vx"], 1)
        E_.add("vector", mprep)
        E_.inc("vx")
        E_.barrier()

        if _stop(1):
            return nc
        xsl3 = x_fm.rearrange("(c p) t -> p c t", p=128)
        for s_ in range(16):
            t0 = s_ * 256
            xs3 = F1.rearrange("p (c w) -> p c w", w=256)
            tlw = _dma(E_, "sync", xs3, xsl3[:, :, t0:t0 + 256])
            E_.wait("vector", "lw", tlw)
            E_.wait("scalar", "lw", tlw)
            x_of = lambda c: F1[:, c * 256:(c + 1) * 256]
            xn_of = lambda c: A3[:, c * 256:(c + 1) * 256]
            tln = ln_block(E_, x_of, 16, 256, xn_of, identf[:], onesr[:], F2, rows, PS)
            E_.wait("tensor", "vx", tln)
            for m in range(4):
                ps = PS["pool"][m % 2]
                ps2 = PS["pool"][2 + m % 2]
                if m >= 2:
                    E_.wait_ev("tensor")
                for kc in range(16):
                    lhsT = wk_sl[:, kc * 512 + m * 128: kc * 512 + m * 128 + 128]
                    if kc == 15:
                        E_.add("tensor", lambda e, o=ps, l=lhsT, c=kc: e.matmul(
                            out=o[:, 0:256], lhsT=l,
                            rhs=A3[:, c * 256:(c + 1) * 256],
                            start=(c == 0), stop=True).then_inc(E_.h["pe"], 1))
                    else:
                        E_.add("tensor", lambda e, o=ps, l=lhsT, c=kc: e.matmul(
                            out=o[:, 0:256], lhsT=l,
                            rhs=A3[:, c * 256:(c + 1) * 256],
                            start=(c == 0), stop=False))
                tpe = E_.inc("pe")
                rope_emit(E_, ps[:, 0:256], ps2[:, 0:256], krop[:, 0:256],
                          kcos[:, t0:t0 + 256], ksin[:, t0:t0 + 256],
                          A1[:, m * 4096 + t0: m * 4096 + t0 + 256], 256,
                          tpe, rotb[:], F2[:, 0:256], F2[:, 256:512])
            for tt in range(2):
                ps = PS["pool"][tt]
                E_.wait_ev("tensor")
                for kc in range(16):
                    lhsT = A3[:, kc * 256 + tt * 128: kc * 256 + tt * 128 + 128]
                    rhs = wv_sl[:, kc * 512:(kc + 1) * 512]
                    if kc == 15:
                        E_.add("tensor", lambda e, o=ps, l=lhsT, r=rhs: e.matmul(
                            out=o[:], lhsT=l, rhs=r, start=False,
                            stop=True).then_inc(E_.h["pe"], 1))
                    else:
                        E_.add("tensor", lambda e, o=ps, l=lhsT, r=rhs, c=kc:
                               e.matmul(out=o[:], lhsT=l, rhs=r, start=(c == 0),
                                        stop=False))
                tpe = E_.inc("pe")
                E_.wait("scalar", "pe", tpe)
                gt = (t0 + tt * 128) // 128
                E_.add("scalar", lambda e, o=A2[:, gt * 512:(gt + 1) * 512],
                       i_=ps: (e.copy(out=o, in_=i_[:]), e.drain(),
                               e.nop().then_inc(E_.h["sx"], 1)))
                E_.inc("sx")
            E_.wait_ev("sync")
            E_.wait("sync", "sx", E_.c["sx"])
            E_.wait("sync", "vx", E_.c["vx"])
        E_.barrier()

        if _stop(2):
            return nc
        xo3 = x_own.rearrange("(c p) t -> p c t", p=128)
        for s_ in range(4):
            t0 = s_ * 256
            xs3 = F1.rearrange("p (c w) -> p c w", w=256)
            tlw = _dma(E_, "sync", xs3, xo3[:, :, t0:t0 + 256])
            E_.wait("vector", "lw", tlw)
            E_.wait("scalar", "lw", tlw)
            x_of = lambda c: F1[:, c * 256:(c + 1) * 256]
            xn_of = lambda c, t0=t0: A3[:, c * 1024 + t0: c * 1024 + t0 + 256]
            tln = ln_block(E_, x_of, 16, 256, xn_of, identf[:], onesr[:], F2, rows, PS)
            E_.wait("sync", "vx", tln)
        E_.barrier()

        if _stop(3):
            return nc
        tmsk = _dma(E_, "sync", mskb.rearrange("p (i w) -> p i w", w=512),
                    masks_d.rearrange("i p w -> p i w"))
        E_.wait("vector", "lw", tmsk)
        wq3 = wt["q"].rearrange("(kc p) m -> p kc m", p=128)
        for h in range(16):
            kvh = h // 4
            wqh3 = wqh.rearrange("p (kc m) -> p kc m", m=128)
            E_.wait("sync", "pe", E_.c["pe"])
            tlw = _dma(E_, "sync", wqh3, wq3[:, :, h * 128:(h + 1) * 128])
            E_.wait("tensor", "lw", tlw)
            for th in range(2):
                ps = PS["pool"][th]
                ps2 = PS["pool"][2 + th]
                E_.wait_ev("tensor")
                for kc in range(16):
                    lhsT = wqh[:, kc * 128:(kc + 1) * 128]
                    rhs = A3[:, kc * 1024 + th * 512: kc * 1024 + th * 512 + 512]
                    if kc == 15:
                        E_.add("tensor", lambda e, o=ps, l=lhsT, r=rhs: e.matmul(
                            out=o[:], lhsT=l, rhs=r, start=False,
                            stop=True).then_inc(E_.h["pe"], 1))
                    else:
                        E_.add("tensor", lambda e, o=ps, l=lhsT, r=rhs, c=kc:
                               e.matmul(out=o[:], lhsT=l, rhs=r, start=(c == 0),
                                        stop=False))
                tpe = E_.inc("pe")
                rope_emit(E_, ps, ps2, krop[:, 0:512],
                          qcos[:, th * 512:(th + 1) * 512],
                          qsin[:, th * 512:(th + 1) * 512],
                          qh[:, th * 512:(th + 1) * 512], 512, tpe, rotb[:],
                          sc_extra[:, 0:512], scm[:, 0:512])
            E_.wait_ev("tensor")

            def p4cut(n):
                if MAIN_P4SUB >= n:
                    return False
                E_.wait("tensor", "vx", E_.c["vx"])
                E_.wait("tensor", "sx", E_.c["sx"])
                return True

            for i in range(8):
                if p4cut(2):
                    continue
                NK = min(4 * i + 5, 32)
                pre = max(0, NK - 28)
                mstart = 4 * i + 1
                mend = min(4 * i + 4, 31)
                mw_ = 128 * (mend - mstart + 1)
                qt = qh[:, 128 * i:128 * i + 128]
                if pre:
                    for k in range(pre):
                        if k == pre - 1:
                            E_.add("tensor", lambda e, k=k, qt=qt, kvh=kvh:
                                   e.matmul(out=PSf[:, 128 * k:128 * k + 128],
                                            lhsT=qt,
                                            rhs=A1[:, kvh * 4096 + 128 * k:
                                                   kvh * 4096 + 128 * (k + 1)],
                                            start=True,
                                            stop=True).then_inc(E_.h["pe"], 1))
                        else:
                            E_.add("tensor", lambda e, k=k, qt=qt, kvh=kvh:
                                   e.matmul(out=PSf[:, 128 * k:128 * k + 128],
                                            lhsT=qt,
                                            rhs=A1[:, kvh * 4096 + 128 * k:
                                                   kvh * 4096 + 128 * (k + 1)],
                                            start=True, stop=True))
                    tpe = E_.inc("pe")
                    E_.wait("scalar", "pe", tpe)
                    E_.add("scalar", lambda e, w=128 * pre: (
                        e.copy(out=sc_extra[:, 0:w], in_=PSf[:, 0:w]), e.drain(),
                        e.nop().then_inc(E_.h["sx"], 1)))
                    tsx = E_.inc("sx")
                    E_.wait("tensor", "sx", tsx)
                k = pre
                groups = []
                while k < NK:
                    n = min(4, NK - k)
                    groups.append((k, n))
                    k += n
                for gi, (k, n) in enumerate(groups):
                    last = gi == len(groups) - 1

                    def qk(e, k=k, n=n, qt=qt, kvh=kvh, pre=pre, last=last):
                        mm = e.matmul(
                            out=PSf[:, 128 * (k - pre):128 * (k - pre) + 128 * n],
                            lhsT=qt,
                            rhs=A1[:, kvh * 4096 + 128 * k:
                                   kvh * 4096 + 128 * (k + n)],
                            start=True, stop=True)
                        if last:
                            mm.then_inc(E_.h["pe"], 1)
                    E_.add("tensor", qk)
                tqk = E_.inc("pe")
                if p4cut(3):
                    continue
                E_.wait("vector", "pe", tqk)
                E_.add("vector", lambda e, o=128 * (mstart - pre), w=mw_, i=i: (
                    e.tensor_tensor(out=scm[:, 0:w], in0=PSf[:, o:o + w],
                                    in1=mskb[:, i * 512:i * 512 + w], op=ALU.add),
                    e.drain()))

                def stats(e, w_clean=128 * (mstart - pre), w=mw_, pre=pre):
                    e.tensor_reduce(out=stat["mx1"][:], in_=PSf[:, 0:w_clean],
                                    axis=AX.X, op=ALU.max)
                    e.tensor_reduce(out=stat["mx2"][:], in_=scm[:, 0:w],
                                    axis=AX.X, op=ALU.max)
                    e.drain()
                    e.tensor_tensor(out=stat["mx1"][:], in0=stat["mx1"][:],
                                    in1=stat["mx2"][:], op=ALU.max)
                    e.drain()
                    if pre:
                        e.tensor_reduce(out=stat["mx2"][:],
                                        in_=sc_extra[:, 0:128 * pre],
                                        axis=AX.X, op=ALU.max)
                        e.drain()
                        e.tensor_tensor(out=stat["mx1"][:], in0=stat["mx1"][:],
                                        in1=stat["mx2"][:], op=ALU.max)
                        e.drain()
                    e.tensor_scalar(out=stat["neg"][:], in0=stat["mx1"][:],
                                    scalar1=-1.0, scalar2=None, op0=ALU.mult)
                    e.drain()
                    e.nop().then_inc(E_.h["vx"], 1)
                E_.add("vector", stats)
                tst = E_.inc("vx")
                if p4cut(4):
                    continue
                E_.wait("scalar", "vx", tst)

                def expf(e, pre=pre, mstart=mstart, NK=NK, w=mw_):
                    if pre:
                        e.activation(out=pr[:, 0:128 * pre],
                                     in_=sc_extra[:, 0:128 * pre], func=AF.Exp,
                                     bias=stat["neg"][:, 0:1])
                    e.activation(out=pr[:, 128 * pre:128 * mstart],
                                 in_=PSf[:, 0:128 * (mstart - pre)], func=AF.Exp,
                                 bias=stat["neg"][:, 0:1])
                    e.activation(out=pr[:, 128 * mstart:128 * mstart + w],
                                 in_=scm[:, 0:w], func=AF.Exp,
                                 bias=stat["neg"][:, 0:1])
                    e.drain()
                    e.nop().then_inc(E_.h["sx"], 1)
                E_.add("scalar", expf)
                texp = E_.inc("sx")
                if p4cut(5):
                    continue
                E_.wait("vector", "sx", texp)

                def sumf(e, NK=NK):
                    e.tensor_reduce(out=stat["sum"][:], in_=pr[:, 0:128 * NK],
                                    axis=AX.X, op=ALU.add)
                    e.drain()
                    e.reciprocal(out=stat["rcp"][:], in_=stat["sum"][:])
                    e.drain()
                    e.nop().then_inc(E_.h["vx"], 1)
                E_.add("vector", sumf)
                trcp = E_.inc("vx")
                if p4cut(6):
                    continue
                E_.wait("tensor", "sx", texp)
                slot_t = [None] * 7
                for k in range(NK):
                    sl_ = k % 7
                    sap = psbslot[sl_]
                    if slot_t[sl_] is not None:
                        E_.wait_ev("tensor", slot_t[sl_])
                    E_.add("tensor", lambda e, k=k, sap=sap: e.transpose(
                        out=sap,
                        in_=pr[:, 128 * k:128 * k + 128],
                        identity=identb[:]).then_inc(E_.h["pe"], 1))
                    tp = E_.inc("pe")
                    eng = "scalar" if k % 2 == 0 else "vector"
                    E_.wait(eng, "pe", tp)
                    if eng == "scalar":
                        E_.inc("evs")
                        E_.add("scalar", lambda e, k=k, sap=sap: (
                            e.copy(out=pT[:, 128 * k:128 * k + 128],
                                   in_=sap).then_inc(E_.h["evs"], 1),
                            e.drain()))
                    else:
                        E_.inc("evv")
                        E_.add("vector", lambda e, k=k, sap=sap: (
                            e.tensor_copy(out=pT[:, 128 * k:128 * k + 128],
                                          in_=sap).then_inc(E_.h["evv"], 1),
                            e.drain()))
                    slot_t[sl_] = E_.ev_now()
                E_.wait("tensor", "vx", trcp)
                E_.add("tensor", lambda e: e.transpose(
                    out=PSf[0:1, 640:768], in_=stat["rcp"][:],
                    identity=identf[:]).then_inc(E_.h["pe"], 1))
                t1 = E_.inc("pe")
                E_.wait("scalar", "pe", t1)
                E_.add("scalar", lambda e: (e.copy(out=rrow[:], in_=PSf[0:1, 640:768]),
                                            e.drain(),
                                            e.nop().then_inc(E_.h["sx"], 1)))
                t2 = E_.inc("sx")
                E_.wait("tensor", "sx", t2)
                t3 = bcast_row(E_, rrow[0:1, 0:128], 128, PSf[:, 512:640], onesr[:])
                E_.wait("scalar", "pe", t3)
                E_.add("scalar", lambda e: (e.copy(out=rcpb[:], in_=PSf[:, 512:640]),
                                            e.drain(),
                                            e.nop().then_inc(E_.h["sx"], 1)))
                trcb = E_.inc("sx")
                if p4cut(7):
                    continue
                E_.wait_ev("tensor")
                for k in range(NK):
                    if k == NK - 1:
                        E_.add("tensor", lambda e, k=k, kvh=kvh: e.matmul(
                            out=PSf[:, 0:128],
                            lhsT=A2[:, 512 * k + 128 * kvh:
                                   512 * k + 128 * kvh + 128],
                            rhs=pT[:, 128 * k:128 * k + 128],
                            start=(k == 0),
                            stop=True).then_inc(E_.h["pe"], 1))
                    else:
                        E_.add("tensor", lambda e, k=k, kvh=kvh: e.matmul(
                            out=PSf[:, 0:128],
                            lhsT=A2[:, 512 * k + 128 * kvh:
                                   512 * k + 128 * kvh + 128],
                            rhs=pT[:, 128 * k:128 * k + 128],
                            start=(k == 0), stop=False))
                tav = E_.inc("pe")
                if p4cut(8):
                    continue
                E_.wait("vector", "pe", tav)
                E_.wait("vector", "sx", trcb)
                E_.add("vector", lambda e, h=h, i=i: (
                    e.tensor_tensor(
                        out=A4[:, h * 1024 + 128 * i: h * 1024 + 128 * i + 128],
                        in0=PSf[:, 0:128], in1=rcpb[:], op=ALU.mult),
                    e.drain(), e.nop().then_inc(E_.h["vx"], 1)))
                tfin = E_.inc("vx")
                E_.wait("tensor", "vx", tfin)
                E_.wait("tensor", "sx", E_.c["sx"])
        E_.barrier()

        if _stop(4):
            return nc
        def wo_evac(m0, t0, ps, tgt):
            E_.wait("scalar", "pe", tgt)
            E_.add("scalar", lambda e, m0=m0, t0=t0, ps=ps: (
                e.copy(out=A1[:, (m0 // 128) * 1024 + t0:
                              (m0 // 128) * 1024 + t0 + 512],
                       in_=ps[:]).then_inc(E_.h["evs"], 1), e.drain()))
            E_.inc("evs")
        E_.wait("sync", "pe", E_.c["pe"])
        mm_fm2(E_, wt["o"], D, D,
               lambda kc, t0: A4[:, kc * 1024 + t0: kc * 1024 + t0 + 512],
               T, wo_evac, [W0, W1], PS["pool"])
        E_.barrier()

        if _stop(5):
            return nc
        rsb_g, rsb_u, rsb_d = F1[:, 0:512], F1[:, 512:1024], F1[:, 1024:1536]
        ftmp = F1[:, 1536:2048]
        g0f = F1[:, 2048:2560]
        xof = F1[:, 2560:3072]
        x1f = [F1[:, 3072:3584], F1[:, 3584:4096]]

        for p_ in range(4):
            t0 = p_ * 256
            tq = quant_block(E_, lambda c, t0=t0: A3[:, c * 1024 + t0: c * 1024 + t0 + 256],
                             16, 256, 128.0,
                             lambda c, t0=t0: A3[:, c * 1024 + t0: c * 1024 + t0 + 256],
                             rows, F1, F2, ibuf, identf[:], onesr[:], PS, clip_hi=True)
            dequant_row(E_, rows["q"], rows["r1"], t0, 256,
                        mrow[0:1, QW_IDX["L0g"]:QW_IDX["L0g"] + 1], 128.0)
            dequant_row(E_, rows["q"], rows["r2"], t0, 256,
                        mrow[0:1, QW_IDX["L0u"]:QW_IDX["L0u"] + 1], 128.0)
        E_.barrier()

        ic_ap = lambda c: (A2[:, c * 512:(c + 1) * 512] if c < 32
                           else A4[:, (c - 32) * 512:(c - 31) * 512])
        for sub in range(2):
            st0 = sub * 512
            for rsrc, rdst in [(rows["r1"], rsb_g), (rows["r2"], rsb_u)]:
                tb = bcast_row(E_, rsrc[0:1, st0:st0 + 512], 512, PS["b1"], onesr[:])
                E_.wait("scalar", "pe", tb)
                E_.add("scalar", lambda e, d=rdst: (
                    e.copy(out=d, in_=PS["b1"]), e.drain(),
                    e.nop().then_inc(E_.h["sx"], 1)))
                E_.inc("sx")
            E_.wait("vector", "sx", E_.c["sx"])

            g_sx = [E_.c["sx"], E_.c["sx"]]

            def gate_evac(m0, t0, ps, tgt):
                c = m0 // 128
                b = c % 2
                ft = ftmp if b == 0 else F2[:, 0:512]
                E_.wait("vector", "pe", tgt)
                E_.wait("vector", "sx", g_sx[b])
                E_.add("vector", lambda e, ps=ps, ft=ft: (
                    e.tensor_tensor(out=ft, in0=ps[:], in1=rsb_g,
                                    op=ALU.mult).then_inc(E_.h["evv"], 1),
                    e.drain(), e.nop().then_inc(E_.h["vx"], 1)))
                E_.inc("evv")
                tv = E_.inc("vx")
                E_.wait("scalar", "vx", tv)
                E_.add("scalar", lambda e, c=c, ft=ft: (
                    e.activation(out=ic_ap(c), in_=ft,
                                 func=AF.Silu).then_inc(E_.h["sx"], 1),
                    e.drain()))
                g_sx[b] = E_.inc("sx")
            E_.wait("sync", "pe", E_.c["pe"])
            mm_fm2(E_, wt["g0"], D, I0,
                   lambda kc, t0, st0=st0: A3[:, kc * 1024 + st0: kc * 1024 + st0 + 512],
                   512, gate_evac, [W0, W1], PS["pool"])

            def up_evac(m0, t0, ps, tgt):
                c = m0 // 128
                E_.wait("vector", "pe", tgt)
                E_.wait("vector", "sx", E_.c["sx"])
                def f(e, c=c, ps=ps):
                    e.tensor_tensor(out=ftmp, in0=ps[:], in1=ic_ap(c), op=ALU.mult)
                    e.drain()
                    e.tensor_tensor(out=ic_ap(c), in0=ftmp, in1=rsb_u,
                                    op=ALU.mult).then_inc(E_.h["evv"], 1)
                    e.drain()
                E_.add("vector", f)
                E_.inc("evv")
            E_.wait("sync", "pe", E_.c["pe"])
            mm_fm2(E_, wt["u0"], D, I0,
                   lambda kc, t0, st0=st0: A3[:, kc * 1024 + st0: kc * 1024 + st0 + 512],
                   512, up_evac, [W0, W1], PS["pool"])
            E_.barrier()

            fwht_pe_stage(E_, ic_ap, 64, 512, h128[:], PS["pool"])
            fwht_chunk_stages(E_, ic_ap, 64, 512, stg)
            for half in range(2):
                t0 = half * 256
                tq = quant_block(E_, lambda c, t0=t0: ic_ap(c)[:, t0:t0 + 256],
                                 64, 256, 128.0,
                                 lambda c, t0=t0: ic_ap(c)[:, t0:t0 + 256],
                                 rows, F1, F2, ibuf, identf[:], onesr[:], PS,
                                 prescale=float(1.0 / np.sqrt(I0)), clip_hi=True)
                dequant_row(E_, rows["q"], rows["r1"], t0, 256,
                            mrow[0:1, QW_IDX["L0d"]:QW_IDX["L0d"] + 1], 128.0)
            tb = bcast_row(E_, rows["r1"][0:1, 0:512], 512, PS["b1"], onesr[:])
            E_.wait("scalar", "pe", tb)
            E_.add("scalar", lambda e: (e.copy(out=rsb_d, in_=PS["b1"]), e.drain(),
                                        e.nop().then_inc(E_.h["sx"], 1)))
            E_.inc("sx")
            E_.wait("vector", "sx", E_.c["sx"])

            xo3b = x_own.rearrange("(c p) t -> p c t", p=128)

            def down_evac(m0, t0, ps, tgt, st0=sub * 512, sub=sub):
                c = m0 // 128
                buf = x1f[c % 2]
                E_.wait("sync", "vx", E_.c["vx"])
                tld = _dma(E_, "sync", xof, xo3b[:, c, st0:st0 + 512])
                E_.wait("vector", "pe", tgt)
                E_.wait("vector", "sx", E_.c["sx"])
                E_.wait("vector", "lw", tld)
                E_.wait("vector", "lg", E_.c["lg"])
                E_.add("vector", lambda e, ps=ps: (
                    e.tensor_tensor(out=ftmp, in0=ps[:], in1=rsb_d,
                                    op=ALU.mult).then_inc(E_.h["evv"], 1),
                    e.drain(), e.nop().then_inc(E_.h["vx"], 1)))
                E_.inc("evv")
                tv = E_.inc("vx")
                E_.wait("scalar", "vx", tv)
                E_.add("scalar", lambda e: (
                    e.activation(out=g0f, in_=ftmp, func=AF.Silu), e.drain(),
                    e.nop().then_inc(E_.h["sx"], 1)))
                tsx = E_.inc("sx")
                E_.wait("vector", "sx", tsx)

                def fin(e, c=c, buf=buf, st0=st0):
                    e.tensor_tensor(out=ftmp,
                                    in0=A1[:, c * 1024 + st0: c * 1024 + st0 + 512],
                                    in1=g0f, op=ALU.mult)
                    e.drain()
                    e.tensor_tensor(out=buf, in0=ftmp, in1=xof, op=ALU.add)
                    e.drain()
                    e.tensor_copy(out=A1[:, c * 1024 + st0: c * 1024 + st0 + 512],
                                  in_=buf)
                    e.drain()
                    e.nop().then_inc(E_.h["vx"], 1)
                E_.add("vector", fin)
                tv2 = E_.inc("vx")
                E_.wait("gpsimd", "vx", tv2)
                _dma(E_, "gpsimd",
                     x1_o[:, c * 1024 + st0: c * 1024 + st0 + 512], buf, "lg")
                _dma(E_, "gpsimd",
                     x1i_d[:, c * 1024 + st0: c * 1024 + st0 + 512], buf, "lg")
            E_.wait("sync", "pe", E_.c["pe"])
            mm_fm2(E_, wt["d0"], I0, D,
                   lambda kc, t0: ic_ap(kc)[:, t0:t0 + 512],
                   512, down_evac, [W0, W1], PS["pool"], mw=128, ksplit=2)
            E_.wait("gpsimd", "lg", E_.c["lg"])
            E_.barrier()
        E_.barrier()
        if _stop(6):
            return nc
        A2f = A2[:].bitcast(F32)
        A3f = A3[:].bitcast(F32)
        A4f = A4[:].bitcast(F32)
        xn1f = lambda c: (A2f[:, c * 1024:(c + 1) * 1024] if c < 8
                          else A3f[:, (c - 8) * 1024:(c - 7) * 1024])
        x1i3 = x1i_d.rearrange("p (c t) -> p c t", t=T)
        acc_s = A4f[:, 0:1024]
        acc_q = A4f[:, 1024:2048]
        strm = A4f[:, 2048:3072]
        sqt = A4f[:, 3072:4096]
        E_.add("vector", lambda e: (e.memset(acc_s, 0.0), e.memset(acc_q, 0.0),
                                    e.drain()))
        for c in range(16):
            tld = _dma(E_, "sync", strm, x1i3[:, c, :])
            E_.wait("vector", "lw", tld)

            def accf(e):
                e.tensor_tensor(out=acc_s, in0=acc_s, in1=strm, op=ALU.add)
                e.drain()
                e.tensor_tensor(out=sqt, in0=strm, in1=strm, op=ALU.mult)
                e.drain()
                e.tensor_tensor(out=acc_q, in0=acc_q, in1=sqt, op=ALU.add)
                e.drain()
                e.nop().then_inc(E_.h["vx"], 1)
            E_.add("vector", accf)
            E_.inc("vx")
            E_.wait("sync", "vx", E_.c["vx"])
        for p_ in range(4):
            t0 = p_ * 256
            col_reduce_to_row(E_, acc_s[:, t0:t0 + 256], 256, rows["m"], ALU.add,
                              identf[:], rows["tc"], PS["pst"], PS["psr"])
            col_reduce_to_row(E_, acc_q[:, t0:t0 + 256], 256, rows["s"], ALU.add,
                              identf[:], rows["tc"], PS["pst"], PS["psr"])

            def lnrow(e, t0=t0):
                e.tensor_scalar(out=rows["m"][0:1, 0:256], in0=rows["m"][0:1, 0:256],
                                scalar1=1.0 / D, scalar2=None, op0=ALU.mult)
                e.tensor_scalar(out=rows["s"][0:1, 0:256], in0=rows["s"][0:1, 0:256],
                                scalar1=1.0 / D, scalar2=None, op0=ALU.mult)
                e.drain()
                e.tensor_tensor(out=rows["v"][0:1, 0:256], in0=rows["m"][0:1, 0:256],
                                in1=rows["m"][0:1, 0:256], op=ALU.mult)
                e.drain()
                e.tensor_tensor(out=rows["v"][0:1, 0:256], in0=rows["s"][0:1, 0:256],
                                in1=rows["v"][0:1, 0:256], op=ALU.subtract)
                e.drain()
                e.tensor_scalar(out=rows["v"][0:1, 0:256], in0=rows["v"][0:1, 0:256],
                                scalar1=1e-5, scalar2=None, op0=ALU.add)
                e.drain()
                e.nop().then_inc(E_.h["vx"], 1)
            E_.add("vector", lnrow)
            tv = E_.inc("vx")
            E_.wait("scalar", "vx", tv)
            E_.add("scalar", lambda e: (
                e.activation(out=rows["v"][0:1, 0:256], in_=rows["v"][0:1, 0:256],
                             func=AF.Sqrt),
                e.drain(), e.nop().then_inc(E_.h["sx"], 1)))
            ts = E_.inc("sx")
            E_.wait("vector", "sx", ts)

            def lnfin(e, t0=t0):
                e.reciprocal(out=rows["v"][0:1, 0:256], in_=rows["v"][0:1, 0:256])
                e.drain()
                e.tensor_scalar(out=rows["m"][0:1, 0:256], in0=rows["m"][0:1, 0:256],
                                scalar1=-1.0, scalar2=None, op0=ALU.mult)
                e.drain()
                e.tensor_copy(out=rows["r1"][0:1, t0:t0 + 256],
                              in_=rows["m"][0:1, 0:256])
                e.tensor_copy(out=rows["r2"][0:1, t0:t0 + 256],
                              in_=rows["v"][0:1, 0:256])
                e.drain()
                e.nop().then_inc(E_.h["vx"], 1)
            E_.add("vector", lnfin)
            E_.inc("vx")
        E_.add("vector", lambda e: (e.memset(acc_q, 0.0), e.drain()))
        for half in range(2):
            t0 = half * 512
            tb1 = bcast_row(E_, rows["r1"][0:1, t0:t0 + 512], 512, PS["b1"], onesr[:])
            tb2 = bcast_row(E_, rows["r2"][0:1, t0:t0 + 512], 512, PS["b2"], onesr[:])
            E_.wait("vector", "pe", tb2)
            for c in range(16):
                tld = _dma(E_, "sync", strm[:, 0:512], x1i3[:, c, t0:t0 + 512])
                E_.wait("vector", "lw", tld)

                def apl(e, c=c, t0=t0):
                    e.tensor_tensor(out=sqt[:, 0:512], in0=strm[:, 0:512],
                                    in1=PS["b1"], op=ALU.add)
                    e.drain()
                    e.tensor_tensor(out=xn1f(c)[:, t0:t0 + 512], in0=sqt[:, 0:512],
                                    in1=PS["b2"], op=ALU.mult)
                    e.drain()
                    e.tensor_tensor(out=sqt[:, 0:512],
                                    in0=xn1f(c)[:, t0:t0 + 512],
                                    in1=xn1f(c)[:, t0:t0 + 512], op=ALU.mult)
                    e.drain()
                    e.tensor_tensor(out=acc_q[:, t0:t0 + 512],
                                    in0=acc_q[:, t0:t0 + 512],
                                    in1=sqt[:, 0:512], op=ALU.add)
                    e.drain()
                    e.nop().then_inc(E_.h["vx"], 1)
                E_.add("vector", apl)
                E_.inc("vx")
                E_.wait("sync", "vx", E_.c["vx"])
        for p_ in range(4):
            t0 = p_ * 256
            col_reduce_to_row(E_, acc_q[:, t0:t0 + 256], 256, rows["s"], ALU.add,
                              identf[:], rows["tc"], PS["pst"], PS["psr"])

            def rm2(e, t0=t0):
                e.tensor_scalar(out=rows["v"][0:1, 0:256], in0=rows["s"][0:1, 0:256],
                                scalar1=1.0 / D, scalar2=1e-8, op0=ALU.mult,
                                op1=ALU.add)
                e.drain()
                e.nop().then_inc(E_.h["vx"], 1)
            E_.add("vector", rm2)
            tv = E_.inc("vx")
            E_.wait("scalar", "vx", tv)
            E_.add("scalar", lambda e: (
                e.activation(out=rows["v"][0:1, 0:256], in_=rows["v"][0:1, 0:256],
                             func=AF.Sqrt),
                e.drain(), e.nop().then_inc(E_.h["sx"], 1)))
            ts = E_.inc("sx")
            E_.wait("vector", "sx", ts)

            def rm2b(e, t0=t0):
                e.reciprocal(out=rows["v"][0:1, 0:256], in_=rows["v"][0:1, 0:256])
                e.drain()
                e.tensor_copy(out=rows["r3"][0:1, t0:t0 + 256],
                              in_=rows["v"][0:1, 0:256])
                e.drain()
                e.nop().then_inc(E_.h["vx"], 1)
            E_.add("vector", rm2b)
            E_.inc("vx")
        E_.barrier()

        rsb0, rsb1 = F1[:, 512:1024], F1[:, 1024:1536]
        ftmp7 = F1[:, 1536:2048]
        ftmp7b = F1[:, 2048:2560]
        i_d3 = i_d.rearrange("p (c t) -> p c t", t=T)

        def gate_half(widx, wname, half, evac):
            t0h = half * 512
            for q_ in range(2):
                t0q = t0h + q_ * 256
                tb2 = bcast_row(E_, rows["r3"][0:1, t0q:t0q + 256], 256,
                                PS["b2"][:, 0:256], onesr[:])
                E_.wait("vector", "pe", tb2)
                for c in range(16):
                    def zp(e, c=c, t0q=t0q, widx=widx):
                        e.tensor_tensor(out=ftmp7[:, 0:256],
                                        in0=xn1f(c)[:, t0q:t0q + 256],
                                        in1=PS["b2"][:, 0:256], op=ALU.mult)
                        e.drain()
                        e.tensor_scalar(out=A4f[:, c * 256:(c + 1) * 256],
                                        in0=ftmp7[:, 0:256],
                                        scalar1=nrm[:, 16 * widx + c:
                                                    16 * widx + c + 1],
                                        scalar2=None, op0=ALU.mult)
                        e.drain()
                        e.nop().then_inc(E_.h["vx"], 1)
                    E_.add("vector", zp)
                    E_.inc("vx")
                quant_block(E_, lambda c: A4f[:, c * 256:(c + 1) * 256],
                            16, 256, 127.0,
                            lambda c, q_=q_: A4[:, 8192 + c * 512 + q_ * 256:
                                               8192 + c * 512 + q_ * 256 + 256],
                            rows, F1, F2, ibuf, identf[:], onesr[:], PS,
                            clip_hi=False)
                dequant_row(E_, rows["q"], rows["r2"], q_ * 256, 256,
                            mrow[0:1, QW_IDX[wname]:QW_IDX[wname] + 1], 127.0)
            tb = bcast_row(E_, rows["r2"][0:1, 0:512], 512, PS["b1"], onesr[:])
            E_.wait("scalar", "pe", tb)
            E_.add("scalar", lambda e: (
                e.copy(out=rsb0, in_=PS["b1"]), e.drain(),
                e.nop().then_inc(E_.h["sx"], 1)))
            E_.inc("sx")
            E_.wait("vector", "sx", E_.c["sx"])
            E_.wait("sync", "pe", E_.c["pe"])
            mm_fm2(E_, wt[{"Wi": "wi", "Wf": "wf", "Wg": "wg"}[wname]], D, D,
                   lambda kc, t0: A4[:, 8192 + kc * 512: 8192 + kc * 512 + 512],
                   512, evac, [W0, W1], PS["pool"])

        for half in range(2):
            f_sx = [None, None]

            def f_evac(m0, t0, ps, tgt, half=half, f_sx=f_sx):
                c = m0 // 128
                b = c % 2
                ft = ftmp7 if b == 0 else F2[:, 0:512]
                E_.wait("vector", "pe", tgt)
                E_.wait("vector", "sx",
                        E_.c["sx"] if f_sx[b] is None else f_sx[b])
                E_.add("vector", lambda e, ps=ps, ft=ft: (
                    e.tensor_tensor(out=ft, in0=ps[:], in1=rsb0,
                                    op=ALU.mult).then_inc(E_.h["evv"], 1),
                    e.drain(), e.nop().then_inc(E_.h["vx"], 1)))
                E_.inc("evv")
                tv = E_.inc("vx")
                E_.wait("scalar", "vx", tv)
                E_.add("scalar", lambda e, c=c, half=half, ft=ft: (
                    e.activation(out=A1[:, c * 1024 + half * 512:
                                       c * 1024 + half * 512 + 512],
                                 in_=ft, func=AF.Sigmoid).then_inc(E_.h["sx"], 1),
                    e.drain()))
                f_sx[b] = E_.inc("sx")
            gate_half(1, "Wf", half, f_evac)
        E_.barrier()
        for half in range(2):
            def i_evac(m0, t0, ps, tgt, half=half):
                c = m0 // 128
                E_.wait("vector", "pe", tgt)
                E_.wait("vector", "sx", E_.c["sx"])
                E_.add("vector", lambda e, ps=ps: (
                    e.tensor_tensor(out=ftmp7, in0=ps[:], in1=rsb0,
                                    op=ALU.mult).then_inc(E_.h["evv"], 1),
                    e.drain(), e.nop().then_inc(E_.h["vx"], 1)))
                E_.inc("evv")
                tv = E_.inc("vx")
                E_.wait("scalar", "vx", tv)
                E_.add("scalar", lambda e: (
                    e.activation(out=ftmp7b, in_=ftmp7,
                                 func=AF.Silu).then_inc(E_.h["sx"], 1),
                    e.drain()))
                ts = E_.inc("sx")
                E_.wait("vector", "sx", ts)
                E_.wait("vector", "lg", E_.c["lg"])

                def fin(e, c=c, half=half):
                    e.tensor_scalar(out=ftmp7,
                                    in0=A1[:, c * 1024 + half * 512:
                                           c * 1024 + half * 512 + 512],
                                    scalar1=-1.0, scalar2=1.0, op0=ALU.mult,
                                    op1=ALU.add)
                    e.drain()
                    e.tensor_tensor(out=stg[:, 0:512], in0=ftmp7b, in1=ftmp7,
                                    op=ALU.mult)
                    e.drain()
                    e.nop().then_inc(E_.h["vx"], 1)
                E_.add("vector", fin)
                tv2 = E_.inc("vx")
                E_.wait("gpsimd", "vx", tv2)
                _dma(E_, "gpsimd", i_d3[:, c, half * 512:half * 512 + 512],
                     stg[:, 0:512], "lg")
            gate_half(0, "Wi", half, i_evac)
        E_.barrier()
        for half in range(2):
            g_sx7 = [None, None]
            g_lg7 = [None, None]

            def g_evac(m0, t0, ps, tgt, half=half, g_sx7=g_sx7, g_lg7=g_lg7):
                c = m0 // 128
                b = c % 2
                ft = ftmp7 if b == 0 else F2[:, 0:512]
                sg = stg[:, 0:512] if b == 0 else stg[:, 512:1024]
                E_.wait("vector", "pe", tgt)
                E_.wait("vector", "sx",
                        E_.c["sx"] if g_sx7[b] is None else g_sx7[b])
                E_.add("vector", lambda e, ps=ps, ft=ft: (
                    e.tensor_tensor(out=ft, in0=ps[:], in1=rsb0,
                                    op=ALU.mult).then_inc(E_.h["evv"], 1),
                    e.drain(), e.nop().then_inc(E_.h["vx"], 1)))
                E_.inc("evv")
                tv = E_.inc("vx")
                E_.wait("scalar", "vx", tv)
                if g_lg7[b] is not None:
                    E_.wait("scalar", "lg", g_lg7[b])
                else:
                    E_.wait("scalar", "lg", E_.c["lg"])
                E_.add("scalar", lambda e, ft=ft, sg=sg: (
                    e.activation(out=sg, in_=ft,
                                 func=AF.Silu).then_inc(E_.h["sx"], 1),
                    e.drain()))
                ts = E_.inc("sx")
                g_sx7[b] = ts
                E_.wait("gpsimd", "sx", ts)
                g_lg7[b] = _dma(E_, "gpsimd",
                                sg_o[:, c * 1024 + half * 512:
                                     c * 1024 + half * 512 + 512], sg, "lg")
            gate_half(2, "Wg", half, g_evac)
        E_.barrier()

        _dma(E_, "gpsimd", xn1_d[:, 0:8192], A2f[:, 0:8192], "lg")
        _dma(E_, "gpsimd", xn1_d[:, 8192:16384], A3f[:, 0:8192], "lg")
        tsp = E_.c["lg"]
        for g_ in ENGS:
            E_.wait(g_, "lg", tsp)
        tld = _dma(E_, "sync", A2[:], i_d)
        E_.wait("vector", "lw", tld)

        E_.add("vector", lambda e: (e.memset(zz[:], 0.0), e.drain()))
        for c in range(16):
            for p_ in range(8):
                o = c * 1024 + p_ * 128

                def sc1(e, o=o):
                    e.tensor_tensor_scan(out=A3[:, o:o + 128],
                                         data0=A1[:, o:o + 128],
                                         data1=A2[:, o:o + 128],
                                         initial=0.0, op0=ALU.mult, op1=ALU.add)
                    e.drain()
                    e.tensor_tensor_scan(out=A4[:, o:o + 128],
                                         data0=A1[:, o:o + 128],
                                         data1=zz[:], initial=1.0,
                                         op0=ALU.mult, op1=ALU.add)
                    e.drain()
                E_.add("vector", sc1)
        E_.add("vector", lambda e: e.nop().then_inc(E_.h["vx"], 1))
        tv = E_.inc("vx")
        for p_ in range(8):
            for c in range(16):

                def smc(e, p_=p_, c=c):
                    e.tensor_copy(out=summ[:, p_ * 32 + c:p_ * 32 + c + 1],
                                  in_=A3[:, c * 1024 + p_ * 128 + 127:
                                         c * 1024 + p_ * 128 + 128])
                    e.tensor_copy(out=summ[:, p_ * 32 + 16 + c:p_ * 32 + 17 + c],
                                  in_=A4[:, c * 1024 + p_ * 128 + 127:
                                         c * 1024 + p_ * 128 + 128])
                E_.add("vector", smc)
        E_.add("vector", lambda e: (e.drain(), e.nop().then_inc(E_.h["vx"], 1)))
        tv = E_.inc("vx")
        E_.wait("gpsimd", "vx", tv)
        _dma(E_, "gpsimd", summ_o, summ[:], "lg")
        _dma(E_, "gpsimd", sl_o, A3[:], "lg")
        _dma(E_, "gpsimd", P_o, A4[:], "lg")
        tlg = E_.c["lg"]
        for g in ENGS:
            E_.wait(g, "lg", tlg)
        E_.barrier()

        if _stop(7):
            return nc
        A1f = A1[:].bitcast(F32)
        tl8 = _dma(E_, "sync", A1f[:, 0:8192], xn1_d[:, 0:8192])
        tl8 = _dma(E_, "sync", A2f[:, 0:8192], xn1_d[:, 8192:16384])
        for g in ["vector", "tensor", "scalar"]:
            E_.wait(g, "lw", tl8)
        wsf = lambda c: (A1f[:, c * 1024:(c + 1) * 1024] if c < 8
                         else A2f[:, (c - 8) * 1024:(c - 7) * 1024])
        slotf = F1[:, 2560:3584]
        for half in range(2):
            hof = lambda c, half=half: wsf(c)[:, half * 512:half * 512 + 512]
            fwht_pe_stage(E_, hof, 16, 512, h128f[:], PS["pool"])
            fwht_chunk_stages(E_, hof, 16, 512, slotf)
        for p_ in range(4):
            t0 = p_ * 256
            quant_block(E_, lambda c, t0=t0: wsf(c)[:, t0:t0 + 256],
                        16, 256, 127.0,
                        lambda c, t0=t0: A3[:, c * 1024 + t0: c * 1024 + t0 + 256],
                        rows, F1, F2, ibuf, identf[:], onesr[:], PS,
                        prescale=float(1.0 / np.sqrt(D)), clip_hi=False)
            dequant_row(E_, rows["q"], rows["r2"], t0, 256,
                        mrow[0:1, QW_IDX["U1"]:QW_IDX["U1"] + 1], 127.0)
        for half, dst in [(0, rsb0), (1, rsb1)]:
            tb = bcast_row(E_, rows["r2"][0:1, half * 512:half * 512 + 512],
                           512, PS["b1"], onesr[:])
            E_.wait("scalar", "pe", tb)
            E_.add("scalar", lambda e, d=dst: (
                e.copy(out=d, in_=PS["b1"]), e.drain(),
                e.nop().then_inc(E_.h["sx"], 1)))
            E_.inc("sx")
        E_.wait("vector", "sx", E_.c["sx"])
        E_.barrier()
        i1f = lambda c: (A1f[:, c * 1024:(c + 1) * 1024] if c < 8
                         else A2f[:, (c - 8) * 1024:(c - 7) * 1024] if c < 16
                         else None)
        for th in range(2):
            t0h = th * 512
            ih = lambda c: (A1f[:, c * 512:(c + 1) * 512] if c < 16
                            else A2f[:, (c - 16) * 512:(c - 15) * 512])

            u_sx = [None, None]

            def u1_evac(m0, t0, ps, tgt, th=th, u_sx=u_sx):
                c = m0 // 128
                b = c % 2
                ft = ftmp7 if b == 0 else F2[:, 0:512]
                ftb = ftmp7b if b == 0 else F2[:, 512:1024]
                rsb = rsb0 if th == 0 else rsb1
                E_.wait("vector", "pe", tgt)
                E_.wait("vector", "sx",
                        E_.c["sx"] if u_sx[b] is None else u_sx[b])
                E_.add("vector", lambda e, ps=ps, rsb=rsb, ft=ft: (
                    e.tensor_tensor(out=ft, in0=ps[:], in1=rsb,
                                    op=ALU.mult).then_inc(E_.h["evv"], 1),
                    e.drain(), e.nop().then_inc(E_.h["vx"], 1)))
                E_.inc("evv")
                tv = E_.inc("vx")
                E_.wait("scalar", "vx", tv)
                E_.add("scalar", lambda e, ft=ft, ftb=ftb: (
                    e.activation(out=ftb, in_=ft,
                                 func=AF.Relu).then_inc(E_.h["sx"], 1),
                    e.drain()))
                ts = E_.inc("sx")
                u_sx[b] = ts
                E_.wait("vector", "sx", ts)
                E_.add("vector", lambda e, c=c, ftb=ftb: (
                    e.tensor_tensor(out=ih(c), in0=ftb,
                                    in1=ftb, op=ALU.mult),
                    e.drain(), e.nop().then_inc(E_.h["vx"], 1)))
                E_.inc("vx")
            E_.wait("sync", "pe", E_.c["pe"])
            mm_fm2(E_, wt["u1"], D, I1,
                   lambda kc, t0, t0h=t0h: A3[:, kc * 1024 + t0h: kc * 1024 + t0h + 512],
                   512, u1_evac, [W0, W1], PS["pool"])
            E_.barrier()
            fwht_pe_stage(E_, ih, 32, 512, h128f[:], PS["pool"])
            fwht_chunk_stages(E_, ih, 32, 512, slotf)
            for p_ in range(2):
                t0 = p_ * 256
                quant_block(E_, lambda c, t0=t0: ih(c)[:, t0:t0 + 256],
                            32, 256, 127.0,
                            lambda c, t0=t0: A4[:, c * 512 + t0: c * 512 + t0 + 256],
                            rows, F1, F2, ibuf, identf[:], onesr[:], PS,
                            prescale=float(1.0 / np.sqrt(I1)), clip_hi=False)
                dequant_row(E_, rows["q"], rows["r2"], t0, 256,
                            mrow[0:1, QW_IDX["D1"]:QW_IDX["D1"] + 1], 127.0)
            tb = bcast_row(E_, rows["r2"][0:1, 0:512], 512, PS["b1"], onesr[:])
            E_.wait("scalar", "pe", tb)
            E_.add("scalar", lambda e, th=th: (
                e.copy(out=rsb1 if th else rsb0, in_=PS["b1"]), e.drain(),
                e.nop().then_inc(E_.h["sx"], 1)))
            E_.inc("sx")
            E_.wait("vector", "sx", E_.c["sx"])

            def d1_evac(m0, t0, ps, tgt, th=th):
                c = m0 // 128
                rsb = rsb1 if th else rsb0
                E_.wait("vector", "pe", tgt)
                E_.wait("vector", "sx", E_.c["sx"])
                E_.add("vector", lambda e, ps=ps, rsb=rsb: (
                    e.tensor_tensor(out=ftmp7, in0=ps[:], in1=rsb,
                                    op=ALU.mult).then_inc(E_.h["evv"], 1),
                    e.drain(), e.nop().then_inc(E_.h["vx"], 1)))
                E_.inc("evv")
                tv = E_.inc("vx")
                E_.wait("scalar", "vx", tv)
                E_.wait("scalar", "lg", E_.c["lg"])
                E_.add("scalar", lambda e, c=c: (
                    e.activation(out=stg[:, 0:512], in_=ftmp7,
                                 func=AF.Silu).then_inc(E_.h["sx"], 1),
                    e.drain()))
                ts = E_.inc("sx")
                E_.wait("gpsimd", "sx", ts)
                _dma(E_, "gpsimd", g1_o[:, c * 1024 + th * 512:
                                        c * 1024 + th * 512 + 512],
                     stg[:, 0:512], "lg")
            E_.wait("sync", "pe", E_.c["pe"])
            mm_fm2(E_, wt["d1"], I1, D,
                   lambda kc, t0: A4[:, kc * 512: kc * 512 + 512],
                   512, d1_evac, [W0, W1], PS["pool"], mw=128)
            E_.barrier()
        tlg = E_.c["lg"]
        tlw = E_.c["lw"]
        for g in ENGS:
            E_.wait(g, "lg", tlg)
            E_.wait(g, "lw", tlw)
        E_.barrier()

        block = es.enter_context(nc.Block())
        E_.replay(block)
    return nc



def build_finish():
    nc = bass.Bass("TRN2", target_bir_lowering=False, debug=False)
    dr = lambda n, s, dt, k: nc.dram_tensor(n, s, dt, kind=k).ap()
    summs_d = dr("summs_d", [128, 1024], F32, "ExternalInput")
    sl_d = dr("sl_d", [128, 16 * T], BF16, "ExternalInput")
    P_d = dr("P_d", [128, 16 * T], BF16, "ExternalInput")
    sg_d = dr("sg_d", [128, 16 * T], BF16, "ExternalInput")
    g1_d = dr("g1_d", [128, 16 * T], BF16, "ExternalInput")
    x1_d = dr("x1_d", [128, 16 * T], F32, "ExternalInput")
    wo1_d = dr("wo1_d", [D, D], BF16, "ExternalInput")
    wsums = dr("wsums_d", [8, 16], F32, "ExternalInput")
    nrm2_d = dr("nrm2_d", [128, 32], F32, "ExternalInput")
    identf_d = dr("identf_d", [128, 128], F32, "ExternalInput")
    onesr_d = dr("onesr_d", [1, 128], F32, "ExternalInput")
    out_o = dr("out_o", [128, 16 * T], F32, "ExternalOutput")

    es = ExitStack()
    E_ = E(nc, es)
    with es:
        B1 = E_.sb("B1", [128, 16384], BF16)
        B2 = E_.sb("B2", [128, 16384], BF16)
        B3 = E_.sb("B3", [128, 16384], BF16)
        B4 = E_.sb("B4", [128, 16384], BF16)
        F1 = E_.sb("F1", [128, 4096], F32)
        F2 = E_.sb("F2", [128, 2048], F32)
        W0 = E_.sb("W0", [128, 4096], BF16)
        W1 = E_.sb("W1", [128, 4096], BF16)
        summs = E_.sb("summs", [128, 1024], F32)
        carr = E_.sb("carr", [128, 512], F32)
        mycar = E_.sb("mycar", [128, 128], F32)
        g1c = E_.sb("g1c", [128, 1024], BF16)
        nrm2 = E_.sb("nrm2", [128, 32], F32)
        identf = E_.sb("identf", [128, 128], F32)
        onesr = E_.sb("onesr", [1, 128], F32)
        wsr = E_.sb("wsr", [8, 16], F32)
        mrow = E_.sb("mrow", [1, 16], F32)
        ibuf = E_.sb("ibuf", [128, 256], I32)
        rows = {}
        for k, w in [("m", 256), ("s", 256), ("v", 256), ("q", 256),
                     ("r1", 1024), ("r2", 1024)]:
            rows[k] = E_.sb("row_" + k, [1, w], F32)
        rows["tc"] = E_.sb("row_tc", [128, 1], F32)
        PSf = E_.psum("PSf", [128, 3584], F32)
        PS = {
            "pool": [PSf[:, i * 512:(i + 1) * 512] for i in range(4)],
            "pst": PSf[:, 2048:2176], "psr": PSf[0:1, 2176:2304],
            "b1": PSf[:, 2560:3072], "b2": PSf[:, 3072:3584],
        }
        for s_ in ["lw", "lwa", "lwb", "lg", "lg2", "pe", "vx", "sx", "gx",
                   "evv", "evs", "bar"]:
            E_.sem(s_)
        rsb0, rsb1 = F1[:, 512:1024], F1[:, 1024:1536]
        ftmp = F1[:, 1536:2048]
        ftmp2 = F1[:, 3584:4096]
        x1c = F1[:, 2048:2560]
        outf = [F1[:, 2560:3072], F1[:, 3072:3584]]

        for dst, src in [(summs[:], summs_d), (B1[:], sl_d), (B2[:], P_d),
                         (B3[:], sg_d), (nrm2[:], nrm2_d), (identf[:], identf_d),
                         (onesr[:], onesr_d), (wsr[:], wsums)]:
            _dma(E_, "sync", dst, src)
        tl = E_.c["lw"]
        for g in ENGS:
            E_.wait(g, "lw", tl)
        E_.add("tensor", lambda e: e.transpose(
            out=PSf[0:16, 2048:2056], in_=wsr[:], identity=identf[0:8, 0:8]
        ).then_inc(E_.h["pe"], 1))
        t = E_.inc("pe")
        E_.wait("vector", "pe", t)
        E_.add("vector", lambda e: (e.tensor_reduce(
            out=rows["tc"][0:16, 0:1], in_=PSf[0:16, 2048:2056],
            axis=AX.X, op=ALU.add),
            e.drain(), e.nop().then_inc(E_.h["vx"], 1)))
        t = E_.inc("vx")
        E_.wait("tensor", "vx", t)
        E_.add("tensor", lambda e: e.transpose(
            out=PSf[0:1, 2176:2192], in_=rows["tc"][0:16, 0:1],
            identity=identf[0:16, 0:16]).then_inc(E_.h["pe"], 1))
        t = E_.inc("pe")
        E_.wait("vector", "pe", t)
        sizes = {n: float(QW_SHAPE[n][0] * QW_SHAPE[n][1]) for n in QW}

        def mprep(e):
            for wi, n in enumerate(QW):
                e.tensor_scalar(out=mrow[0:1, wi:wi + 1],
                                in0=PSf[0:1, 2176 + wi:2177 + wi],
                                scalar1=1.0 / sizes[n], scalar2=1e-5,
                                op0=ALU.mult, op1=ALU.max)
            e.drain()
            e.nop().then_inc(E_.h["vx"], 1)
        E_.add("vector", mprep)
        E_.inc("vx")
        E_.barrier()

        def carries(e):
            e.memset(carr[:, 0:16], 0.0)
            e.drain()
            for g in range(31):
                e.tensor_tensor(out=carr[:, (g + 1) * 16:(g + 2) * 16],
                                in0=summs[:, g * 32 + 16:g * 32 + 32],
                                in1=carr[:, g * 16:(g + 1) * 16], op=ALU.mult)
                e.drain()
                e.tensor_tensor(out=carr[:, (g + 1) * 16:(g + 2) * 16],
                                in0=carr[:, (g + 1) * 16:(g + 2) * 16],
                                in1=summs[:, g * 32:g * 32 + 16], op=ALU.add)
                e.drain()
            pid = e.partition_id()
            for j in e.Switch(pid, 8):
                jj = j % 4
                for i_ in range(8):
                    g = 4 * i_ + jj
                    e.tensor_copy(out=mycar[:, i_ * 16:(i_ + 1) * 16],
                                  in_=carr[:, g * 16:(g + 1) * 16])
                e.drain()
            e.drain()
            e.nop().then_inc(E_.h["vx"], 1)
        E_.add("vector", carries)
        tv = E_.inc("vx")
        for c in range(16):
            for i_ in range(8):
                o = c * 1024 + i_ * 128

                def sall(e, o=o, c=c, i_=i_):
                    e.scalar_tensor_tensor(out=B4[:, o:o + 128],
                                           in0=B2[:, o:o + 128],
                                           scalar=mycar[:, i_ * 16 + c:i_ * 16 + c + 1],
                                           in1=B1[:, o:o + 128],
                                           op0=ALU.mult, op1=ALU.add)
                E_.add("vector", sall)
        E_.add("vector", lambda e: (e.drain(), e.nop().then_inc(E_.h["vx"], 1)))
        E_.inc("vx")
        E_.barrier()

        for p_ in range(4):
            t0 = p_ * 256
            s_of = lambda c, t0=t0: B4[:, c * 1024 + t0: c * 1024 + t0 + 256]
            def sq0(e, s_of=s_of):
                e.tensor_tensor(out=F2[:, 0:256], in0=s_of(0), in1=s_of(0),
                                op=ALU.mult)
                e.drain()
            E_.add("vector", sq0)
            for c in range(1, 16):
                def sqc(e, c=c, s_of=s_of):
                    e.tensor_tensor(out=F2[:, 256:512], in0=s_of(c), in1=s_of(c),
                                    op=ALU.mult)
                    e.drain()
                    e.tensor_tensor(out=F2[:, 0:256], in0=F2[:, 0:256],
                                    in1=F2[:, 256:512], op=ALU.add)
                    e.drain()
                E_.add("vector", sqc)
            col_reduce_to_row(E_, F2[:, 0:256], 256, rows["s"], ALU.add,
                              identf[:], rows["tc"], PS["pst"], PS["psr"])

            def rmrow(e):
                e.tensor_scalar(out=rows["v"][0:1, 0:256], in0=rows["s"][0:1, 0:256],
                                scalar1=1.0 / D, scalar2=1e-5, op0=ALU.mult,
                                op1=ALU.add)
                e.drain()
                e.nop().then_inc(E_.h["vx"], 1)
            E_.add("vector", rmrow)
            tv = E_.inc("vx")
            E_.wait("scalar", "vx", tv)
            E_.add("scalar", lambda e: (
                e.activation(out=rows["v"][0:1, 0:256], in_=rows["v"][0:1, 0:256],
                             func=AF.Sqrt),
                e.drain(), e.nop().then_inc(E_.h["sx"], 1)))
            ts = E_.inc("sx")
            E_.wait("vector", "sx", ts)
            E_.add("vector", lambda e: (
                e.reciprocal(out=rows["v"][0:1, 0:256], in_=rows["v"][0:1, 0:256]),
                e.drain(), e.nop().then_inc(E_.h["vx"], 1)))
            tv = E_.inc("vx")
            E_.wait("tensor", "vx", tv)
            tb = bcast_row(E_, rows["v"][0:1, 0:256], 256, PS["b2"][:, 0:256],
                           onesr[:])
            E_.wait("vector", "pe", tb)
            for c in range(16):
                def yc(e, c=c, t0=t0, s_of=s_of):
                    e.tensor_tensor(out=F2[:, 512:768],
                                    in0=s_of(c), in1=PS["b2"][:, 0:256],
                                    op=ALU.mult)
                    e.drain()
                    e.tensor_scalar(out=F2[:, 512:768], in0=F2[:, 512:768],
                                    scalar1=nrm2[:, 16 + c:17 + c], scalar2=None,
                                    op0=ALU.mult)
                    e.drain()
                    e.tensor_tensor(out=B1[:, c * 1024 + t0: c * 1024 + t0 + 256],
                                    in0=F2[:, 512:768],
                                    in1=B3[:, c * 1024 + t0: c * 1024 + t0 + 256],
                                    op=ALU.mult)
                    e.drain()
                E_.add("vector", yc)
            E_.add("vector", lambda e: e.nop().then_inc(E_.h["vx"], 1))
            E_.inc("vx")
        E_.barrier()

        for p_ in range(4):
            t0 = p_ * 256
            y_of = lambda c, t0=t0: B1[:, c * 1024 + t0: c * 1024 + t0 + 256]
            def sq0b(e, y_of=y_of):
                e.tensor_tensor(out=F2[:, 0:256], in0=y_of(0), in1=y_of(0),
                                op=ALU.mult)
                e.drain()
            E_.add("vector", sq0b)
            for c in range(1, 16):
                def sqcb(e, c=c, y_of=y_of):
                    e.tensor_tensor(out=F2[:, 256:512], in0=y_of(c), in1=y_of(c),
                                    op=ALU.mult)
                    e.drain()
                    e.tensor_tensor(out=F2[:, 0:256], in0=F2[:, 0:256],
                                    in1=F2[:, 256:512], op=ALU.add)
                    e.drain()
                E_.add("vector", sqcb)
            col_reduce_to_row(E_, F2[:, 0:256], 256, rows["s"], ALU.add,
                              identf[:], rows["tc"], PS["pst"], PS["psr"])

            def rmrowb(e):
                e.tensor_scalar(out=rows["v"][0:1, 0:256], in0=rows["s"][0:1, 0:256],
                                scalar1=1.0 / D, scalar2=1e-8, op0=ALU.mult,
                                op1=ALU.add)
                e.drain()
                e.nop().then_inc(E_.h["vx"], 1)
            E_.add("vector", rmrowb)
            tv = E_.inc("vx")
            E_.wait("scalar", "vx", tv)
            E_.add("scalar", lambda e: (
                e.activation(out=rows["v"][0:1, 0:256], in_=rows["v"][0:1, 0:256],
                             func=AF.Sqrt),
                e.drain(), e.nop().then_inc(E_.h["sx"], 1)))
            ts = E_.inc("sx")
            E_.wait("vector", "sx", ts)
            E_.add("vector", lambda e: (
                e.reciprocal(out=rows["v"][0:1, 0:256], in_=rows["v"][0:1, 0:256]),
                e.drain(), e.nop().then_inc(E_.h["vx"], 1)))
            tv = E_.inc("vx")
            E_.wait("tensor", "vx", tv)
            tb = bcast_row(E_, rows["v"][0:1, 0:256], 256, PS["b2"][:, 0:256],
                           onesr[:])
            E_.wait("vector", "pe", tb)
            for c in range(16):
                def zc(e, c=c, t0=t0, y_of=y_of):
                    e.tensor_tensor(out=F2[:, 512:768], in0=y_of(c),
                                    in1=PS["b2"][:, 0:256], op=ALU.mult)
                    e.drain()
                    e.tensor_scalar(out=B2[:, c * 1024 + t0: c * 1024 + t0 + 256],
                                    in0=F2[:, 512:768],
                                    scalar1=nrm2[:, c:c + 1], scalar2=None,
                                    op0=ALU.mult)
                    e.drain()
                E_.add("vector", zc)
            E_.add("vector", lambda e: e.nop().then_inc(E_.h["vx"], 1))
            E_.inc("vx")
            quant_block(E_, lambda c, t0=t0: B2[:, c * 1024 + t0: c * 1024 + t0 + 256],
                        16, 256, 127.0,
                        lambda c, t0=t0: B2[:, c * 1024 + t0: c * 1024 + t0 + 256],
                        rows, F1, F2, ibuf, identf[:], onesr[:], PS, clip_hi=False)
            dequant_row(E_, rows["q"], rows["r2"], t0, 256,
                        mrow[0:1, QW_IDX["Wo1"]:QW_IDX["Wo1"] + 1], 127.0)
        for half, dst in [(0, rsb0), (1, rsb1)]:
            tb = bcast_row(E_, rows["r2"][0:1, half * 512:half * 512 + 512],
                           512, PS["b1"], onesr[:])
            E_.wait("scalar", "pe", tb)
            E_.add("scalar", lambda e, d=dst: (
                e.copy(out=d, in_=PS["b1"]), e.drain(),
                e.nop().then_inc(E_.h["sx"], 1)))
            E_.inc("sx")
        E_.wait("vector", "sx", E_.c["sx"])
        E_.barrier()

        x13 = x1_d.rearrange("p (c t) -> p c t", t=T)
        g13 = g1_d.rearrange("p (c t) -> p c t", t=T)

        fin_vx = [None, None]
        fin_lg = [None, None]

        def fin_evac(m0, t0, ps, tgt):
            c = m0 // 128
            b = (c * 2 + t0 // 512) % 2
            rsb = rsb0 if t0 == 0 else rsb1
            buf = outf[b]
            ft = ftmp if b == 0 else ftmp2
            xb = x1c if b == 0 else F2[:, 0:512]
            gb = g1c[:, 0:512] if b == 0 else g1c[:, 512:1024]
            E_.wait("sync", "vx",
                    E_.c["vx"] if fin_vx[b] is None else fin_vx[b])
            tld1 = _dma(E_, "sync", xb, x13[:, c, t0:t0 + 512])
            tld2 = _dma(E_, "sync", gb, g13[:, c, t0:t0 + 512], "gx")
            E_.wait("vector", "pe", tgt)
            E_.wait("vector", "lw", tld1)
            E_.wait("vector", "gx", tld2)
            if fin_lg[b] is not None:
                E_.wait("vector", "lg" if b == 0 else "lg2", fin_lg[b])

            def f(e, c=c, t0=t0, buf=buf, rsb=rsb, ft=ft, xb=xb, gb=gb):
                e.tensor_tensor(out=ft, in0=ps[:], in1=rsb,
                                op=ALU.mult).then_inc(E_.h["evv"], 1)
                e.drain()
                e.tensor_tensor(out=ft, in0=ft, in1=gb, op=ALU.mult)
                e.drain()
                e.tensor_tensor(out=buf, in0=ft, in1=xb, op=ALU.add)
                e.drain()
                e.nop().then_inc(E_.h["vx"], 1)
            E_.add("vector", f)
            E_.inc("evv")
            tv = E_.inc("vx")
            fin_vx[b] = tv
            E_.wait("gpsimd", "vx", tv)
            fin_lg[b] = _dma(E_, "gpsimd",
                             out_o[:, c * 1024 + t0: c * 1024 + t0 + 512],
                             buf, "lg" if b == 0 else "lg2")
        E_.wait("sync", "pe", E_.c["pe"])
        mm_fm2(E_, wo1_d, D, D,
               lambda kc, t0: B2[:, kc * 1024 + t0: kc * 1024 + t0 + 512],
               T, fin_evac, [W0, W1], PS["pool"])
        tlg = E_.c["lg"]
        tlg2 = E_.c["lg2"]
        tgx = E_.c["gx"]
        for g in ENGS:
            E_.wait(g, "lg", tlg)
            E_.wait(g, "lg2", tlg2)
            E_.wait(g, "gx", tgx)
        E_.barrier()
        block = es.enter_context(nc.Block())
        E_.replay(block)
    return nc

def _core_pos(j):
    return np.concatenate([np.arange(128 * (4 * i + j), 128 * (4 * i + j) + 128)
                           for i in range(8)])


def _rope_tables():
    inv = (1.0 / THETA ** (np.arange(0, DH, 2, dtype=np.float32) / DH)).astype(np.float32)
    fr = np.arange(S, dtype=np.float32)[:, None] * inv[None, :]
    cos = np.repeat(np.cos(fr), 2, axis=-1).astype(np.float32)
    sin = np.repeat(np.sin(fr), 2, axis=-1).astype(np.float32)
    sinm = sin.T.copy()
    sinm[0:64] *= -1.0
    return cos.T.copy(), sinm


def _masks_for(j):
    m = np.zeros((8, 128, 512), np.float32)
    for i in range(8):
        tq = 4 * i + j
        mstart = 4 * i + 1
        for kk in range(4):
            k = mstart + kk
            blk = m[i][:, kk * 128:(kk + 1) * 128]
            if k > 31 or k > tq + 1:
                blk[:] = -1e9
            elif k == tq + 1:
                qq = np.arange(128)[:, None]
                kkx = np.arange(128)[None, :]
                blk[:] = np.where(kkx <= qq, 0.0, -1e9)
    return _bf16(m)


def main_in_maps(inputs, qw, wsums_all):
    cos, sin = _rope_tables()
    scale = np.float32(DH ** -0.5)
    identf = np.eye(128, dtype=np.float32)
    identb = _bf16(identf)
    h = np.array([[1.0]], np.float32)
    while h.shape[0] < 128:
        h = np.block([[h, h], [h, -h]])
    h128 = _bf16(h)
    rot = np.zeros((128, 128), np.float32)
    for d in range(64):
        rot[d, d + 64] = 1.0
    for d in range(64, 128):
        rot[d, d - 64] = 1.0
    rotb = _bf16(rot)
    onesr = np.ones((1, 128), np.float32)
    nrms = np.zeros((128, 48), np.float32)
    for widx, key in enumerate(["L1_attn_norm_i", "L1_attn_norm_f", "L1_attn_norm_g"]):
        nrms[:, widx * 16:(widx + 1) * 16] = np.asarray(inputs[key]).reshape(16, 128).T
    wtd = {}
    for n, key in [("q", "Wq"), ("k", "Wk"), ("v", "Wv"), ("o", "Wo"),
                   ("g0", "L0g"), ("u0", "L0u"), ("d0", "L0d"),
                   ("wi", "Wi"), ("wf", "Wf"), ("wg", "Wg"),
                   ("u1", "U1"), ("d1", "D1")]:
        wtd["wt_" + n] = np.ascontiguousarray(np.asarray(qw[key]).T)
    x = np.asarray(inputs["L0_attn_x"])
    maps = []
    for c in range(8):
        b, j = divmod(c, 4)
        pos = _core_pos(j)
        xb = np.ascontiguousarray(x[b].T)
        m = dict(wtd)
        m["x_fm"] = xb
        m["x_own"] = np.ascontiguousarray(xb[:, pos])
        m["wsums_d"] = np.ascontiguousarray(wsums_all)
        m["kcos_d"] = _bf16(cos)
        m["ksin_d"] = _bf16(sin)
        m["qcos_d"] = _bf16(cos[:, pos] * scale)
        m["qsin_d"] = _bf16(sin[:, pos] * scale)
        m["masks_d"] = _masks_for(j)
        m["nrms_d"] = nrms
        m["identf_d"] = identf
        m["identb_d"] = identb
        m["h128_d"] = h128
        m["h128f_d"] = h.astype(np.float32)
        m["rotb_d"] = rotb
        m["onesr_d"] = onesr
        maps.append(m)
    return maps


def finish_in_maps(inputs, qw, wsums_all, res_main):
    identf = np.eye(128, dtype=np.float32)
    onesr = np.ones((1, 128), np.float32)
    nrm2 = np.zeros((128, 32), np.float32)
    nrm2[:, 0:16] = np.asarray(inputs["L1_attn_norm_o"]).reshape(16, 128).T
    nrm2[:, 16:32] = np.asarray(inputs["L1_attn_rms_w"]).reshape(16, 128).T
    wo1 = np.ascontiguousarray(np.asarray(qw["Wo1"]).T)
    maps = []
    for c in range(8):
        b, j = divmod(c, 4)
        summs = np.zeros((128, 1024), np.float32)
        for g in range(32):
            i_, jj = divmod(g, 4)
            src = np.asarray(res_main[b * 4 + jj]["summ_o"])
            summs[:, g * 32:g * 32 + 32] = src[:, i_ * 32:(i_ + 1) * 32]
        m = {
            "summs_d": summs,
            "sl_d": np.asarray(res_main[c]["sl_o"]),
            "P_d": np.asarray(res_main[c]["P_o"]),
            "sg_d": np.asarray(res_main[c]["sg_o"]),
            "g1_d": np.asarray(res_main[c]["g1_o"]),
            "x1_d": np.asarray(res_main[c]["x1_o"]),
            "wo1_d": wo1,
            "wsums_d": np.ascontiguousarray(wsums_all),
            "nrm2_d": nrm2,
            "identf_d": identf,
            "onesr_d": onesr,
        }
        maps.append(m)
    return maps


def kernel(**inputs):
    nc0 = build_stats()
    res0 = run_bass_kernel_spmd(nc0, stats_in_maps(inputs), core_ids=list(range(8)))
    wsums_all = np.stack([np.asarray(res0.results[c]["wsum"])[0] for c in range(8)])

    nc1 = build_quantize()
    res1 = run_bass_kernel_spmd(nc1, quantize_in_maps(inputs, wsums_all),
                                core_ids=list(range(8)))
    qw = {}
    for n in QW + AW:
        qw[n] = np.concatenate([np.asarray(res1.results[c]["q_" + n])
                                for c in range(8)], axis=0)

    nc2 = build_main()
    res2 = run_bass_kernel_spmd(nc2, main_in_maps(inputs, qw, wsums_all),
                                core_ids=list(range(8)))

    nc3 = build_finish()
    res3 = run_bass_kernel_spmd(nc3, finish_in_maps(inputs, qw, wsums_all,
                                                    [r for r in res2.results]),
                                core_ids=list(range(8)))

    out = np.zeros((B, S, D), np.float32)
    for c in range(8):
        b, j = divmod(c, 4)
        pos = _core_pos(j)
        o = np.asarray(res3.results[c]["out_o"])
        o = o.reshape(128, 16, T).transpose(2, 1, 0).reshape(T, D)
        out[b, pos, :] = o
    return out


# version 08
# speedup vs baseline: 1.1506x; 1.0009x over previous
import numpy as np
import ml_dtypes
from contextlib import ExitStack
import concourse.bass as bass
import concourse.mybir as mybir
from concourse.bass_utils import run_bass_kernel_spmd

F32 = mybir.dt.float32
BF16 = mybir.dt.bfloat16
I32 = mybir.dt.int32
AF = mybir.ActivationFunctionType
ALU = mybir.AluOpType
AX = mybir.AxisListType

B, S, D = 2, 4096, 2048
T = 1024
NCH = D // 128
H, KVH, DH = 16, 4, 128
WIN = 128
THETA = 10000.0
I0 = 8192
I1 = 4096

QW = ["L0g", "L0u", "L0d", "Wi", "Wf", "Wg", "Wo1", "U1", "D1"]
QW_SHAPE = {"L0g": (I0, D), "L0u": (I0, D), "L0d": (D, I0),
            "Wi": (D, D), "Wf": (D, D), "Wg": (D, D), "Wo1": (D, D),
            "U1": (I1, D), "D1": (D, I1)}
AW = ["Wq", "Wk", "Wv", "Wo"]
AW_SHAPE = {"Wq": (D, D), "Wk": (512, D), "Wv": (512, D), "Wo": (D, D)}
QW_IDX = {n: i for i, n in enumerate(QW)}
TWO26 = 67108864.0
HALF_MINUS = 0.5 - 2.0 ** -25
FWHT_POOL = False
MAIN_STOP = 99
MAIN_P4SUB = 99


def _stripe(w, c, n=8):
    w = np.asarray(w)
    k = w.shape[0] // n
    return np.ascontiguousarray(w[c * k:(c + 1) * k])


def _bf16(a):
    return np.asarray(a).astype(ml_dtypes.bfloat16)


def build_stats():
    nc = bass.Bass("TRN2", target_bir_lowering=False, debug=False)
    ins = {}
    for n in QW:
        r, c = QW_SHAPE[n]
        ins[n] = nc.dram_tensor("w_" + n, [r // 8, c], F32, kind="ExternalInput").ap()
    ones_in = nc.dram_tensor("ones_col", [128, 1], F32, kind="ExternalInput").ap()
    out = nc.dram_tensor("wsum", [1, 16], F32, kind="ExternalOutput").ap()

    tiles = []
    for n in QW:
        r, c = QW_SHAPE[n]
        for r0 in range(0, r // 8, 128):
            for c0 in range(0, c, 2048):
                tiles.append((n, r0, c0, min(2048, c - c0)))

    es = ExitStack()
    with es:
        buf = [es.enter_context(nc.sbuf_tensor(f"buf{i}", [128, 2048], F32)) for i in range(2)]
        acc = es.enter_context(nc.sbuf_tensor("acc", [128, 16], F32))
        red = es.enter_context(nc.sbuf_tensor("red", [128, 1], F32))
        onesc = es.enter_context(nc.sbuf_tensor("onesc", [128, 1], F32))
        row16 = es.enter_context(nc.sbuf_tensor("row16", [1, 16], F32))
        psum = es.enter_context(nc.psum_tensor("psum", [1, 16], F32))
        ld = es.enter_context(nc.semaphore("ld"))
        lda = es.enter_context(nc.semaphore("lda"))
        ldb = es.enter_context(nc.semaphore("ldb"))
        dv = es.enter_context(nc.semaphore("dv"))
        pe = es.enter_context(nc.semaphore("pe"))
        sv = es.enter_context(nc.semaphore("sv"))
        st = es.enter_context(nc.semaphore("st"))
        block = es.enter_context(nc.Block())

        @block.sync
        def _(sync):
            sync.dma_start(out=onesc[:], in_=ones_in).then_inc(ld, 16)
            for k, (n, r0, c0, cc) in enumerate(tiles):
                if k >= 2:
                    sync.wait_ge(dv, k - 1)
                sync.dma_start(out=buf[k % 2][:, 0:cc],
                               in_=ins[n][r0:r0 + 128, c0:c0 + cc]).then_inc(
                                   lda if k % 2 == 0 else ldb, 16)

        @block.vector
        def _(v):
            v.memset(acc[:], 0.0)
            v.drain()
            v.wait_ge(ld, 16)
            for k, (n, r0, c0, cc) in enumerate(tiles):
                wi = QW.index(n)
                v.wait_ge(lda if k % 2 == 0 else ldb, 16 * (k // 2 + 1))
                v.tensor_reduce(out=red[:], in_=buf[k % 2][:, 0:cc],
                                axis=AX.X, op=ALU.add, apply_absolute_value=True)
                v.drain()
                v.tensor_tensor(out=acc[:, wi:wi + 1], in0=acc[:, wi:wi + 1],
                                in1=red[:], op=ALU.add).then_inc(dv, 1)
                v.drain()

        @block.tensor
        def _(t):
            t.wait_ge(dv, len(tiles))
            t.wait_ge(ld, 16)
            t.matmul(out=psum[:], lhsT=onesc[:], rhs=acc[:], start=True,
                     stop=True).then_inc(pe, 1)

        @block.scalar
        def _(s):
            s.wait_ge(pe, 1)
            s.copy(out=row16[:], in_=psum[:]).then_inc(sv, 1)

        @block.gpsimd
        def _(g):
            g.wait_ge(sv, 1)
            g.dma_start(out=out, in_=row16[:]).then_inc(st, 16)
            g.wait_ge(st, 16)
    return nc


def stats_in_maps(inputs):
    maps = []
    for c in range(8):
        m = {"ones_col": np.ones((128, 1), np.float32)}
        m["w_L0g"] = _stripe(inputs["L0_ffn_w_gate"], c)
        m["w_L0u"] = _stripe(inputs["L0_ffn_w_up"], c)
        m["w_L0d"] = _stripe(inputs["L0_ffn_w_down"], c)
        m["w_Wi"] = _stripe(inputs["L1_attn_Wi"], c)
        m["w_Wf"] = _stripe(inputs["L1_attn_Wf"], c)
        m["w_Wg"] = _stripe(inputs["L1_attn_Wg"], c)
        m["w_Wo1"] = _stripe(inputs["L1_attn_Wo"], c)
        m["w_U1"] = _stripe(inputs["L1_ffn_w_up"], c)
        m["w_D1"] = _stripe(inputs["L1_ffn_w_down"], c)
        maps.append(m)
    return maps


def build_quantize():
    nc = bass.Bass("TRN2", target_bir_lowering=False, debug=False)
    ins, outs = {}, {}
    for n in QW:
        r, c = QW_SHAPE[n]
        ins[n] = nc.dram_tensor("w_" + n, [r // 8, c], F32, kind="ExternalInput").ap()
        outs[n] = nc.dram_tensor("q_" + n, [r // 8, c], BF16, kind="ExternalOutput").ap()
    for n in AW:
        r, c = AW_SHAPE[n]
        ins[n] = nc.dram_tensor("w_" + n, [r // 8, c], F32, kind="ExternalInput").ap()
        outs[n] = nc.dram_tensor("q_" + n, [r // 8, c], BF16, kind="ExternalOutput").ap()
    wsums = nc.dram_tensor("wsums", [8, 16], F32, kind="ExternalInput").ap()
    ones_row = nc.dram_tensor("ones_row", [1, 128], F32, kind="ExternalInput").ap()

    tiles = []
    for n in QW + AW:
        r, c = (QW_SHAPE | AW_SHAPE)[n]
        for r0 in range(0, r // 8, 128):
            rr = min(128, r // 8 - r0)
            for c0 in range(0, c, 2048):
                tiles.append((n, r0, rr, c0, min(2048, c - c0)))

    es = ExitStack()
    with es:
        buf = [es.enter_context(nc.sbuf_tensor(f"buf{i}", [128, 2048], F32)) for i in range(2)]
        qi = es.enter_context(nc.sbuf_tensor("qi", [128, 2048], I32))
        scr0 = es.enter_context(nc.sbuf_tensor("scr0", [128, 2048], F32))
        scr1 = es.enter_context(nc.sbuf_tensor("scr1", [128, 2048], F32))
        qb = [es.enter_context(nc.sbuf_tensor(f"qb{i}", [128, 2048], BF16)) for i in range(2)]
        ws = es.enter_context(nc.sbuf_tensor("ws", [8, 16], F32))
        onesf = es.enter_context(nc.sbuf_tensor("onesf", [1, 128], F32))
        srow = es.enter_context(nc.sbuf_tensor("srow", [1, 16], F32))
        scol = es.enter_context(nc.sbuf_tensor("scol", [128, 16], F32))
        pst = es.enter_context(nc.psum_tensor("pst", [16, 8], F32))
        psc = es.enter_context(nc.psum_tensor("psc", [128, 16], F32))
        idf = es.enter_context(nc.sbuf_tensor("idf", [128, 128], F32))
        ident_in = nc.dram_tensor("ident", [128, 128], F32, kind="ExternalInput").ap()
        ld = es.enter_context(nc.semaphore("ld"))
        lda = es.enter_context(nc.semaphore("lda"))
        ldb = es.enter_context(nc.semaphore("ldb"))
        pe = es.enter_context(nc.semaphore("pe"))
        dv = es.enter_context(nc.semaphore("dv"))
        dvt = es.enter_context(nc.semaphore("dvt"))
        stq = es.enter_context(nc.semaphore("stq"))
        stqa = es.enter_context(nc.semaphore("stqa"))
        stqb = es.enter_context(nc.semaphore("stqb"))
        block = es.enter_context(nc.Block())

        @block.sync
        def _(sync):
            sync.dma_start(out=ws[:], in_=wsums).then_inc(ld, 16)
            sync.dma_start(out=onesf[:], in_=ones_row).then_inc(ld, 16)
            sync.dma_start(out=idf[:], in_=ident_in).then_inc(ld, 16)
            for k, (n, r0, rr, c0, cc) in enumerate(tiles):
                if k >= 2:
                    sync.wait_ge(dvt, k - 1)
                sync.dma_start(out=buf[k % 2][0:rr, 0:cc],
                               in_=ins[n][r0:r0 + rr, c0:c0 + cc]).then_inc(
                                   lda if k % 2 == 0 else ldb, 16)

        @block.tensor
        def _(t):
            t.wait_ge(ld, 48)
            t.transpose(out=pst[:], in_=ws[:], identity=idf[0:8, 0:8]).then_inc(pe, 1)

        @block.vector
        def _(v):
            v.wait_ge(pe, 1)
            v.tensor_reduce(out=scol[0:16, 0:1], in_=pst[0:16, 0:8],
                            axis=AX.X, op=ALU.add).then_inc(dv, 1)
            v.drain()

        @block.tensor
        def _(t):
            t.wait_ge(dv, 1)
            t.transpose(out=psc[0:1, 0:16], in_=scol[0:16, 0:1],
                        identity=idf[0:16, 0:16]).then_inc(pe, 1)

        @block.vector
        def _(v):
            v.wait_ge(pe, 2)
            sizes = [float(QW_SHAPE[n][0] * QW_SHAPE[n][1]) for n in QW] + [1.0] * 7
            for wi in range(16):
                v.tensor_scalar(out=srow[0:1, wi:wi + 1], in0=psc[0:1, wi:wi + 1],
                                scalar1=1.0 / sizes[wi], scalar2=1e-5,
                                op0=ALU.mult, op1=ALU.max)
            v.drain()
            v.reciprocal(out=srow[:], in_=srow[:])
            v.drain()
            v.nop().then_inc(dv, 1)

        @block.tensor
        def _(t):
            t.wait_ge(dv, 2)
            t.matmul(out=psc[:], lhsT=onesf[:], rhs=srow[:], start=True,
                     stop=True).then_inc(pe, 1)

        @block.vector
        def _(v):
            v.wait_ge(pe, 3)
            v.tensor_copy(out=scol[:], in_=psc[:])
            v.drain()
            for k, (n, r0, rr, c0, cc) in enumerate(tiles):
                v.wait_ge(lda if k % 2 == 0 else ldb, 16 * (k // 2 + 1))
                if k >= 2:
                    v.wait_ge(stqa if k % 2 == 0 else stqb, 16 * (k // 2))
                src = buf[k % 2]
                if n in QW:
                    wi = QW.index(n)
                    v.tensor_scalar(out=src[0:rr, 0:cc], in0=src[0:rr, 0:cc],
                                    scalar1=scol[:, wi:wi + 1], scalar2=None,
                                    op0=ALU.mult)
                    v.drain()
                    v.tensor_scalar(out=scr0[0:rr, 0:cc], in0=src[0:rr, 0:cc],
                                    scalar1=HALF_MINUS, scalar2=TWO26,
                                    op0=ALU.subtract, op1=ALU.mult)
                    v.tensor_scalar(out=scr1[0:rr, 0:cc], in0=src[0:rr, 0:cc],
                                    scalar1=0.5, scalar2=-TWO26,
                                    op0=ALU.add, op1=ALU.mult)
                    v.drain()
                    v.tensor_scalar(out=scr0[0:rr, 0:cc], in0=scr0[0:rr, 0:cc],
                                    scalar1=0.0, scalar2=1.0,
                                    op0=ALU.max, op1=ALU.min)
                    v.tensor_scalar(out=scr1[0:rr, 0:cc], in0=scr1[0:rr, 0:cc],
                                    scalar1=0.0, scalar2=1.0,
                                    op0=ALU.max, op1=ALU.min)
                    v.drain()
                    v.tensor_tensor(out=qb[k % 2][0:rr, 0:cc],
                                    in0=scr0[0:rr, 0:cc], in1=scr1[0:rr, 0:cc],
                                    op=ALU.subtract).then_inc(dvt, 1)
                    v.drain()
                else:
                    v.tensor_copy(out=qb[k % 2][0:rr, 0:cc],
                                  in_=src[0:rr, 0:cc]).then_inc(dvt, 1)
                    v.drain()

        @block.gpsimd
        def _(g):
            na = nb = 0
            for k, (n, r0, rr, c0, cc) in enumerate(tiles):
                g.wait_ge(dvt, k + 1)
                if k % 2 == 0:
                    na += 16
                    g.dma_start(out=outs[n][r0:r0 + rr, c0:c0 + cc],
                                in_=qb[0][0:rr, 0:cc]).then_inc(stqa, 16)
                else:
                    nb += 16
                    g.dma_start(out=outs[n][r0:r0 + rr, c0:c0 + cc],
                                in_=qb[1][0:rr, 0:cc]).then_inc(stqb, 16)
            g.wait_ge(stqa, na)
            g.wait_ge(stqb, nb)
    return nc


def quantize_in_maps(inputs, wsums_all):
    key = {"L0g": "L0_ffn_w_gate", "L0u": "L0_ffn_w_up", "L0d": "L0_ffn_w_down",
           "Wi": "L1_attn_Wi", "Wf": "L1_attn_Wf", "Wg": "L1_attn_Wg",
           "Wo1": "L1_attn_Wo", "U1": "L1_ffn_w_up", "D1": "L1_ffn_w_down",
           "Wq": "L0_attn_Wq", "Wk": "L0_attn_Wk", "Wv": "L0_attn_Wv",
           "Wo": "L0_attn_Wo"}
    maps = []
    for c in range(8):
        m = {"wsums": np.ascontiguousarray(wsums_all),
             "ones_row": np.ones((1, 128), np.float32),
             "ident": np.eye(128, dtype=np.float32)}
        for n in QW + AW:
            m["w_" + n] = _stripe(inputs[key[n]], c)
        maps.append(m)
    return maps


ENGS = ("sync", "tensor", "vector", "scalar", "gpsimd")


class E:
    def __init__(self, nc, es):
        self.nc = nc
        self.es = es
        self.q = {e: [] for e in ENGS}
        self.c = {}
        self.h = {}
        self.n_bar = 0

    def sem(self, name):
        self.h[name] = self.es.enter_context(self.nc.semaphore(name))
        self.c[name] = 0
        return self.h[name]

    def sb(self, name, shape, dt):
        return self.es.enter_context(self.nc.sbuf_tensor(name, shape, dt))

    def psum(self, name, shape, dt=F32):
        return self.es.enter_context(self.nc.psum_tensor(name, shape, dt))

    def add(self, eng, fn):
        self.q[eng].append(fn)

    def inc(self, sem, by=1):
        self.c[sem] += by
        return self.c[sem]

    def wait(self, eng, sem, val):
        h = self.h[sem]
        self.add(eng, lambda e: e.wait_ge(h, val))

    def wait_ev(self, eng, tgt=None):
        if tgt is None:
            tgt = (self.c["evv"], self.c["evs"])
        self.wait(eng, "evv", tgt[0])
        self.wait(eng, "evs", tgt[1])

    def ev_now(self):
        return (self.c["evv"], self.c["evs"])

    def barrier(self):
        self.n_bar += 1
        tgt = 5 * self.n_bar
        h = self.h["bar"]
        for eng in ENGS:
            def f(e, h=h):
                e.drain()
                e.nop().then_inc(h, 1)
            self.add(eng, f)
        for eng in ENGS:
            self.wait(eng, "bar", tgt)

    def replay(self, block):
        for eng in ENGS:
            dec = getattr(block, eng)
            fns = self.q[eng]

            def body(e, fns=fns):
                for fn in fns:
                    fn(e)
            dec(body)


def _dma(E_, eng, out_ap, in_ap, sem="lw"):
    cur = E_.c[sem]
    h = E_.h[sem]
    tgt = E_.inc(sem, 16)
    E_.add(eng, lambda e: (e.wait_ge(h, cur),
                           e.dma_start(out=out_ap, in_=in_ap).then_inc(h, 16)))
    return tgt


def mm_fm2(E_, wt_dram, Kdim, Mout, rhs_of, n_tok, evac, wbufs, pool,
           mw=256, tw=512, ksplit=1):
    KC = Kdim // 128
    KCS = KC // ksplit
    src3 = wt_dram.rearrange("(kc p) m -> p kc m", p=128)
    pv = E_.h["pe"]
    E_.wait_ev("tensor")
    E_.wait("tensor", "vx", E_.c["vx"])
    E_.wait("tensor", "sx", E_.c["sx"])
    pe_done_at_bank = [None, None, None, None]
    bank_of = {}
    nbk = 0
    slab_done = []
    nslab = (Mout // mw) * ksplit
    for s in range(nslab):
        mi, kv = divmod(s, ksplit)
        wb = wbufs[s % 2]
        wsem = "lwa" if s % 2 == 0 else "lwb"
        if s >= 2:
            E_.wait("sync", "pe", slab_done[s - 2])
        wb3 = wb.rearrange("p (kc m) -> p kc m", m=mw)
        t_l = _dma(E_, "sync", wb3[:, 0:KCS, :],
                   src3[:, kv * KCS:(kv + 1) * KCS, mi * mw:(mi + 1) * mw],
                   wsem)
        E_.wait("tensor", wsem, t_l)
        for mm_i in range(mw // 128):
            for t0 in range(0, n_tok, tw):
                key = (mi * (mw // 128) + mm_i, t0)
                if key not in bank_of:
                    bank_of[key] = nbk % 4
                    nbk += 1
                bk = bank_of[key]
                ps = pool[bk]
                if kv == 0 and pe_done_at_bank[bk] is not None:
                    E_.wait_ev("tensor", pe_done_at_bank[bk])
                for kc in range(KCS):
                    st_ = (kv == 0 and kc == 0)
                    sp_ = (kv == ksplit - 1 and kc == KCS - 1)
                    inc_ = sp_ or kc == KCS - 1
                    lhsT = wb[:, kc * mw + mm_i * 128: kc * mw + mm_i * 128 + 128]
                    rhs = rhs_of(kv * KCS + kc, t0)
                    if inc_:
                        E_.add("tensor", lambda e, o=ps, l=lhsT, r=rhs, a=st_,
                               b=sp_: e.matmul(out=o[:], lhsT=l, rhs=r, start=a,
                                               stop=b).then_inc(pv, 1))
                        E_.inc("pe")
                    else:
                        E_.add("tensor", lambda e, o=ps, l=lhsT, r=rhs, a=st_:
                               e.matmul(out=o[:], lhsT=l, rhs=r, start=a,
                                        stop=False))
                if kv == ksplit - 1:
                    evac(mi * mw + mm_i * 128, t0, ps, E_.c["pe"])
                    pe_done_at_bank[bk] = E_.ev_now()
        slab_done.append(E_.c["pe"])


def col_reduce_to_row(E_, src_sb, W, out_row, op, identf, tmpcol, pst, psr,
                      extra_in=None):
    pv, vv, sv = E_.h["pe"], E_.h["vx"], E_.h["sx"]
    E_.add("vector", lambda e: e.nop().then_inc(vv, 1))
    t0_ = E_.inc("vx")
    E_.wait("tensor", "vx", t0_)
    for b in range(W // 128):
        sl = src_sb[:, b * 128:(b + 1) * 128]
        E_.add("tensor", lambda e, s=sl: e.transpose(out=pst[:], in_=s,
                                                     identity=identf[:]).then_inc(pv, 1))
        t1 = E_.inc("pe")
        E_.wait("vector", "pe", t1)
        E_.add("vector", lambda e: e.tensor_reduce(
            out=tmpcol[:], in_=pst[:], axis=AX.X, op=op).then_inc(vv, 1))
        t2 = E_.inc("vx")
        E_.wait("tensor", "vx", t2)
        E_.add("tensor", lambda e: e.transpose(out=psr[0:1, 0:128], in_=tmpcol[:],
                                               identity=identf[:]).then_inc(pv, 1))
        t3 = E_.inc("pe")
        E_.wait("scalar", "pe", t3)
        E_.add("scalar", lambda e, b=b: e.copy(
            out=out_row[0:1, b * 128:(b + 1) * 128],
            in_=psr[0:1, 0:128]).then_inc(sv, 1))
        t4 = E_.inc("sx")
        E_.wait("tensor", "vx", t2)
        E_.wait("tensor", "sx", t4)
    E_.wait("vector", "sx", E_.c["sx"])

def fold_chain(E_, srcs, out, op, eng="vector"):
    v = E_.h["vx" if eng == "vector" else "gx"]
    E_.add(eng, lambda e, a=srcs[0], b=srcs[1], o=out: (
        e.tensor_tensor(out=o, in0=a, in1=b, op=op), e.drain()))
    for s in srcs[2:]:
        E_.add(eng, lambda e, a=s, o=out: (
            e.tensor_tensor(out=o, in0=o, in1=a, op=op), e.drain()))


def row_recip(E_, row, W):
    E_.add("vector", lambda e: (e.reciprocal(out=row[0:1, 0:W], in_=row[0:1, 0:W]),
                                e.drain()))


def bcast_row(E_, row_ap, W, ps_out, ones_row_f):
    pv = E_.h["pe"]
    E_.wait("tensor", "vx", E_.c["vx"])
    E_.wait("tensor", "sx", E_.c["sx"])
    E_.add("tensor", lambda e: e.matmul(out=ps_out, lhsT=ones_row_f,
                                        rhs=row_ap, start=True,
                                        stop=True).then_inc(pv, 1))
    return E_.inc("pe")


def ln_block(E_, x_of, nch, W, xn_of, ident_f, ones_row_f, F2, rows, PS, eps=1e-5):
    vv, sv = E_.h["vx"], E_.h["sx"]
    D_ = nch * 128
    fold_chain(E_, [x_of(c) for c in range(nch)], F2[:, 0:W], ALU.add)
    def sq0(e):
        e.tensor_tensor(out=F2[:, W:2 * W], in0=x_of(0), in1=x_of(0), op=ALU.mult)
        e.drain()
    E_.add("vector", sq0)
    for c in range(1, nch):
        def sqc(e, c=c):
            e.tensor_tensor(out=F2[:, 2 * W:3 * W], in0=x_of(c), in1=x_of(c),
                            op=ALU.mult)
            e.drain()
            e.tensor_tensor(out=F2[:, W:2 * W], in0=F2[:, W:2 * W],
                            in1=F2[:, 2 * W:3 * W], op=ALU.add)
            e.drain()
        E_.add("vector", sqc)
    col_reduce_to_row(E_, F2[:, 0:W], W, rows["m"], ALU.add, ident_f,
                      rows["tc"], PS["pst"], PS["psr"])
    col_reduce_to_row(E_, F2[:, W:2 * W], W, rows["s"], ALU.add, ident_f,
                      rows["tc"], PS["pst"], PS["psr"])
    def rowmath(e):
        e.tensor_scalar(out=rows["m"][0:1, 0:W], in0=rows["m"][0:1, 0:W],
                        scalar1=1.0 / D_, scalar2=None, op0=ALU.mult)
        e.tensor_scalar(out=rows["s"][0:1, 0:W], in0=rows["s"][0:1, 0:W],
                        scalar1=1.0 / D_, scalar2=None, op0=ALU.mult)
        e.drain()
        e.tensor_tensor(out=rows["v"][0:1, 0:W], in0=rows["m"][0:1, 0:W],
                        in1=rows["m"][0:1, 0:W], op=ALU.mult)
        e.drain()
        e.tensor_tensor(out=rows["v"][0:1, 0:W], in0=rows["s"][0:1, 0:W],
                        in1=rows["v"][0:1, 0:W], op=ALU.subtract)
        e.drain()
        e.tensor_scalar(out=rows["v"][0:1, 0:W], in0=rows["v"][0:1, 0:W],
                        scalar1=float(eps), scalar2=None, op0=ALU.add)
        e.drain()
        e.nop().then_inc(E_.h["vx"], 1)
    E_.add("vector", rowmath)
    t = E_.inc("vx")
    E_.wait("scalar", "vx", t)
    E_.add("scalar", lambda e: (e.activation(out=rows["v"][0:1, 0:W],
                                             in_=rows["v"][0:1, 0:W],
                                             func=AF.Sqrt),
                                e.drain(), e.nop().then_inc(sv, 1)))
    ts = E_.inc("sx")
    E_.wait("vector", "sx", ts)
    def rowfin(e):
        e.reciprocal(out=rows["v"][0:1, 0:W], in_=rows["v"][0:1, 0:W])
        e.drain()
        e.tensor_scalar(out=rows["m"][0:1, 0:W], in0=rows["m"][0:1, 0:W],
                        scalar1=-1.0, scalar2=None, op0=ALU.mult)
        e.drain()
        e.nop().then_inc(E_.h["vx"], 1)
    E_.add("vector", rowfin)
    tv = E_.inc("vx")
    E_.wait("tensor", "vx", tv)
    t2 = bcast_row(E_, rows["m"][0:1, 0:W], W, PS["b1"][:, 0:W], ones_row_f)
    t2 = bcast_row(E_, rows["v"][0:1, 0:W], W, PS["b2"][:, 0:W], ones_row_f)
    E_.wait("vector", "pe", t2)
    for c in range(nch):
        E_.add("vector", lambda e, c=c: (
            e.tensor_tensor(out=F2[:, 2 * W:3 * W], in0=x_of(c),
                            in1=PS["b1"][:, 0:W], op=ALU.add),
            e.drain(),
            e.tensor_tensor(out=xn_of(c), in0=F2[:, 2 * W:3 * W],
                            in1=PS["b2"][:, 0:W], op=ALU.mult),
            e.drain()))
    E_.add("vector", lambda e: e.nop().then_inc(vv, 1))
    return E_.inc("vx")


def quant_block(E_, src_of, nch, W, qb, out_of, rows, F1, F2, I32buf,
                ident_f, ones_row_f, PS, prescale=1.0, clip_hi=False):
    vv = E_.h["vx"]
    fold_chain(E_, [src_of(c) for c in range(nch)], F2[:, 0:W], ALU.max)
    fold_chain(E_, [src_of(c) for c in range(nch)], F2[:, W:2 * W], ALU.min)
    col_reduce_to_row(E_, F2[:, 0:W], W, rows["q"], ALU.max, ident_f,
                      rows["tc"], PS["pst"], PS["psr"])
    col_reduce_to_row(E_, F2[:, W:2 * W], W, rows["v"], ALU.min, ident_f,
                      rows["tc"], PS["pst"], PS["psr"])
    def rowmath(e):
        e.tensor_scalar(out=rows["v"][0:1, 0:W], in0=rows["v"][0:1, 0:W],
                        scalar1=-1.0, scalar2=None, op0=ALU.mult)
        e.drain()
        e.tensor_tensor(out=rows["q"][0:1, 0:W], in0=rows["q"][0:1, 0:W],
                        in1=rows["v"][0:1, 0:W], op=ALU.max)
        e.drain()
        e.tensor_scalar(out=rows["q"][0:1, 0:W], in0=rows["q"][0:1, 0:W],
                        scalar1=prescale, scalar2=1e-5, op0=ALU.mult,
                        op1=ALU.max)
        e.drain()
        e.reciprocal(out=rows["m"][0:1, 0:W], in_=rows["q"][0:1, 0:W])
        e.drain()
        e.tensor_scalar(out=rows["s"][0:1, 0:W], in0=rows["m"][0:1, 0:W],
                        scalar1=qb * prescale, scalar2=None, op0=ALU.mult)
        e.drain()
    E_.add("vector", rowmath)
    tv = E_.inc("vx")
    E_.add("vector", lambda e: (e.drain(), e.nop().then_inc(vv, 1)))
    E_.wait("tensor", "vx", tv)
    tb = bcast_row(E_, rows["s"][0:1, 0:W], W, PS["b1"][:, 0:W], ones_row_f)
    E_.wait("vector", "pe", tb)
    for c in range(nch):
        def f(e, c=c):
            e.tensor_tensor(out=F1[:, 0:W], in0=src_of(c), in1=PS["b1"][:, 0:W],
                            op=ALU.mult)
            e.drain()
            e.tensor_copy(out=I32buf[:, 0:W], in_=F1[:, 0:W])
            e.drain()
            e.tensor_copy(out=F2[:, 0:W], in_=I32buf[:, 0:W])
            e.drain()
            e.tensor_tensor(out=F2[:, W:2 * W], in0=F1[:, 0:W],
                            in1=F2[:, 0:W], op=ALU.subtract)
            e.drain()
            e.tensor_scalar(out=F1[:, 0:W], in0=F2[:, W:2 * W],
                            scalar1=HALF_MINUS, scalar2=TWO26,
                            op0=ALU.subtract, op1=ALU.mult)
            e.tensor_scalar(out=F2[:, W:2 * W], in0=F2[:, W:2 * W],
                            scalar1=0.5, scalar2=-TWO26,
                            op0=ALU.add, op1=ALU.mult)
            e.drain()
            e.tensor_scalar(out=F1[:, 0:W], in0=F1[:, 0:W],
                            scalar1=0.0, scalar2=1.0,
                            op0=ALU.max, op1=ALU.min)
            e.tensor_scalar(out=F2[:, W:2 * W], in0=F2[:, W:2 * W],
                            scalar1=0.0, scalar2=1.0,
                            op0=ALU.max, op1=ALU.min)
            e.drain()
            e.tensor_tensor(out=F2[:, 0:W], in0=F2[:, 0:W],
                            in1=F1[:, 0:W], op=ALU.add)
            e.drain()
            if clip_hi:
                e.tensor_tensor(out=F2[:, 0:W], in0=F2[:, 0:W],
                                in1=F2[:, W:2 * W], op=ALU.subtract)
                e.drain()
                e.tensor_scalar(out=out_of(c), in0=F2[:, 0:W],
                                scalar1=float(qb - 1), scalar2=None, op0=ALU.min)
            else:
                e.tensor_tensor(out=out_of(c), in0=F2[:, 0:W],
                                in1=F2[:, W:2 * W], op=ALU.subtract)
            e.drain()
        E_.add("vector", f)
    tv = E_.inc("vx")
    E_.add("vector", lambda e: (e.drain(), e.nop().then_inc(vv, 1)))
    return tv


def dequant_row(E_, src_row, dst_row, dst_off, W, mrow_ap, qb, extra=1.0):
    vv = E_.h["vx"]

    def f(e):
        e.tensor_scalar(out=dst_row[0:1, dst_off:dst_off + W],
                        in0=src_row[0:1, 0:W],
                        scalar1=mrow_ap, scalar2=extra / qb,
                        op0=ALU.mult, op1=ALU.mult)
        e.drain()
        e.nop().then_inc(vv, 1)
    E_.add("vector", f)
    return E_.inc("vx")


def fwht_pe_stage(E_, ch_of, nch, W, h128, pool):
    pv = E_.h["pe"]
    E_.wait("tensor", "vx", E_.c["vx"])
    E_.wait("tensor", "sx", E_.c["sx"])
    E_.wait_ev("tensor")
    done = [None, None, None, None]
    for c in range(nch):
        bk = c % 4
        ps = pool[bk]
        if done[bk] is not None:
            E_.wait_ev("tensor", done[bk])
        E_.add("tensor", lambda e, ps=ps, c=c: e.matmul(
            out=ps[:, 0:W], lhsT=h128, rhs=ch_of(c), start=True,
            stop=True).then_inc(pv, 1))
        t = E_.inc("pe")
        E_.wait("scalar", "pe", t)
        E_.add("scalar", lambda e, ps=ps, c=c: (
            e.copy(out=ch_of(c), in_=ps[:, 0:W]).then_inc(E_.h["evs"], 1),
            e.drain()))
        E_.inc("evs")
        done[bk] = E_.ev_now()


def fwht_chunk_stages(E_, ch_of, nch, W, slot):
    import math

    Wh = W if not FWHT_POOL else ((W * 5) // 8) & ~31
    ptmp = E_.fwht_ptmp
    E_.wait("vector", "sx", E_.c["sx"])
    E_.wait_ev("vector")
    E_.wait("gpsimd", "sx", E_.c["sx"])
    E_.wait("gpsimd", "vx", E_.c["vx"])
    E_.wait("gpsimd", "lg", E_.c["lg"])
    E_.wait_ev("gpsimd")

    def buf(key):
        kind, idx = key
        return ch_of(idx) if kind == "c" else slot[:, idx * W:(idx + 1) * W]

    def emit_pair(pa, pb, o1, o2):
        def fv(e, pa=pa, pb=pb, o1=o1, o2=o2):
            e.tensor_tensor(out=o1[:, 0:Wh], in0=pa[:, 0:Wh],
                            in1=pb[:, 0:Wh], op=ALU.add)
            e.tensor_tensor(out=o2[:, 0:Wh], in0=pa[:, 0:Wh],
                            in1=pb[:, 0:Wh], op=ALU.subtract)
            e.drain()
        E_.add("vector", fv)

        if Wh == W:
            return

        def fg(e, pa=pa, pb=pb, o1=o1, o2=o2):
            e.tensor_scalar_mul(ptmp[:, 0:W - Wh], pb[:, Wh:W], -1.0)
            e.drain()
            e.tensor_add(out=o1[:, Wh:W], in0=pa[:, Wh:W], in1=pb[:, Wh:W])
            e.tensor_add(out=o2[:, Wh:W], in0=pa[:, Wh:W],
                         in1=ptmp[:, 0:W - Wh])
            e.drain()
        E_.add("gpsimd", fg)

    def emit_move(s, d):
        E_.add("vector", lambda e, s=s, d=d: (
            e.tensor_copy(out=d[:, 0:Wh], in_=s[:, 0:Wh]), e.drain()))
        if Wh == W:
            return
        E_.add("gpsimd", lambda e, s=s, d=d: (
            e.tensor_copy(out=d[:, Wh:W], in_=s[:, Wh:W]), e.drain()))

    loc = {c: ("c", c) for c in range(nch)}
    free = [("s", 0), ("s", 1)]
    for t in range(int(math.log2(nch))):
        step = 1 << t
        for base in range(0, nch, 2 * step):
            for off in range(step):
                a_, b_ = base + off, base + off + step
                d1 = free.pop()
                d2 = free.pop()
                emit_pair(buf(loc[a_]), buf(loc[b_]), buf(d1), buf(d2))
                free.append(loc[a_])
                free.append(loc[b_])
                loc[a_] = d1
                loc[b_] = d2
    occupied = {loc[c]: c for c in range(nch)}
    pending = set(c for c in range(nch) if loc[c] != ("c", c))
    while pending:
        prog = False
        for c in sorted(pending):
            tgt = ("c", c)
            if tgt not in occupied:
                emit_move(buf(loc[c]), buf(tgt))
                del occupied[loc[c]]
                free.append(loc[c])
                if tgt in free:
                    free.remove(tgt)
                loc[c] = tgt
                occupied[tgt] = c
                pending.discard(c)
                prog = True
                break
        if not prog:
            c = sorted(pending)[0]
            sp = free.pop()
            emit_move(buf(loc[c]), buf(sp))
            del occupied[loc[c]]
            free.append(loc[c])
            loc[c] = sp
            occupied[sp] = c
    E_.add("gpsimd", lambda e: (e.drain(), e.nop().then_inc(E_.h["gx"], 1)))
    tg = E_.inc("gx")
    E_.wait("vector", "gx", tg)
    E_.wait("tensor", "gx", tg)
    E_.wait("scalar", "gx", tg)
    E_.add("vector", lambda e: e.nop().then_inc(E_.h["vx"], 1))
    return E_.inc("vx")


def rope_emit(E_, ps, ps2, sb_raw, cos, sin, out, W, pe_tgt, rotb, s1, s2):
    E_.wait("scalar", "pe", pe_tgt)
    E_.wait_ev("scalar")
    E_.add("scalar", lambda e: (e.copy(out=sb_raw, in_=ps), e.drain(),
                                e.nop().then_inc(E_.h["sx"], 1)))
    ts = E_.inc("sx")
    E_.wait("tensor", "sx", ts)
    E_.add("tensor", lambda e: e.matmul(out=ps2, lhsT=rotb, rhs=sb_raw,
                                        start=True, stop=True).then_inc(E_.h["pe"], 1))
    tp = E_.inc("pe")
    E_.wait("vector", "pe", tp)

    def f(e):
        e.tensor_tensor(out=s1, in0=sb_raw, in1=cos, op=ALU.mult)
        e.tensor_tensor(out=s2, in0=ps2, in1=sin, op=ALU.mult)
        e.drain()
        e.tensor_tensor(out=out, in0=s1, in1=s2,
                        op=ALU.add).then_inc(E_.h["evv"], 1)
        e.drain()
    E_.add("vector", f)
    return E_.inc("evv")


def build_main():
    nc = bass.Bass("TRN2", target_bir_lowering=False, debug=False)
    dr = lambda n, s, dt, k: nc.dram_tensor(n, s, dt, kind=k).ap()
    x_fm = dr("x_fm", [D, S], F32, "ExternalInput")
    x_own = dr("x_own", [D, T], F32, "ExternalInput")
    wt = {}
    for n, (r, c) in {"q": (D, D), "k": (D, 512), "v": (D, 512), "o": (D, D),
                      "g0": (D, I0), "u0": (D, I0), "d0": (I0, D),
                      "wi": (D, D), "wf": (D, D), "wg": (D, D),
                      "u1": (D, I1), "d1": (I1, D)}.items():
        wt[n] = dr("wt_" + n, [r, c], BF16, "ExternalInput")
    wsums = dr("wsums_d", [8, 16], F32, "ExternalInput")
    kcos_d = dr("kcos_d", [128, S], BF16, "ExternalInput")
    ksin_d = dr("ksin_d", [128, S], BF16, "ExternalInput")
    qcos_d = dr("qcos_d", [128, T], BF16, "ExternalInput")
    qsin_d = dr("qsin_d", [128, T], BF16, "ExternalInput")
    masks_d = dr("masks_d", [8, 128, 512], BF16, "ExternalInput")
    nrm_d = dr("nrms_d", [128, 48], F32, "ExternalInput")
    identf_d = dr("identf_d", [128, 128], F32, "ExternalInput")
    identb_d = dr("identb_d", [128, 128], BF16, "ExternalInput")
    h128_d = dr("h128_d", [128, 128], BF16, "ExternalInput")
    h128f_d = dr("h128f_d", [128, 128], F32, "ExternalInput")
    rotb_d = dr("rotb_d", [128, 128], BF16, "ExternalInput")
    onesr_d = dr("onesr_d", [1, 128], F32, "ExternalInput")
    x1_o = dr("x1_o", [128, 16 * T], F32, "ExternalOutput")
    sl_o = dr("sl_o", [128, 16 * T], BF16, "ExternalOutput")
    P_o = dr("P_o", [128, 16 * T], BF16, "ExternalOutput")
    sg_o = dr("sg_o", [128, 16 * T], BF16, "ExternalOutput")
    g1_o = dr("g1_o", [128, 16 * T], BF16, "ExternalOutput")
    summ_o = dr("summ_o", [128, 256], F32, "ExternalOutput")
    xn1_d = dr("xn1_d", [128, 16 * T], F32, "Internal")
    x1i_d = dr("x1i_d", [128, 16 * T], F32, "Internal")
    i_d = dr("i_d", [128, 16 * T], BF16, "Internal")

    es = ExitStack()
    E_ = E(nc, es)
    with es:
        A1 = E_.sb("A1", [128, 16384], BF16)
        A2 = E_.sb("A2", [128, 16384], BF16)
        A3 = E_.sb("A3", [128, 16384], BF16)
        A4 = E_.sb("A4", [128, 16384], BF16)
        F1 = E_.sb("F1", [128, 4096], F32)
        F2 = E_.sb("F2", [128, 2048], F32)
        W0 = E_.sb("W0", [128, 4096], BF16)
        W1 = E_.sb("W1", [128, 4096], BF16)
        qcos = E_.sb("qcos", [128, T], BF16)
        qsin = E_.sb("qsin", [128, T], BF16)
        nrm = E_.sb("nrm", [128, 48], F32)
        identf = E_.sb("identf", [128, 128], F32)
        identb = E_.sb("identb", [128, 128], BF16)
        h128 = E_.sb("h128", [128, 128], BF16)
        h128f = E_.sb("h128f", [128, 128], F32)
        rotb = E_.sb("rotb", [128, 128], BF16)
        onesr = E_.sb("onesr", [1, 128], F32)
        krop = E_.sb("krop", [128, 512], BF16)
        wqh = E_.sb("wqh", [128, 2048], BF16)
        qh = E_.sb("qh", [128, T], BF16)
        zz = E_.sb("zz", [128, 128], BF16)
        E_.fwht_ptmp = E_.sb("ptmp", [128, 192], F32)
        sc_extra = E_.sb("sc_extra", [128, 512], F32)
        scm = E_.sb("scm", [128, 512], F32)
        rcpb = E_.sb("rcpb", [128, 128], F32)
        rrow = E_.sb("rrow", [1, 128], F32)
        ibuf = E_.sb("ibuf", [128, 256], I32)
        summ = E_.sb("summ", [128, 256], F32)
        wsr = E_.sb("wsr", [8, 16], F32)
        mrow = E_.sb("mrow", [1, 16], F32)
        stg = E_.sb("stg", [128, 1024], BF16)
        rows = {}
        for k, w in [("m", 256), ("s", 256), ("v", 256), ("q", 256),
                     ("r1", 1024), ("r2", 1024), ("r3", 1024)]:
            rows[k] = E_.sb("row_" + k, [1, w], F32)
        rows["tc"] = E_.sb("row_tc", [128, 1], F32)
        stat = {k: E_.sb("st_" + k, [128, 1], F32)
                for k in ["mx1", "mx2", "neg", "sum", "rcp"]}
        PSf = E_.psum("PSf", [128, 3584], F32)
        PSB = E_.psum("PSBt", [128, 1024], BF16)
        PS = {
            "pool": [PSf[:, i * 512:(i + 1) * 512] for i in range(4)],
            "pst": PSf[:, 2048:2176], "psr": PSf[0:1, 2176:2304],
            "b1": PSf[:, 2560:3072], "b2": PSf[:, 3072:3584],
        }
        for s_ in ["lw", "lwa", "lwb", "lg", "pe", "vx", "sx", "gx", "evv",
                   "evs", "bar"]:
            E_.sem(s_)

        def _stop(n):
            if MAIN_STOP > n:
                return False
            tlg_ = E_.c["lg"]
            tlw_ = E_.c["lw"]
            for g_ in ENGS:
                E_.wait(g_, "lg", tlg_)
                E_.wait(g_, "lw", tlw_)
            E_.barrier()
            blk_ = es.enter_context(nc.Block())
            E_.replay(blk_)
            return True

        psbslot = [PSB[:, 0:128],
                   PSf[:, 0:512].bitcast(BF16)[:, 0:128],
                   PSf[:, 1024:1536].bitcast(BF16)[:, 0:128],
                   PSf[:, 1536:2048].bitcast(BF16)[:, 0:128],
                   PSf[:, 2048:2560].bitcast(BF16)[:, 0:128],
                   PSf[:, 2560:3072].bitcast(BF16)[:, 0:128],
                   PSf[:, 3072:3584].bitcast(BF16)[:, 0:128]]

        kcos = A4[:, 8192:12288]
        ksin = A4[:, 12288:16384]
        wk_sl = A3[:, 4096:12288]
        wv_sl = A4[:, 0:8192]
        mskb = F2[:].bitcast(BF16)
        pr = F1[:].bitcast(BF16)[:, 0:4096]
        pT = F1[:].bitcast(BF16)[:, 4096:8192]

        for dst, src in [(kcos, kcos_d), (ksin, ksin_d), (qcos[:], qcos_d),
                         (qsin[:], qsin_d), (nrm[:], nrm_d), (identf[:], identf_d),
                         (identb[:], identb_d), (h128[:], h128_d),
                         (h128f[:], h128f_d),
                         (rotb[:], rotb_d),
                         (onesr[:], onesr_d), (wsr[:], wsums)]:
            _dma(E_, "sync", dst, src)
        for wname, dstsl in [("k", wk_sl), ("v", wv_sl)]:
            src3 = wt[wname].rearrange("(kc p) m -> p kc m", p=128)
            dst3 = dstsl.rearrange("p (kc m) -> p kc m", m=512)
            _dma(E_, "sync", dst3, src3)
        tl = E_.c["lw"]
        for g in ENGS:
            E_.wait(g, "lw", tl)
        E_.add("vector", lambda e: (e.memset(zz[:], 0.0), e.drain()))
        E_.add("tensor", lambda e: e.transpose(
            out=PSf[0:16, 2048:2056], in_=wsr[:], identity=identf[0:8, 0:8]
        ).then_inc(E_.h["pe"], 1))
        t = E_.inc("pe")
        E_.wait("vector", "pe", t)
        E_.add("vector", lambda e: (e.tensor_reduce(
            out=rows["tc"][0:16, 0:1], in_=PSf[0:16, 2048:2056],
            axis=AX.X, op=ALU.add),
            e.drain(), e.nop().then_inc(E_.h["vx"], 1)))
        t = E_.inc("vx")
        E_.wait("tensor", "vx", t)
        E_.add("tensor", lambda e: e.transpose(
            out=PSf[0:1, 2176:2192], in_=rows["tc"][0:16, 0:1],
            identity=identf[0:16, 0:16]).then_inc(E_.h["pe"], 1))
        t = E_.inc("pe")
        E_.wait("vector", "pe", t)
        sizes = {n: float(QW_SHAPE[n][0] * QW_SHAPE[n][1]) for n in QW}

        def mprep(e):
            for wi, n in enumerate(QW):
                e.tensor_scalar(out=mrow[0:1, wi:wi + 1],
                                in0=PSf[0:1, 2176 + wi:2177 + wi],
                                scalar1=1.0 / sizes[n], scalar2=1e-5,
                                op0=ALU.mult, op1=ALU.max)
            e.drain()
            e.nop().then_inc(E_.h["vx"], 1)
        E_.add("vector", mprep)
        E_.inc("vx")
        E_.barrier()

        if _stop(1):
            return nc
        xsl3 = x_fm.rearrange("(c p) t -> p c t", p=128)
        for s_ in range(16):
            t0 = s_ * 256
            xs3 = F1.rearrange("p (c w) -> p c w", w=256)
            tlw = _dma(E_, "sync", xs3, xsl3[:, :, t0:t0 + 256])
            E_.wait("vector", "lw", tlw)
            E_.wait("scalar", "lw", tlw)
            x_of = lambda c: F1[:, c * 256:(c + 1) * 256]
            xn_of = lambda c: A3[:, c * 256:(c + 1) * 256]
            tln = ln_block(E_, x_of, 16, 256, xn_of, identf[:], onesr[:], F2, rows, PS)
            E_.wait("tensor", "vx", tln)
            for m in range(4):
                ps = PS["pool"][m % 2]
                ps2 = PS["pool"][2 + m % 2]
                if m >= 2:
                    E_.wait_ev("tensor")
                for kc in range(16):
                    lhsT = wk_sl[:, kc * 512 + m * 128: kc * 512 + m * 128 + 128]
                    if kc == 15:
                        E_.add("tensor", lambda e, o=ps, l=lhsT, c=kc: e.matmul(
                            out=o[:, 0:256], lhsT=l,
                            rhs=A3[:, c * 256:(c + 1) * 256],
                            start=(c == 0), stop=True).then_inc(E_.h["pe"], 1))
                    else:
                        E_.add("tensor", lambda e, o=ps, l=lhsT, c=kc: e.matmul(
                            out=o[:, 0:256], lhsT=l,
                            rhs=A3[:, c * 256:(c + 1) * 256],
                            start=(c == 0), stop=False))
                tpe = E_.inc("pe")
                rope_emit(E_, ps[:, 0:256], ps2[:, 0:256], krop[:, 0:256],
                          kcos[:, t0:t0 + 256], ksin[:, t0:t0 + 256],
                          A1[:, m * 4096 + t0: m * 4096 + t0 + 256], 256,
                          tpe, rotb[:], F2[:, 0:256], F2[:, 256:512])
            for tt in range(2):
                ps = PS["pool"][tt]
                E_.wait_ev("tensor")
                for kc in range(16):
                    lhsT = A3[:, kc * 256 + tt * 128: kc * 256 + tt * 128 + 128]
                    rhs = wv_sl[:, kc * 512:(kc + 1) * 512]
                    if kc == 15:
                        E_.add("tensor", lambda e, o=ps, l=lhsT, r=rhs: e.matmul(
                            out=o[:], lhsT=l, rhs=r, start=False,
                            stop=True).then_inc(E_.h["pe"], 1))
                    else:
                        E_.add("tensor", lambda e, o=ps, l=lhsT, r=rhs, c=kc:
                               e.matmul(out=o[:], lhsT=l, rhs=r, start=(c == 0),
                                        stop=False))
                tpe = E_.inc("pe")
                E_.wait("scalar", "pe", tpe)
                gt = (t0 + tt * 128) // 128
                E_.add("scalar", lambda e, o=A2[:, gt * 512:(gt + 1) * 512],
                       i_=ps: (e.copy(out=o, in_=i_[:]), e.drain(),
                               e.nop().then_inc(E_.h["sx"], 1)))
                E_.inc("sx")
            E_.wait_ev("sync")
            E_.wait("sync", "sx", E_.c["sx"])
            E_.wait("sync", "vx", E_.c["vx"])
        E_.barrier()

        if _stop(2):
            return nc
        xo3 = x_own.rearrange("(c p) t -> p c t", p=128)
        for s_ in range(4):
            t0 = s_ * 256
            xs3 = F1.rearrange("p (c w) -> p c w", w=256)
            tlw = _dma(E_, "sync", xs3, xo3[:, :, t0:t0 + 256])
            E_.wait("vector", "lw", tlw)
            E_.wait("scalar", "lw", tlw)
            x_of = lambda c: F1[:, c * 256:(c + 1) * 256]
            xn_of = lambda c, t0=t0: A3[:, c * 1024 + t0: c * 1024 + t0 + 256]
            tln = ln_block(E_, x_of, 16, 256, xn_of, identf[:], onesr[:], F2, rows, PS)
            E_.wait("sync", "vx", tln)
        E_.barrier()

        if _stop(3):
            return nc
        tmsk = _dma(E_, "sync", mskb.rearrange("p (i w) -> p i w", w=512),
                    masks_d.rearrange("i p w -> p i w"))
        E_.wait("vector", "lw", tmsk)
        wq3 = wt["q"].rearrange("(kc p) m -> p kc m", p=128)
        for h in range(16):
            kvh = h // 4
            wqh3 = wqh.rearrange("p (kc m) -> p kc m", m=128)
            E_.wait("sync", "pe", E_.c["pe"])
            tlw = _dma(E_, "sync", wqh3, wq3[:, :, h * 128:(h + 1) * 128])
            E_.wait("tensor", "lw", tlw)
            for th in range(2):
                ps = PS["pool"][th]
                ps2 = PS["pool"][2 + th]
                E_.wait_ev("tensor")
                for kc in range(16):
                    lhsT = wqh[:, kc * 128:(kc + 1) * 128]
                    rhs = A3[:, kc * 1024 + th * 512: kc * 1024 + th * 512 + 512]
                    if kc == 15:
                        E_.add("tensor", lambda e, o=ps, l=lhsT, r=rhs: e.matmul(
                            out=o[:], lhsT=l, rhs=r, start=False,
                            stop=True).then_inc(E_.h["pe"], 1))
                    else:
                        E_.add("tensor", lambda e, o=ps, l=lhsT, r=rhs, c=kc:
                               e.matmul(out=o[:], lhsT=l, rhs=r, start=(c == 0),
                                        stop=False))
                tpe = E_.inc("pe")
                rope_emit(E_, ps, ps2, krop[:, 0:512],
                          qcos[:, th * 512:(th + 1) * 512],
                          qsin[:, th * 512:(th + 1) * 512],
                          qh[:, th * 512:(th + 1) * 512], 512, tpe, rotb[:],
                          sc_extra[:, 0:512], scm[:, 0:512])
            E_.wait_ev("tensor")

            def p4cut(n):
                if MAIN_P4SUB >= n:
                    return False
                E_.wait("tensor", "vx", E_.c["vx"])
                E_.wait("tensor", "sx", E_.c["sx"])
                return True

            for i in range(8):
                if p4cut(2):
                    continue
                NK = min(4 * i + 5, 32)
                pre = max(0, NK - 28)
                mstart = 4 * i + 1
                mend = min(4 * i + 4, 31)
                mw_ = 128 * (mend - mstart + 1)
                qt = qh[:, 128 * i:128 * i + 128]
                if pre:
                    for k in range(pre):
                        if k == pre - 1:
                            E_.add("tensor", lambda e, k=k, qt=qt, kvh=kvh:
                                   e.matmul(out=PSf[:, 128 * k:128 * k + 128],
                                            lhsT=qt,
                                            rhs=A1[:, kvh * 4096 + 128 * k:
                                                   kvh * 4096 + 128 * (k + 1)],
                                            start=True,
                                            stop=True).then_inc(E_.h["pe"], 1))
                        else:
                            E_.add("tensor", lambda e, k=k, qt=qt, kvh=kvh:
                                   e.matmul(out=PSf[:, 128 * k:128 * k + 128],
                                            lhsT=qt,
                                            rhs=A1[:, kvh * 4096 + 128 * k:
                                                   kvh * 4096 + 128 * (k + 1)],
                                            start=True, stop=True))
                    tpe = E_.inc("pe")
                    E_.wait("scalar", "pe", tpe)
                    E_.add("scalar", lambda e, w=128 * pre: (
                        e.copy(out=sc_extra[:, 0:w], in_=PSf[:, 0:w]), e.drain(),
                        e.nop().then_inc(E_.h["sx"], 1)))
                    tsx = E_.inc("sx")
                    E_.wait("tensor", "sx", tsx)
                k = pre
                groups = []
                while k < NK:
                    n = min(4, NK - k)
                    groups.append((k, n))
                    k += n
                for gi, (k, n) in enumerate(groups):
                    last = gi == len(groups) - 1

                    def qk(e, k=k, n=n, qt=qt, kvh=kvh, pre=pre, last=last):
                        mm = e.matmul(
                            out=PSf[:, 128 * (k - pre):128 * (k - pre) + 128 * n],
                            lhsT=qt,
                            rhs=A1[:, kvh * 4096 + 128 * k:
                                   kvh * 4096 + 128 * (k + n)],
                            start=True, stop=True)
                        if last:
                            mm.then_inc(E_.h["pe"], 1)
                    E_.add("tensor", qk)
                tqk = E_.inc("pe")
                if p4cut(3):
                    continue
                E_.wait("vector", "pe", tqk)
                E_.add("vector", lambda e, o=128 * (mstart - pre), w=mw_, i=i: (
                    e.tensor_tensor(out=scm[:, 0:w], in0=PSf[:, o:o + w],
                                    in1=mskb[:, i * 512:i * 512 + w], op=ALU.add),
                    e.drain()))

                def stats(e, w_clean=128 * (mstart - pre), w=mw_, pre=pre):
                    e.tensor_reduce(out=stat["mx1"][:], in_=PSf[:, 0:w_clean],
                                    axis=AX.X, op=ALU.max)
                    e.tensor_reduce(out=stat["mx2"][:], in_=scm[:, 0:w],
                                    axis=AX.X, op=ALU.max)
                    e.drain()
                    e.tensor_tensor(out=stat["mx1"][:], in0=stat["mx1"][:],
                                    in1=stat["mx2"][:], op=ALU.max)
                    e.drain()
                    if pre:
                        e.tensor_reduce(out=stat["mx2"][:],
                                        in_=sc_extra[:, 0:128 * pre],
                                        axis=AX.X, op=ALU.max)
                        e.drain()
                        e.tensor_tensor(out=stat["mx1"][:], in0=stat["mx1"][:],
                                        in1=stat["mx2"][:], op=ALU.max)
                        e.drain()
                    e.tensor_scalar(out=stat["neg"][:], in0=stat["mx1"][:],
                                    scalar1=-1.0, scalar2=None, op0=ALU.mult)
                    e.drain()
                    e.nop().then_inc(E_.h["vx"], 1)
                E_.add("vector", stats)
                tst = E_.inc("vx")
                if p4cut(4):
                    continue
                E_.wait("scalar", "vx", tst)

                def expf(e, pre=pre, mstart=mstart, NK=NK, w=mw_):
                    if pre:
                        e.activation(out=pr[:, 0:128 * pre],
                                     in_=sc_extra[:, 0:128 * pre], func=AF.Exp,
                                     bias=stat["neg"][:, 0:1])
                    e.activation(out=pr[:, 128 * pre:128 * mstart],
                                 in_=PSf[:, 0:128 * (mstart - pre)], func=AF.Exp,
                                 bias=stat["neg"][:, 0:1])
                    e.activation(out=pr[:, 128 * mstart:128 * mstart + w],
                                 in_=scm[:, 0:w], func=AF.Exp,
                                 bias=stat["neg"][:, 0:1])
                    e.drain()
                    e.nop().then_inc(E_.h["sx"], 1)
                E_.add("scalar", expf)
                texp = E_.inc("sx")
                if p4cut(5):
                    continue
                E_.wait("vector", "sx", texp)

                def sumf(e, NK=NK):
                    e.tensor_reduce(out=stat["sum"][:], in_=pr[:, 0:128 * NK],
                                    axis=AX.X, op=ALU.add)
                    e.drain()
                    e.reciprocal(out=stat["rcp"][:], in_=stat["sum"][:])
                    e.drain()
                    e.nop().then_inc(E_.h["vx"], 1)
                E_.add("vector", sumf)
                trcp = E_.inc("vx")
                if p4cut(6):
                    continue
                E_.wait("tensor", "sx", texp)
                slot_t = [None] * 7
                for k in range(NK):
                    sl_ = k % 7
                    sap = psbslot[sl_]
                    if slot_t[sl_] is not None:
                        E_.wait_ev("tensor", slot_t[sl_])
                    E_.add("tensor", lambda e, k=k, sap=sap: e.transpose(
                        out=sap,
                        in_=pr[:, 128 * k:128 * k + 128],
                        identity=identb[:]).then_inc(E_.h["pe"], 1))
                    tp = E_.inc("pe")
                    eng = "scalar" if k % 2 == 0 else "vector"
                    E_.wait(eng, "pe", tp)
                    if eng == "scalar":
                        E_.inc("evs")
                        E_.add("scalar", lambda e, k=k, sap=sap: (
                            e.copy(out=pT[:, 128 * k:128 * k + 128],
                                   in_=sap).then_inc(E_.h["evs"], 1),
                            e.drain()))
                    else:
                        E_.inc("evv")
                        E_.add("vector", lambda e, k=k, sap=sap: (
                            e.tensor_copy(out=pT[:, 128 * k:128 * k + 128],
                                          in_=sap).then_inc(E_.h["evv"], 1),
                            e.drain()))
                    slot_t[sl_] = E_.ev_now()
                E_.wait("tensor", "vx", trcp)
                E_.add("tensor", lambda e: e.transpose(
                    out=PSf[0:1, 640:768], in_=stat["rcp"][:],
                    identity=identf[:]).then_inc(E_.h["pe"], 1))
                t1 = E_.inc("pe")
                E_.wait("scalar", "pe", t1)
                E_.add("scalar", lambda e: (e.copy(out=rrow[:], in_=PSf[0:1, 640:768]),
                                            e.drain(),
                                            e.nop().then_inc(E_.h["sx"], 1)))
                t2 = E_.inc("sx")
                E_.wait("tensor", "sx", t2)
                t3 = bcast_row(E_, rrow[0:1, 0:128], 128, PSf[:, 512:640], onesr[:])
                E_.wait("scalar", "pe", t3)
                E_.add("scalar", lambda e: (e.copy(out=rcpb[:], in_=PSf[:, 512:640]),
                                            e.drain(),
                                            e.nop().then_inc(E_.h["sx"], 1)))
                trcb = E_.inc("sx")
                if p4cut(7):
                    continue
                E_.wait_ev("tensor")
                for k in range(NK):
                    if k == NK - 1:
                        E_.add("tensor", lambda e, k=k, kvh=kvh: e.matmul(
                            out=PSf[:, 0:128],
                            lhsT=A2[:, 512 * k + 128 * kvh:
                                   512 * k + 128 * kvh + 128],
                            rhs=pT[:, 128 * k:128 * k + 128],
                            start=(k == 0),
                            stop=True).then_inc(E_.h["pe"], 1))
                    else:
                        E_.add("tensor", lambda e, k=k, kvh=kvh: e.matmul(
                            out=PSf[:, 0:128],
                            lhsT=A2[:, 512 * k + 128 * kvh:
                                   512 * k + 128 * kvh + 128],
                            rhs=pT[:, 128 * k:128 * k + 128],
                            start=(k == 0), stop=False))
                tav = E_.inc("pe")
                if p4cut(8):
                    continue
                E_.wait("vector", "pe", tav)
                E_.wait("vector", "sx", trcb)
                E_.add("vector", lambda e, h=h, i=i: (
                    e.tensor_tensor(
                        out=A4[:, h * 1024 + 128 * i: h * 1024 + 128 * i + 128],
                        in0=PSf[:, 0:128], in1=rcpb[:], op=ALU.mult),
                    e.drain(), e.nop().then_inc(E_.h["vx"], 1)))
                tfin = E_.inc("vx")
                E_.wait("tensor", "vx", tfin)
                E_.wait("tensor", "sx", E_.c["sx"])
        E_.barrier()

        if _stop(4):
            return nc
        def wo_evac(m0, t0, ps, tgt):
            E_.wait("scalar", "pe", tgt)
            E_.add("scalar", lambda e, m0=m0, t0=t0, ps=ps: (
                e.copy(out=A1[:, (m0 // 128) * 1024 + t0:
                              (m0 // 128) * 1024 + t0 + 512],
                       in_=ps[:]).then_inc(E_.h["evs"], 1), e.drain()))
            E_.inc("evs")
        E_.wait("sync", "pe", E_.c["pe"])
        mm_fm2(E_, wt["o"], D, D,
               lambda kc, t0: A4[:, kc * 1024 + t0: kc * 1024 + t0 + 512],
               T, wo_evac, [W0, W1], PS["pool"])
        E_.barrier()

        if _stop(5):
            return nc
        rsb_g, rsb_u, rsb_d = F1[:, 0:512], F1[:, 512:1024], F1[:, 1024:1536]
        ftmp = F1[:, 1536:2048]
        g0f = F1[:, 2048:2560]
        xof = F1[:, 2560:3072]
        x1f = [F1[:, 3072:3584], F1[:, 3584:4096]]

        for p_ in range(4):
            t0 = p_ * 256
            tq = quant_block(E_, lambda c, t0=t0: A3[:, c * 1024 + t0: c * 1024 + t0 + 256],
                             16, 256, 128.0,
                             lambda c, t0=t0: A3[:, c * 1024 + t0: c * 1024 + t0 + 256],
                             rows, F1, F2, ibuf, identf[:], onesr[:], PS, clip_hi=True)
            dequant_row(E_, rows["q"], rows["r1"], t0, 256,
                        mrow[0:1, QW_IDX["L0g"]:QW_IDX["L0g"] + 1], 128.0)
            dequant_row(E_, rows["q"], rows["r2"], t0, 256,
                        mrow[0:1, QW_IDX["L0u"]:QW_IDX["L0u"] + 1], 128.0)
        E_.barrier()

        ic_ap = lambda c: (A2[:, c * 512:(c + 1) * 512] if c < 32
                           else A4[:, (c - 32) * 512:(c - 31) * 512])
        for sub in range(2):
            st0 = sub * 512
            for rsrc, rdst in [(rows["r1"], rsb_g), (rows["r2"], rsb_u)]:
                tb = bcast_row(E_, rsrc[0:1, st0:st0 + 512], 512, PS["b1"], onesr[:])
                E_.wait("scalar", "pe", tb)
                E_.add("scalar", lambda e, d=rdst: (
                    e.copy(out=d, in_=PS["b1"]), e.drain(),
                    e.nop().then_inc(E_.h["sx"], 1)))
                E_.inc("sx")
            E_.wait("vector", "sx", E_.c["sx"])

            g_sx = [E_.c["sx"], E_.c["sx"]]

            def gate_evac(m0, t0, ps, tgt):
                c = m0 // 128
                b = c % 2
                ft = ftmp if b == 0 else F2[:, 0:512]
                E_.wait("vector", "pe", tgt)
                E_.wait("vector", "sx", g_sx[b])
                E_.add("vector", lambda e, ps=ps, ft=ft: (
                    e.tensor_tensor(out=ft, in0=ps[:], in1=rsb_g,
                                    op=ALU.mult).then_inc(E_.h["evv"], 1),
                    e.drain(), e.nop().then_inc(E_.h["vx"], 1)))
                E_.inc("evv")
                tv = E_.inc("vx")
                E_.wait("scalar", "vx", tv)
                E_.add("scalar", lambda e, c=c, ft=ft: (
                    e.activation(out=ic_ap(c), in_=ft,
                                 func=AF.Silu).then_inc(E_.h["sx"], 1),
                    e.drain()))
                g_sx[b] = E_.inc("sx")
            E_.wait("sync", "pe", E_.c["pe"])
            mm_fm2(E_, wt["g0"], D, I0,
                   lambda kc, t0, st0=st0: A3[:, kc * 1024 + st0: kc * 1024 + st0 + 512],
                   512, gate_evac, [W0, W1], PS["pool"])

            def up_evac(m0, t0, ps, tgt):
                c = m0 // 128
                E_.wait("vector", "pe", tgt)
                E_.wait("vector", "sx", E_.c["sx"])
                def f(e, c=c, ps=ps):
                    e.tensor_tensor(out=ftmp, in0=ps[:], in1=ic_ap(c), op=ALU.mult)
                    e.drain()
                    e.tensor_tensor(out=ic_ap(c), in0=ftmp, in1=rsb_u,
                                    op=ALU.mult).then_inc(E_.h["evv"], 1)
                    e.drain()
                E_.add("vector", f)
                E_.inc("evv")
            E_.wait("sync", "pe", E_.c["pe"])
            mm_fm2(E_, wt["u0"], D, I0,
                   lambda kc, t0, st0=st0: A3[:, kc * 1024 + st0: kc * 1024 + st0 + 512],
                   512, up_evac, [W0, W1], PS["pool"])
            E_.barrier()

            fwht_pe_stage(E_, ic_ap, 64, 512, h128[:], PS["pool"])
            fwht_chunk_stages(E_, ic_ap, 64, 512, stg)
            for half in range(2):
                t0 = half * 256
                tq = quant_block(E_, lambda c, t0=t0: ic_ap(c)[:, t0:t0 + 256],
                                 64, 256, 128.0,
                                 lambda c, t0=t0: ic_ap(c)[:, t0:t0 + 256],
                                 rows, F1, F2, ibuf, identf[:], onesr[:], PS,
                                 prescale=float(1.0 / np.sqrt(I0)), clip_hi=True)
                dequant_row(E_, rows["q"], rows["r1"], t0, 256,
                            mrow[0:1, QW_IDX["L0d"]:QW_IDX["L0d"] + 1], 128.0)
            tb = bcast_row(E_, rows["r1"][0:1, 0:512], 512, PS["b1"], onesr[:])
            E_.wait("scalar", "pe", tb)
            E_.add("scalar", lambda e: (e.copy(out=rsb_d, in_=PS["b1"]), e.drain(),
                                        e.nop().then_inc(E_.h["sx"], 1)))
            E_.inc("sx")
            E_.wait("vector", "sx", E_.c["sx"])

            xo3b = x_own.rearrange("(c p) t -> p c t", p=128)

            def down_evac(m0, t0, ps, tgt, st0=sub * 512, sub=sub):
                c = m0 // 128
                buf = x1f[c % 2]
                E_.wait("sync", "vx", E_.c["vx"])
                tld = _dma(E_, "sync", xof, xo3b[:, c, st0:st0 + 512])
                E_.wait("vector", "pe", tgt)
                E_.wait("vector", "sx", E_.c["sx"])
                E_.wait("vector", "lw", tld)
                E_.wait("vector", "lg", E_.c["lg"])
                E_.add("vector", lambda e, ps=ps: (
                    e.tensor_tensor(out=ftmp, in0=ps[:], in1=rsb_d,
                                    op=ALU.mult).then_inc(E_.h["evv"], 1),
                    e.drain(), e.nop().then_inc(E_.h["vx"], 1)))
                E_.inc("evv")
                tv = E_.inc("vx")
                E_.wait("scalar", "vx", tv)
                E_.add("scalar", lambda e: (
                    e.activation(out=g0f, in_=ftmp, func=AF.Silu), e.drain(),
                    e.nop().then_inc(E_.h["sx"], 1)))
                tsx = E_.inc("sx")
                E_.wait("vector", "sx", tsx)

                def fin(e, c=c, buf=buf, st0=st0):
                    e.tensor_tensor(out=ftmp,
                                    in0=A1[:, c * 1024 + st0: c * 1024 + st0 + 512],
                                    in1=g0f, op=ALU.mult)
                    e.drain()
                    e.tensor_tensor(out=buf, in0=ftmp, in1=xof, op=ALU.add)
                    e.drain()
                    e.tensor_copy(out=A1[:, c * 1024 + st0: c * 1024 + st0 + 512],
                                  in_=buf)
                    e.drain()
                    e.nop().then_inc(E_.h["vx"], 1)
                E_.add("vector", fin)
                tv2 = E_.inc("vx")
                E_.wait("gpsimd", "vx", tv2)
                _dma(E_, "gpsimd",
                     x1_o[:, c * 1024 + st0: c * 1024 + st0 + 512], buf, "lg")
                _dma(E_, "gpsimd",
                     x1i_d[:, c * 1024 + st0: c * 1024 + st0 + 512], buf, "lg")
            E_.wait("sync", "pe", E_.c["pe"])
            mm_fm2(E_, wt["d0"], I0, D,
                   lambda kc, t0: ic_ap(kc)[:, t0:t0 + 512],
                   512, down_evac, [W0, W1], PS["pool"], mw=128, ksplit=2)
            E_.wait("gpsimd", "lg", E_.c["lg"])
            E_.barrier()
        E_.barrier()
        if _stop(6):
            return nc
        A2f = A2[:].bitcast(F32)
        A3f = A3[:].bitcast(F32)
        A4f = A4[:].bitcast(F32)
        xn1f = lambda c: (A2f[:, c * 1024:(c + 1) * 1024] if c < 8
                          else A3f[:, (c - 8) * 1024:(c - 7) * 1024])
        x1i3 = x1i_d.rearrange("p (c t) -> p c t", t=T)
        acc_s = A4f[:, 0:1024]
        acc_q = A4f[:, 1024:2048]
        strm = A4f[:, 2048:3072]
        sqt = A4f[:, 3072:4096]
        E_.add("vector", lambda e: (e.memset(acc_s, 0.0), e.memset(acc_q, 0.0),
                                    e.drain()))
        for c in range(16):
            tld = _dma(E_, "sync", strm, x1i3[:, c, :])
            E_.wait("vector", "lw", tld)

            def accf(e):
                e.tensor_tensor(out=acc_s, in0=acc_s, in1=strm, op=ALU.add)
                e.drain()
                e.tensor_tensor(out=sqt, in0=strm, in1=strm, op=ALU.mult)
                e.drain()
                e.tensor_tensor(out=acc_q, in0=acc_q, in1=sqt, op=ALU.add)
                e.drain()
                e.nop().then_inc(E_.h["vx"], 1)
            E_.add("vector", accf)
            E_.inc("vx")
            E_.wait("sync", "vx", E_.c["vx"])
        for p_ in range(4):
            t0 = p_ * 256
            col_reduce_to_row(E_, acc_s[:, t0:t0 + 256], 256, rows["m"], ALU.add,
                              identf[:], rows["tc"], PS["pst"], PS["psr"])
            col_reduce_to_row(E_, acc_q[:, t0:t0 + 256], 256, rows["s"], ALU.add,
                              identf[:], rows["tc"], PS["pst"], PS["psr"])

            def lnrow(e, t0=t0):
                e.tensor_scalar(out=rows["m"][0:1, 0:256], in0=rows["m"][0:1, 0:256],
                                scalar1=1.0 / D, scalar2=None, op0=ALU.mult)
                e.tensor_scalar(out=rows["s"][0:1, 0:256], in0=rows["s"][0:1, 0:256],
                                scalar1=1.0 / D, scalar2=None, op0=ALU.mult)
                e.drain()
                e.tensor_tensor(out=rows["v"][0:1, 0:256], in0=rows["m"][0:1, 0:256],
                                in1=rows["m"][0:1, 0:256], op=ALU.mult)
                e.drain()
                e.tensor_tensor(out=rows["v"][0:1, 0:256], in0=rows["s"][0:1, 0:256],
                                in1=rows["v"][0:1, 0:256], op=ALU.subtract)
                e.drain()
                e.tensor_scalar(out=rows["v"][0:1, 0:256], in0=rows["v"][0:1, 0:256],
                                scalar1=1e-5, scalar2=None, op0=ALU.add)
                e.drain()
                e.nop().then_inc(E_.h["vx"], 1)
            E_.add("vector", lnrow)
            tv = E_.inc("vx")
            E_.wait("scalar", "vx", tv)
            E_.add("scalar", lambda e: (
                e.activation(out=rows["v"][0:1, 0:256], in_=rows["v"][0:1, 0:256],
                             func=AF.Sqrt),
                e.drain(), e.nop().then_inc(E_.h["sx"], 1)))
            ts = E_.inc("sx")
            E_.wait("vector", "sx", ts)

            def lnfin(e, t0=t0):
                e.reciprocal(out=rows["v"][0:1, 0:256], in_=rows["v"][0:1, 0:256])
                e.drain()
                e.tensor_scalar(out=rows["m"][0:1, 0:256], in0=rows["m"][0:1, 0:256],
                                scalar1=-1.0, scalar2=None, op0=ALU.mult)
                e.drain()
                e.tensor_copy(out=rows["r1"][0:1, t0:t0 + 256],
                              in_=rows["m"][0:1, 0:256])
                e.tensor_copy(out=rows["r2"][0:1, t0:t0 + 256],
                              in_=rows["v"][0:1, 0:256])
                e.drain()
                e.nop().then_inc(E_.h["vx"], 1)
            E_.add("vector", lnfin)
            E_.inc("vx")
        E_.add("vector", lambda e: (e.memset(acc_q, 0.0), e.drain()))
        for half in range(2):
            t0 = half * 512
            tb1 = bcast_row(E_, rows["r1"][0:1, t0:t0 + 512], 512, PS["b1"], onesr[:])
            tb2 = bcast_row(E_, rows["r2"][0:1, t0:t0 + 512], 512, PS["b2"], onesr[:])
            E_.wait("vector", "pe", tb2)
            for c in range(16):
                tld = _dma(E_, "sync", strm[:, 0:512], x1i3[:, c, t0:t0 + 512])
                E_.wait("vector", "lw", tld)

                def apl(e, c=c, t0=t0):
                    e.tensor_tensor(out=sqt[:, 0:512], in0=strm[:, 0:512],
                                    in1=PS["b1"], op=ALU.add)
                    e.drain()
                    e.tensor_tensor(out=xn1f(c)[:, t0:t0 + 512], in0=sqt[:, 0:512],
                                    in1=PS["b2"], op=ALU.mult)
                    e.drain()
                    e.tensor_tensor(out=sqt[:, 0:512],
                                    in0=xn1f(c)[:, t0:t0 + 512],
                                    in1=xn1f(c)[:, t0:t0 + 512], op=ALU.mult)
                    e.drain()
                    e.tensor_tensor(out=acc_q[:, t0:t0 + 512],
                                    in0=acc_q[:, t0:t0 + 512],
                                    in1=sqt[:, 0:512], op=ALU.add)
                    e.drain()
                    e.nop().then_inc(E_.h["vx"], 1)
                E_.add("vector", apl)
                E_.inc("vx")
                E_.wait("sync", "vx", E_.c["vx"])
        for p_ in range(4):
            t0 = p_ * 256
            col_reduce_to_row(E_, acc_q[:, t0:t0 + 256], 256, rows["s"], ALU.add,
                              identf[:], rows["tc"], PS["pst"], PS["psr"])

            def rm2(e, t0=t0):
                e.tensor_scalar(out=rows["v"][0:1, 0:256], in0=rows["s"][0:1, 0:256],
                                scalar1=1.0 / D, scalar2=1e-8, op0=ALU.mult,
                                op1=ALU.add)
                e.drain()
                e.nop().then_inc(E_.h["vx"], 1)
            E_.add("vector", rm2)
            tv = E_.inc("vx")
            E_.wait("scalar", "vx", tv)
            E_.add("scalar", lambda e: (
                e.activation(out=rows["v"][0:1, 0:256], in_=rows["v"][0:1, 0:256],
                             func=AF.Sqrt),
                e.drain(), e.nop().then_inc(E_.h["sx"], 1)))
            ts = E_.inc("sx")
            E_.wait("vector", "sx", ts)

            def rm2b(e, t0=t0):
                e.reciprocal(out=rows["v"][0:1, 0:256], in_=rows["v"][0:1, 0:256])
                e.drain()
                e.tensor_copy(out=rows["r3"][0:1, t0:t0 + 256],
                              in_=rows["v"][0:1, 0:256])
                e.drain()
                e.nop().then_inc(E_.h["vx"], 1)
            E_.add("vector", rm2b)
            E_.inc("vx")
        E_.barrier()

        rsb0, rsb1 = F1[:, 512:1024], F1[:, 1024:1536]
        ftmp7 = F1[:, 1536:2048]
        ftmp7b = F1[:, 2048:2560]
        i_d3 = i_d.rearrange("p (c t) -> p c t", t=T)

        def gate_half(widx, wname, half, evac):
            t0h = half * 512
            for q_ in range(2):
                t0q = t0h + q_ * 256
                tb2 = bcast_row(E_, rows["r3"][0:1, t0q:t0q + 256], 256,
                                PS["b2"][:, 0:256], onesr[:])
                E_.wait("vector", "pe", tb2)
                for c in range(16):
                    def zp(e, c=c, t0q=t0q, widx=widx):
                        e.scalar_tensor_tensor(
                            out=A4f[:, c * 256:(c + 1) * 256],
                            in0=xn1f(c)[:, t0q:t0q + 256],
                            scalar=nrm[:, 16 * widx + c:16 * widx + c + 1],
                            in1=PS["b2"][:, 0:256],
                            op0=ALU.mult, op1=ALU.mult)
                        e.drain()
                        e.nop().then_inc(E_.h["vx"], 1)
                    E_.add("vector", zp)
                    E_.inc("vx")
                quant_block(E_, lambda c: A4f[:, c * 256:(c + 1) * 256],
                            16, 256, 127.0,
                            lambda c, q_=q_: A4[:, 8192 + c * 512 + q_ * 256:
                                               8192 + c * 512 + q_ * 256 + 256],
                            rows, F1, F2, ibuf, identf[:], onesr[:], PS,
                            clip_hi=False)
                dequant_row(E_, rows["q"], rows["r2"], q_ * 256, 256,
                            mrow[0:1, QW_IDX[wname]:QW_IDX[wname] + 1], 127.0)
            tb = bcast_row(E_, rows["r2"][0:1, 0:512], 512, PS["b1"], onesr[:])
            E_.wait("scalar", "pe", tb)
            E_.add("scalar", lambda e: (
                e.copy(out=rsb0, in_=PS["b1"]), e.drain(),
                e.nop().then_inc(E_.h["sx"], 1)))
            E_.inc("sx")
            E_.wait("vector", "sx", E_.c["sx"])
            E_.wait("sync", "pe", E_.c["pe"])
            mm_fm2(E_, wt[{"Wi": "wi", "Wf": "wf", "Wg": "wg"}[wname]], D, D,
                   lambda kc, t0: A4[:, 8192 + kc * 512: 8192 + kc * 512 + 512],
                   512, evac, [W0, W1], PS["pool"])

        for half in range(2):
            f_sx = [None, None]

            def f_evac(m0, t0, ps, tgt, half=half, f_sx=f_sx):
                c = m0 // 128
                b = c % 2
                ft = ftmp7 if b == 0 else F2[:, 0:512]
                E_.wait("vector", "pe", tgt)
                E_.wait("vector", "sx",
                        E_.c["sx"] if f_sx[b] is None else f_sx[b])
                E_.add("vector", lambda e, ps=ps, ft=ft: (
                    e.tensor_tensor(out=ft, in0=ps[:], in1=rsb0,
                                    op=ALU.mult).then_inc(E_.h["evv"], 1),
                    e.drain(), e.nop().then_inc(E_.h["vx"], 1)))
                E_.inc("evv")
                tv = E_.inc("vx")
                E_.wait("scalar", "vx", tv)
                E_.add("scalar", lambda e, c=c, half=half, ft=ft: (
                    e.activation(out=A1[:, c * 1024 + half * 512:
                                       c * 1024 + half * 512 + 512],
                                 in_=ft, func=AF.Sigmoid).then_inc(E_.h["sx"], 1),
                    e.drain()))
                f_sx[b] = E_.inc("sx")
            gate_half(1, "Wf", half, f_evac)
        E_.barrier()
        for half in range(2):
            def i_evac(m0, t0, ps, tgt, half=half):
                c = m0 // 128
                E_.wait("vector", "pe", tgt)
                E_.wait("vector", "sx", E_.c["sx"])
                E_.add("vector", lambda e, ps=ps: (
                    e.tensor_tensor(out=ftmp7, in0=ps[:], in1=rsb0,
                                    op=ALU.mult).then_inc(E_.h["evv"], 1),
                    e.drain(), e.nop().then_inc(E_.h["vx"], 1)))
                E_.inc("evv")
                tv = E_.inc("vx")
                E_.wait("scalar", "vx", tv)
                E_.add("scalar", lambda e: (
                    e.activation(out=ftmp7b, in_=ftmp7,
                                 func=AF.Silu).then_inc(E_.h["sx"], 1),
                    e.drain()))
                ts = E_.inc("sx")
                E_.wait("vector", "sx", ts)
                E_.wait("vector", "lg", E_.c["lg"])

                def fin(e, c=c, half=half):
                    e.tensor_scalar(out=ftmp7,
                                    in0=A1[:, c * 1024 + half * 512:
                                           c * 1024 + half * 512 + 512],
                                    scalar1=-1.0, scalar2=1.0, op0=ALU.mult,
                                    op1=ALU.add)
                    e.drain()
                    e.tensor_tensor(out=stg[:, 0:512], in0=ftmp7b, in1=ftmp7,
                                    op=ALU.mult)
                    e.drain()
                    e.nop().then_inc(E_.h["vx"], 1)
                E_.add("vector", fin)
                tv2 = E_.inc("vx")
                E_.wait("gpsimd", "vx", tv2)
                _dma(E_, "gpsimd", i_d3[:, c, half * 512:half * 512 + 512],
                     stg[:, 0:512], "lg")
            gate_half(0, "Wi", half, i_evac)
        E_.barrier()
        for half in range(2):
            g_sx7 = [None, None]
            g_lg7 = [None, None]

            def g_evac(m0, t0, ps, tgt, half=half, g_sx7=g_sx7, g_lg7=g_lg7):
                c = m0 // 128
                b = c % 2
                ft = ftmp7 if b == 0 else F2[:, 0:512]
                sg = stg[:, 0:512] if b == 0 else stg[:, 512:1024]
                E_.wait("vector", "pe", tgt)
                E_.wait("vector", "sx",
                        E_.c["sx"] if g_sx7[b] is None else g_sx7[b])
                E_.add("vector", lambda e, ps=ps, ft=ft: (
                    e.tensor_tensor(out=ft, in0=ps[:], in1=rsb0,
                                    op=ALU.mult).then_inc(E_.h["evv"], 1),
                    e.drain(), e.nop().then_inc(E_.h["vx"], 1)))
                E_.inc("evv")
                tv = E_.inc("vx")
                E_.wait("scalar", "vx", tv)
                if g_lg7[b] is not None:
                    E_.wait("scalar", "lg", g_lg7[b])
                else:
                    E_.wait("scalar", "lg", E_.c["lg"])
                E_.add("scalar", lambda e, ft=ft, sg=sg: (
                    e.activation(out=sg, in_=ft,
                                 func=AF.Silu).then_inc(E_.h["sx"], 1),
                    e.drain()))
                ts = E_.inc("sx")
                g_sx7[b] = ts
                E_.wait("gpsimd", "sx", ts)
                g_lg7[b] = _dma(E_, "gpsimd",
                                sg_o[:, c * 1024 + half * 512:
                                     c * 1024 + half * 512 + 512], sg, "lg")
            gate_half(2, "Wg", half, g_evac)
        E_.barrier()

        _dma(E_, "gpsimd", xn1_d[:, 0:8192], A2f[:, 0:8192], "lg")
        _dma(E_, "gpsimd", xn1_d[:, 8192:16384], A3f[:, 0:8192], "lg")
        tsp = E_.c["lg"]
        for g_ in ENGS:
            E_.wait(g_, "lg", tsp)
        tld = _dma(E_, "sync", A2[:], i_d)
        E_.wait("vector", "lw", tld)

        E_.add("vector", lambda e: (e.memset(zz[:], 0.0), e.drain()))
        for c in range(16):
            for p_ in range(8):
                o = c * 1024 + p_ * 128

                def sc1(e, o=o):
                    e.tensor_tensor_scan(out=A3[:, o:o + 128],
                                         data0=A1[:, o:o + 128],
                                         data1=A2[:, o:o + 128],
                                         initial=0.0, op0=ALU.mult, op1=ALU.add)
                    e.drain()
                    e.tensor_tensor_scan(out=A4[:, o:o + 128],
                                         data0=A1[:, o:o + 128],
                                         data1=zz[:], initial=1.0,
                                         op0=ALU.mult, op1=ALU.add)
                    e.drain()
                E_.add("vector", sc1)
        E_.add("vector", lambda e: e.nop().then_inc(E_.h["vx"], 1))
        tv = E_.inc("vx")
        for p_ in range(8):
            for c in range(16):

                def smc(e, p_=p_, c=c):
                    e.tensor_copy(out=summ[:, p_ * 32 + c:p_ * 32 + c + 1],
                                  in_=A3[:, c * 1024 + p_ * 128 + 127:
                                         c * 1024 + p_ * 128 + 128])
                    e.tensor_copy(out=summ[:, p_ * 32 + 16 + c:p_ * 32 + 17 + c],
                                  in_=A4[:, c * 1024 + p_ * 128 + 127:
                                         c * 1024 + p_ * 128 + 128])
                E_.add("vector", smc)
        E_.add("vector", lambda e: (e.drain(), e.nop().then_inc(E_.h["vx"], 1)))
        tv = E_.inc("vx")
        E_.wait("gpsimd", "vx", tv)
        _dma(E_, "gpsimd", summ_o, summ[:], "lg")
        _dma(E_, "gpsimd", sl_o, A3[:], "lg")
        _dma(E_, "gpsimd", P_o, A4[:], "lg")
        tlg = E_.c["lg"]
        for g in ENGS:
            E_.wait(g, "lg", tlg)
        E_.barrier()

        if _stop(7):
            return nc
        A1f = A1[:].bitcast(F32)
        tl8 = _dma(E_, "sync", A1f[:, 0:8192], xn1_d[:, 0:8192])
        tl8 = _dma(E_, "sync", A2f[:, 0:8192], xn1_d[:, 8192:16384])
        for g in ["vector", "tensor", "scalar"]:
            E_.wait(g, "lw", tl8)
        wsf = lambda c: (A1f[:, c * 1024:(c + 1) * 1024] if c < 8
                         else A2f[:, (c - 8) * 1024:(c - 7) * 1024])
        slotf = F1[:, 2560:3584]
        for half in range(2):
            hof = lambda c, half=half: wsf(c)[:, half * 512:half * 512 + 512]
            fwht_pe_stage(E_, hof, 16, 512, h128f[:], PS["pool"])
            fwht_chunk_stages(E_, hof, 16, 512, slotf)
        for p_ in range(4):
            t0 = p_ * 256
            quant_block(E_, lambda c, t0=t0: wsf(c)[:, t0:t0 + 256],
                        16, 256, 127.0,
                        lambda c, t0=t0: A3[:, c * 1024 + t0: c * 1024 + t0 + 256],
                        rows, F1, F2, ibuf, identf[:], onesr[:], PS,
                        prescale=float(1.0 / np.sqrt(D)), clip_hi=False)
            dequant_row(E_, rows["q"], rows["r2"], t0, 256,
                        mrow[0:1, QW_IDX["U1"]:QW_IDX["U1"] + 1], 127.0)
        for half, dst in [(0, rsb0), (1, rsb1)]:
            tb = bcast_row(E_, rows["r2"][0:1, half * 512:half * 512 + 512],
                           512, PS["b1"], onesr[:])
            E_.wait("scalar", "pe", tb)
            E_.add("scalar", lambda e, d=dst: (
                e.copy(out=d, in_=PS["b1"]), e.drain(),
                e.nop().then_inc(E_.h["sx"], 1)))
            E_.inc("sx")
        E_.wait("vector", "sx", E_.c["sx"])
        E_.barrier()
        i1f = lambda c: (A1f[:, c * 1024:(c + 1) * 1024] if c < 8
                         else A2f[:, (c - 8) * 1024:(c - 7) * 1024] if c < 16
                         else None)
        for th in range(2):
            t0h = th * 512
            ih = lambda c: (A1f[:, c * 512:(c + 1) * 512] if c < 16
                            else A2f[:, (c - 16) * 512:(c - 15) * 512])

            u_sx = [None, None]

            def u1_evac(m0, t0, ps, tgt, th=th, u_sx=u_sx):
                c = m0 // 128
                b = c % 2
                ft = ftmp7 if b == 0 else F2[:, 0:512]
                ftb = ftmp7b if b == 0 else F2[:, 512:1024]
                rsb = rsb0 if th == 0 else rsb1
                E_.wait("vector", "pe", tgt)
                E_.wait("vector", "sx",
                        E_.c["sx"] if u_sx[b] is None else u_sx[b])
                E_.add("vector", lambda e, ps=ps, rsb=rsb, ft=ft: (
                    e.tensor_tensor(out=ft, in0=ps[:], in1=rsb,
                                    op=ALU.mult).then_inc(E_.h["evv"], 1),
                    e.drain(), e.nop().then_inc(E_.h["vx"], 1)))
                E_.inc("evv")
                tv = E_.inc("vx")
                E_.wait("scalar", "vx", tv)
                E_.add("scalar", lambda e, ft=ft, ftb=ftb: (
                    e.activation(out=ftb, in_=ft,
                                 func=AF.Relu).then_inc(E_.h["sx"], 1),
                    e.drain()))
                ts = E_.inc("sx")
                u_sx[b] = ts
                E_.wait("vector", "sx", ts)
                E_.add("vector", lambda e, c=c, ftb=ftb: (
                    e.tensor_tensor(out=ih(c), in0=ftb,
                                    in1=ftb, op=ALU.mult),
                    e.drain(), e.nop().then_inc(E_.h["vx"], 1)))
                E_.inc("vx")
            E_.wait("sync", "pe", E_.c["pe"])
            mm_fm2(E_, wt["u1"], D, I1,
                   lambda kc, t0, t0h=t0h: A3[:, kc * 1024 + t0h: kc * 1024 + t0h + 512],
                   512, u1_evac, [W0, W1], PS["pool"])
            E_.barrier()
            fwht_pe_stage(E_, ih, 32, 512, h128f[:], PS["pool"])
            fwht_chunk_stages(E_, ih, 32, 512, slotf)
            for p_ in range(2):
                t0 = p_ * 256
                quant_block(E_, lambda c, t0=t0: ih(c)[:, t0:t0 + 256],
                            32, 256, 127.0,
                            lambda c, t0=t0: A4[:, c * 512 + t0: c * 512 + t0 + 256],
                            rows, F1, F2, ibuf, identf[:], onesr[:], PS,
                            prescale=float(1.0 / np.sqrt(I1)), clip_hi=False)
                dequant_row(E_, rows["q"], rows["r2"], t0, 256,
                            mrow[0:1, QW_IDX["D1"]:QW_IDX["D1"] + 1], 127.0)
            tb = bcast_row(E_, rows["r2"][0:1, 0:512], 512, PS["b1"], onesr[:])
            E_.wait("scalar", "pe", tb)
            E_.add("scalar", lambda e, th=th: (
                e.copy(out=rsb1 if th else rsb0, in_=PS["b1"]), e.drain(),
                e.nop().then_inc(E_.h["sx"], 1)))
            E_.inc("sx")
            E_.wait("vector", "sx", E_.c["sx"])

            d_sx = [None, None]
            d_lg = [None, None]

            def d1_evac(m0, t0, ps, tgt, th=th, d_sx=d_sx, d_lg=d_lg):
                c = m0 // 128
                b = c % 2
                ft = ftmp7 if b == 0 else F2[:, 0:512]
                sgb = stg[:, 0:512] if b == 0 else stg[:, 512:1024]
                rsb = rsb1 if th else rsb0
                E_.wait("vector", "pe", tgt)
                E_.wait("vector", "sx",
                        E_.c["sx"] if d_sx[b] is None else d_sx[b])
                E_.add("vector", lambda e, ps=ps, rsb=rsb, ft=ft: (
                    e.tensor_tensor(out=ft, in0=ps[:], in1=rsb,
                                    op=ALU.mult).then_inc(E_.h["evv"], 1),
                    e.drain(), e.nop().then_inc(E_.h["vx"], 1)))
                E_.inc("evv")
                tv = E_.inc("vx")
                E_.wait("scalar", "vx", tv)
                E_.wait("scalar", "lg",
                        E_.c["lg"] if d_lg[b] is None else d_lg[b])
                E_.add("scalar", lambda e, ft=ft, sgb=sgb: (
                    e.activation(out=sgb, in_=ft,
                                 func=AF.Silu).then_inc(E_.h["sx"], 1),
                    e.drain()))
                ts = E_.inc("sx")
                d_sx[b] = ts
                E_.wait("gpsimd", "sx", ts)
                d_lg[b] = _dma(E_, "gpsimd",
                               g1_o[:, c * 1024 + th * 512:
                                    c * 1024 + th * 512 + 512], sgb, "lg")
            E_.wait("sync", "pe", E_.c["pe"])
            mm_fm2(E_, wt["d1"], I1, D,
                   lambda kc, t0: A4[:, kc * 512: kc * 512 + 512],
                   512, d1_evac, [W0, W1], PS["pool"], mw=128)
            E_.barrier()
        tlg = E_.c["lg"]
        tlw = E_.c["lw"]
        for g in ENGS:
            E_.wait(g, "lg", tlg)
            E_.wait(g, "lw", tlw)
        E_.barrier()

        block = es.enter_context(nc.Block())
        E_.replay(block)
    return nc



def build_finish():
    nc = bass.Bass("TRN2", target_bir_lowering=False, debug=False)
    dr = lambda n, s, dt, k: nc.dram_tensor(n, s, dt, kind=k).ap()
    summs_d = dr("summs_d", [128, 1024], F32, "ExternalInput")
    sl_d = dr("sl_d", [128, 16 * T], BF16, "ExternalInput")
    P_d = dr("P_d", [128, 16 * T], BF16, "ExternalInput")
    sg_d = dr("sg_d", [128, 16 * T], BF16, "ExternalInput")
    g1_d = dr("g1_d", [128, 16 * T], BF16, "ExternalInput")
    x1_d = dr("x1_d", [128, 16 * T], F32, "ExternalInput")
    wo1_d = dr("wo1_d", [D, D], BF16, "ExternalInput")
    wsums = dr("wsums_d", [8, 16], F32, "ExternalInput")
    nrm2_d = dr("nrm2_d", [128, 32], F32, "ExternalInput")
    identf_d = dr("identf_d", [128, 128], F32, "ExternalInput")
    onesr_d = dr("onesr_d", [1, 128], F32, "ExternalInput")
    out_o = dr("out_o", [128, 16 * T], F32, "ExternalOutput")

    es = ExitStack()
    E_ = E(nc, es)
    with es:
        B1 = E_.sb("B1", [128, 16384], BF16)
        B2 = E_.sb("B2", [128, 16384], BF16)
        B3 = E_.sb("B3", [128, 16384], BF16)
        B4 = E_.sb("B4", [128, 16384], BF16)
        F1 = E_.sb("F1", [128, 4096], F32)
        F2 = E_.sb("F2", [128, 2048], F32)
        W0 = E_.sb("W0", [128, 4096], BF16)
        W1 = E_.sb("W1", [128, 4096], BF16)
        summs = E_.sb("summs", [128, 1024], F32)
        carr = E_.sb("carr", [128, 512], F32)
        mycar = E_.sb("mycar", [128, 128], F32)
        g1c = E_.sb("g1c", [128, 1024], BF16)
        nrm2 = E_.sb("nrm2", [128, 32], F32)
        identf = E_.sb("identf", [128, 128], F32)
        onesr = E_.sb("onesr", [1, 128], F32)
        wsr = E_.sb("wsr", [8, 16], F32)
        mrow = E_.sb("mrow", [1, 16], F32)
        ibuf = E_.sb("ibuf", [128, 256], I32)
        rows = {}
        for k, w in [("m", 256), ("s", 256), ("v", 256), ("q", 256),
                     ("r1", 1024), ("r2", 1024)]:
            rows[k] = E_.sb("row_" + k, [1, w], F32)
        rows["tc"] = E_.sb("row_tc", [128, 1], F32)
        PSf = E_.psum("PSf", [128, 3584], F32)
        PS = {
            "pool": [PSf[:, i * 512:(i + 1) * 512] for i in range(4)],
            "pst": PSf[:, 2048:2176], "psr": PSf[0:1, 2176:2304],
            "b1": PSf[:, 2560:3072], "b2": PSf[:, 3072:3584],
        }
        for s_ in ["lw", "lwa", "lwb", "lg", "lg2", "pe", "vx", "sx", "gx",
                   "evv", "evs", "bar"]:
            E_.sem(s_)
        rsb0, rsb1 = F1[:, 512:1024], F1[:, 1024:1536]
        ftmp = F1[:, 1536:2048]
        ftmp2 = F1[:, 3584:4096]
        x1c = F1[:, 2048:2560]
        outf = [F1[:, 2560:3072], F1[:, 3072:3584]]

        for dst, src in [(summs[:], summs_d), (B1[:], sl_d), (B2[:], P_d),
                         (B3[:], sg_d), (nrm2[:], nrm2_d), (identf[:], identf_d),
                         (onesr[:], onesr_d), (wsr[:], wsums)]:
            _dma(E_, "sync", dst, src)
        tl = E_.c["lw"]
        for g in ENGS:
            E_.wait(g, "lw", tl)
        E_.add("tensor", lambda e: e.transpose(
            out=PSf[0:16, 2048:2056], in_=wsr[:], identity=identf[0:8, 0:8]
        ).then_inc(E_.h["pe"], 1))
        t = E_.inc("pe")
        E_.wait("vector", "pe", t)
        E_.add("vector", lambda e: (e.tensor_reduce(
            out=rows["tc"][0:16, 0:1], in_=PSf[0:16, 2048:2056],
            axis=AX.X, op=ALU.add),
            e.drain(), e.nop().then_inc(E_.h["vx"], 1)))
        t = E_.inc("vx")
        E_.wait("tensor", "vx", t)
        E_.add("tensor", lambda e: e.transpose(
            out=PSf[0:1, 2176:2192], in_=rows["tc"][0:16, 0:1],
            identity=identf[0:16, 0:16]).then_inc(E_.h["pe"], 1))
        t = E_.inc("pe")
        E_.wait("vector", "pe", t)
        sizes = {n: float(QW_SHAPE[n][0] * QW_SHAPE[n][1]) for n in QW}

        def mprep(e):
            for wi, n in enumerate(QW):
                e.tensor_scalar(out=mrow[0:1, wi:wi + 1],
                                in0=PSf[0:1, 2176 + wi:2177 + wi],
                                scalar1=1.0 / sizes[n], scalar2=1e-5,
                                op0=ALU.mult, op1=ALU.max)
            e.drain()
            e.nop().then_inc(E_.h["vx"], 1)
        E_.add("vector", mprep)
        E_.inc("vx")
        E_.barrier()

        def carries(e):
            e.memset(carr[:, 0:16], 0.0)
            e.drain()
            for g in range(31):
                e.tensor_tensor(out=carr[:, (g + 1) * 16:(g + 2) * 16],
                                in0=summs[:, g * 32 + 16:g * 32 + 32],
                                in1=carr[:, g * 16:(g + 1) * 16], op=ALU.mult)
                e.drain()
                e.tensor_tensor(out=carr[:, (g + 1) * 16:(g + 2) * 16],
                                in0=carr[:, (g + 1) * 16:(g + 2) * 16],
                                in1=summs[:, g * 32:g * 32 + 16], op=ALU.add)
                e.drain()
            pid = e.partition_id()
            for j in e.Switch(pid, 8):
                jj = j % 4
                for i_ in range(8):
                    g = 4 * i_ + jj
                    e.tensor_copy(out=mycar[:, i_ * 16:(i_ + 1) * 16],
                                  in_=carr[:, g * 16:(g + 1) * 16])
                e.drain()
            e.drain()
            e.nop().then_inc(E_.h["vx"], 1)
        E_.add("vector", carries)
        tv = E_.inc("vx")
        for c in range(16):
            for i_ in range(8):
                o = c * 1024 + i_ * 128

                def sall(e, o=o, c=c, i_=i_):
                    e.scalar_tensor_tensor(out=B4[:, o:o + 128],
                                           in0=B2[:, o:o + 128],
                                           scalar=mycar[:, i_ * 16 + c:i_ * 16 + c + 1],
                                           in1=B1[:, o:o + 128],
                                           op0=ALU.mult, op1=ALU.add)
                E_.add("vector", sall)
        E_.add("vector", lambda e: (e.drain(), e.nop().then_inc(E_.h["vx"], 1)))
        E_.inc("vx")
        E_.barrier()

        for p_ in range(4):
            t0 = p_ * 256
            s_of = lambda c, t0=t0: B4[:, c * 1024 + t0: c * 1024 + t0 + 256]
            def sq0(e, s_of=s_of):
                e.tensor_tensor(out=F2[:, 0:256], in0=s_of(0), in1=s_of(0),
                                op=ALU.mult)
                e.drain()
            E_.add("vector", sq0)
            for c in range(1, 16):
                def sqc(e, c=c, s_of=s_of):
                    e.tensor_tensor(out=F2[:, 256:512], in0=s_of(c), in1=s_of(c),
                                    op=ALU.mult)
                    e.drain()
                    e.tensor_tensor(out=F2[:, 0:256], in0=F2[:, 0:256],
                                    in1=F2[:, 256:512], op=ALU.add)
                    e.drain()
                E_.add("vector", sqc)
            col_reduce_to_row(E_, F2[:, 0:256], 256, rows["s"], ALU.add,
                              identf[:], rows["tc"], PS["pst"], PS["psr"])

            def rmrow(e):
                e.tensor_scalar(out=rows["v"][0:1, 0:256], in0=rows["s"][0:1, 0:256],
                                scalar1=1.0 / D, scalar2=1e-5, op0=ALU.mult,
                                op1=ALU.add)
                e.drain()
                e.nop().then_inc(E_.h["vx"], 1)
            E_.add("vector", rmrow)
            tv = E_.inc("vx")
            E_.wait("scalar", "vx", tv)
            E_.add("scalar", lambda e: (
                e.activation(out=rows["v"][0:1, 0:256], in_=rows["v"][0:1, 0:256],
                             func=AF.Sqrt),
                e.drain(), e.nop().then_inc(E_.h["sx"], 1)))
            ts = E_.inc("sx")
            E_.wait("vector", "sx", ts)
            E_.add("vector", lambda e: (
                e.reciprocal(out=rows["v"][0:1, 0:256], in_=rows["v"][0:1, 0:256]),
                e.drain(), e.nop().then_inc(E_.h["vx"], 1)))
            tv = E_.inc("vx")
            E_.wait("tensor", "vx", tv)
            tb = bcast_row(E_, rows["v"][0:1, 0:256], 256, PS["b2"][:, 0:256],
                           onesr[:])
            E_.wait("vector", "pe", tb)
            for c in range(16):
                def yc(e, c=c, t0=t0, s_of=s_of):
                    e.tensor_tensor(out=F2[:, 512:768],
                                    in0=s_of(c), in1=PS["b2"][:, 0:256],
                                    op=ALU.mult)
                    e.drain()
                    e.tensor_scalar(out=F2[:, 512:768], in0=F2[:, 512:768],
                                    scalar1=nrm2[:, 16 + c:17 + c], scalar2=None,
                                    op0=ALU.mult)
                    e.drain()
                    e.tensor_tensor(out=B1[:, c * 1024 + t0: c * 1024 + t0 + 256],
                                    in0=F2[:, 512:768],
                                    in1=B3[:, c * 1024 + t0: c * 1024 + t0 + 256],
                                    op=ALU.mult)
                    e.drain()
                E_.add("vector", yc)
            E_.add("vector", lambda e: e.nop().then_inc(E_.h["vx"], 1))
            E_.inc("vx")
        E_.barrier()

        for p_ in range(4):
            t0 = p_ * 256
            y_of = lambda c, t0=t0: B1[:, c * 1024 + t0: c * 1024 + t0 + 256]
            def sq0b(e, y_of=y_of):
                e.tensor_tensor(out=F2[:, 0:256], in0=y_of(0), in1=y_of(0),
                                op=ALU.mult)
                e.drain()
            E_.add("vector", sq0b)
            for c in range(1, 16):
                def sqcb(e, c=c, y_of=y_of):
                    e.tensor_tensor(out=F2[:, 256:512], in0=y_of(c), in1=y_of(c),
                                    op=ALU.mult)
                    e.drain()
                    e.tensor_tensor(out=F2[:, 0:256], in0=F2[:, 0:256],
                                    in1=F2[:, 256:512], op=ALU.add)
                    e.drain()
                E_.add("vector", sqcb)
            col_reduce_to_row(E_, F2[:, 0:256], 256, rows["s"], ALU.add,
                              identf[:], rows["tc"], PS["pst"], PS["psr"])

            def rmrowb(e):
                e.tensor_scalar(out=rows["v"][0:1, 0:256], in0=rows["s"][0:1, 0:256],
                                scalar1=1.0 / D, scalar2=1e-8, op0=ALU.mult,
                                op1=ALU.add)
                e.drain()
                e.nop().then_inc(E_.h["vx"], 1)
            E_.add("vector", rmrowb)
            tv = E_.inc("vx")
            E_.wait("scalar", "vx", tv)
            E_.add("scalar", lambda e: (
                e.activation(out=rows["v"][0:1, 0:256], in_=rows["v"][0:1, 0:256],
                             func=AF.Sqrt),
                e.drain(), e.nop().then_inc(E_.h["sx"], 1)))
            ts = E_.inc("sx")
            E_.wait("vector", "sx", ts)
            E_.add("vector", lambda e: (
                e.reciprocal(out=rows["v"][0:1, 0:256], in_=rows["v"][0:1, 0:256]),
                e.drain(), e.nop().then_inc(E_.h["vx"], 1)))
            tv = E_.inc("vx")
            E_.wait("tensor", "vx", tv)
            tb = bcast_row(E_, rows["v"][0:1, 0:256], 256, PS["b2"][:, 0:256],
                           onesr[:])
            E_.wait("vector", "pe", tb)
            for c in range(16):
                def zc(e, c=c, t0=t0, y_of=y_of):
                    e.tensor_tensor(out=F2[:, 512:768], in0=y_of(c),
                                    in1=PS["b2"][:, 0:256], op=ALU.mult)
                    e.drain()
                    e.tensor_scalar(out=B2[:, c * 1024 + t0: c * 1024 + t0 + 256],
                                    in0=F2[:, 512:768],
                                    scalar1=nrm2[:, c:c + 1], scalar2=None,
                                    op0=ALU.mult)
                    e.drain()
                E_.add("vector", zc)
            E_.add("vector", lambda e: e.nop().then_inc(E_.h["vx"], 1))
            E_.inc("vx")
            quant_block(E_, lambda c, t0=t0: B2[:, c * 1024 + t0: c * 1024 + t0 + 256],
                        16, 256, 127.0,
                        lambda c, t0=t0: B2[:, c * 1024 + t0: c * 1024 + t0 + 256],
                        rows, F1, F2, ibuf, identf[:], onesr[:], PS, clip_hi=False)
            dequant_row(E_, rows["q"], rows["r2"], t0, 256,
                        mrow[0:1, QW_IDX["Wo1"]:QW_IDX["Wo1"] + 1], 127.0)
        for half, dst in [(0, rsb0), (1, rsb1)]:
            tb = bcast_row(E_, rows["r2"][0:1, half * 512:half * 512 + 512],
                           512, PS["b1"], onesr[:])
            E_.wait("scalar", "pe", tb)
            E_.add("scalar", lambda e, d=dst: (
                e.copy(out=d, in_=PS["b1"]), e.drain(),
                e.nop().then_inc(E_.h["sx"], 1)))
            E_.inc("sx")
        E_.wait("vector", "sx", E_.c["sx"])
        E_.barrier()

        x13 = x1_d.rearrange("p (c t) -> p c t", t=T)
        g13 = g1_d.rearrange("p (c t) -> p c t", t=T)

        fin_vx = [None, None]
        fin_lg = [None, None]

        def fin_evac(m0, t0, ps, tgt):
            c = m0 // 128
            b = (c * 2 + t0 // 512) % 2
            rsb = rsb0 if t0 == 0 else rsb1
            buf = outf[b]
            ft = ftmp if b == 0 else ftmp2
            xb = x1c if b == 0 else F2[:, 0:512]
            gb = g1c[:, 0:512] if b == 0 else g1c[:, 512:1024]
            E_.wait("sync", "vx",
                    E_.c["vx"] if fin_vx[b] is None else fin_vx[b])
            tld1 = _dma(E_, "sync", xb, x13[:, c, t0:t0 + 512])
            tld2 = _dma(E_, "sync", gb, g13[:, c, t0:t0 + 512], "gx")
            E_.wait("vector", "pe", tgt)
            E_.wait("vector", "lw", tld1)
            E_.wait("vector", "gx", tld2)
            if fin_lg[b] is not None:
                E_.wait("vector", "lg" if b == 0 else "lg2", fin_lg[b])

            def f(e, c=c, t0=t0, buf=buf, rsb=rsb, ft=ft, xb=xb, gb=gb):
                e.tensor_tensor(out=ft, in0=ps[:], in1=rsb,
                                op=ALU.mult).then_inc(E_.h["evv"], 1)
                e.drain()
                e.tensor_tensor(out=ft, in0=ft, in1=gb, op=ALU.mult)
                e.drain()
                e.tensor_tensor(out=buf, in0=ft, in1=xb, op=ALU.add)
                e.drain()
                e.nop().then_inc(E_.h["vx"], 1)
            E_.add("vector", f)
            E_.inc("evv")
            tv = E_.inc("vx")
            fin_vx[b] = tv
            E_.wait("gpsimd", "vx", tv)
            fin_lg[b] = _dma(E_, "gpsimd",
                             out_o[:, c * 1024 + t0: c * 1024 + t0 + 512],
                             buf, "lg" if b == 0 else "lg2")
        E_.wait("sync", "pe", E_.c["pe"])
        mm_fm2(E_, wo1_d, D, D,
               lambda kc, t0: B2[:, kc * 1024 + t0: kc * 1024 + t0 + 512],
               T, fin_evac, [W0, W1], PS["pool"])
        tlg = E_.c["lg"]
        tlg2 = E_.c["lg2"]
        tgx = E_.c["gx"]
        for g in ENGS:
            E_.wait(g, "lg", tlg)
            E_.wait(g, "lg2", tlg2)
            E_.wait(g, "gx", tgx)
        E_.barrier()
        block = es.enter_context(nc.Block())
        E_.replay(block)
    return nc

def _core_pos(j):
    return np.concatenate([np.arange(128 * (4 * i + j), 128 * (4 * i + j) + 128)
                           for i in range(8)])


def _rope_tables():
    inv = (1.0 / THETA ** (np.arange(0, DH, 2, dtype=np.float32) / DH)).astype(np.float32)
    fr = np.arange(S, dtype=np.float32)[:, None] * inv[None, :]
    cos = np.repeat(np.cos(fr), 2, axis=-1).astype(np.float32)
    sin = np.repeat(np.sin(fr), 2, axis=-1).astype(np.float32)
    sinm = sin.T.copy()
    sinm[0:64] *= -1.0
    return cos.T.copy(), sinm


def _masks_for(j):
    m = np.zeros((8, 128, 512), np.float32)
    for i in range(8):
        tq = 4 * i + j
        mstart = 4 * i + 1
        for kk in range(4):
            k = mstart + kk
            blk = m[i][:, kk * 128:(kk + 1) * 128]
            if k > 31 or k > tq + 1:
                blk[:] = -1e9
            elif k == tq + 1:
                qq = np.arange(128)[:, None]
                kkx = np.arange(128)[None, :]
                blk[:] = np.where(kkx <= qq, 0.0, -1e9)
    return _bf16(m)


def main_in_maps(inputs, qw, wsums_all):
    cos, sin = _rope_tables()
    scale = np.float32(DH ** -0.5)
    identf = np.eye(128, dtype=np.float32)
    identb = _bf16(identf)
    h = np.array([[1.0]], np.float32)
    while h.shape[0] < 128:
        h = np.block([[h, h], [h, -h]])
    h128 = _bf16(h)
    rot = np.zeros((128, 128), np.float32)
    for d in range(64):
        rot[d, d + 64] = 1.0
    for d in range(64, 128):
        rot[d, d - 64] = 1.0
    rotb = _bf16(rot)
    onesr = np.ones((1, 128), np.float32)
    nrms = np.zeros((128, 48), np.float32)
    for widx, key in enumerate(["L1_attn_norm_i", "L1_attn_norm_f", "L1_attn_norm_g"]):
        nrms[:, widx * 16:(widx + 1) * 16] = np.asarray(inputs[key]).reshape(16, 128).T
    wtd = {}
    for n, key in [("q", "Wq"), ("k", "Wk"), ("v", "Wv"), ("o", "Wo"),
                   ("g0", "L0g"), ("u0", "L0u"), ("d0", "L0d"),
                   ("wi", "Wi"), ("wf", "Wf"), ("wg", "Wg"),
                   ("u1", "U1"), ("d1", "D1")]:
        wtd["wt_" + n] = np.ascontiguousarray(np.asarray(qw[key]).T)
    x = np.asarray(inputs["L0_attn_x"])
    maps = []
    for c in range(8):
        b, j = divmod(c, 4)
        pos = _core_pos(j)
        xb = np.ascontiguousarray(x[b].T)
        m = dict(wtd)
        m["x_fm"] = xb
        m["x_own"] = np.ascontiguousarray(xb[:, pos])
        m["wsums_d"] = np.ascontiguousarray(wsums_all)
        m["kcos_d"] = _bf16(cos)
        m["ksin_d"] = _bf16(sin)
        m["qcos_d"] = _bf16(cos[:, pos] * scale)
        m["qsin_d"] = _bf16(sin[:, pos] * scale)
        m["masks_d"] = _masks_for(j)
        m["nrms_d"] = nrms
        m["identf_d"] = identf
        m["identb_d"] = identb
        m["h128_d"] = h128
        m["h128f_d"] = h.astype(np.float32)
        m["rotb_d"] = rotb
        m["onesr_d"] = onesr
        maps.append(m)
    return maps


def finish_in_maps(inputs, qw, wsums_all, res_main):
    identf = np.eye(128, dtype=np.float32)
    onesr = np.ones((1, 128), np.float32)
    nrm2 = np.zeros((128, 32), np.float32)
    nrm2[:, 0:16] = np.asarray(inputs["L1_attn_norm_o"]).reshape(16, 128).T
    nrm2[:, 16:32] = np.asarray(inputs["L1_attn_rms_w"]).reshape(16, 128).T
    wo1 = np.ascontiguousarray(np.asarray(qw["Wo1"]).T)
    maps = []
    for c in range(8):
        b, j = divmod(c, 4)
        summs = np.zeros((128, 1024), np.float32)
        for g in range(32):
            i_, jj = divmod(g, 4)
            src = np.asarray(res_main[b * 4 + jj]["summ_o"])
            summs[:, g * 32:g * 32 + 32] = src[:, i_ * 32:(i_ + 1) * 32]
        m = {
            "summs_d": summs,
            "sl_d": np.asarray(res_main[c]["sl_o"]),
            "P_d": np.asarray(res_main[c]["P_o"]),
            "sg_d": np.asarray(res_main[c]["sg_o"]),
            "g1_d": np.asarray(res_main[c]["g1_o"]),
            "x1_d": np.asarray(res_main[c]["x1_o"]),
            "wo1_d": wo1,
            "wsums_d": np.ascontiguousarray(wsums_all),
            "nrm2_d": nrm2,
            "identf_d": identf,
            "onesr_d": onesr,
        }
        maps.append(m)
    return maps


def kernel(**inputs):
    nc0 = build_stats()
    res0 = run_bass_kernel_spmd(nc0, stats_in_maps(inputs), core_ids=list(range(8)))
    wsums_all = np.stack([np.asarray(res0.results[c]["wsum"])[0] for c in range(8)])

    nc1 = build_quantize()
    res1 = run_bass_kernel_spmd(nc1, quantize_in_maps(inputs, wsums_all),
                                core_ids=list(range(8)))
    qw = {}
    for n in QW + AW:
        qw[n] = np.concatenate([np.asarray(res1.results[c]["q_" + n])
                                for c in range(8)], axis=0)

    nc2 = build_main()
    res2 = run_bass_kernel_spmd(nc2, main_in_maps(inputs, qw, wsums_all),
                                core_ids=list(range(8)))

    nc3 = build_finish()
    res3 = run_bass_kernel_spmd(nc3, finish_in_maps(inputs, qw, wsums_all,
                                                    [r for r in res2.results]),
                                core_ids=list(range(8)))

    out = np.zeros((B, S, D), np.float32)
    for c in range(8):
        b, j = divmod(c, 4)
        pos = _core_pos(j)
        o = np.asarray(res3.results[c]["out_o"])
        o = o.reshape(128, 16, T).transpose(2, 1, 0).reshape(T, D)
        out[b, pos, :] = o
    return out
